# version 06
# baseline (speedup 1.0000x reference)
import contextlib
import numpy as np
import concourse.bass as bass
import concourse.mybir as mybir
from concourse.bass_utils import run_bass_kernel_spmd
from concourse.bass import IndirectOffsetOnAxis

F32 = mybir.dt.float32
BF16 = mybir.dt.bfloat16
AF = mybir.ActivationFunctionType
ALU = mybir.AluOpType
AX = mybir.AxisListType

NB = 2
L = 2048
D = 1024
KT = 8
DFF = 4096
JT = 32
CHA = 256
NCA = L // CHA
NTA = CHA // 128
TB = 512
NCB = NB * L // TB
TC = 512
NCC = NB * L // TC
NTC = TC // 128
EPS = 1e-5
NZ = 8


def _names(args):
    s = set()
    for ap in args:
        if ap is None or isinstance(ap, (int, float)):
            continue
        t = getattr(ap, "tensor", None)
        n = getattr(t, "name", None)
        if n is not None:
            s.add(n)
    return s


class Sem:
    def __init__(self, nc, stack, name):
        self.h = stack.enter_context(nc.semaphore(name))
        self.n = 0

    def bump(self, instr, k=1):
        instr.then_inc(self.h, k)
        self.n += k
        return self.n


class Eng:

    def __init__(self, eng):
        self.e = eng
        self.dirty = set()

    def wait(self, sem, n=None):
        self.e.wait_ge(sem.h, sem.n if n is None else n)

    def drain(self):
        self.e.drain()
        self.dirty = set()

    def inc(self, sem, k=1):
        self.e.sem_inc(sem.h, k)
        sem.n += k

    def fence(self, sem):
        self.drain()
        self.inc(sem)

    def _pre(self, outs, ins):
        if (_names(outs) | _names(ins)) & self.dirty:
            self.e.drain()
            self.dirty = set()
        self.dirty |= _names(outs)

    def copy(self, out, in_):
        self._pre([out], [in_])
        return self.e.tensor_copy(out, in_)

    def memset(self, out, v):
        self._pre([out], [])
        return self.e.memset(out, v)

    def tt(self, out, a, b, op):
        self._pre([out], [a, b])
        return self.e.tensor_tensor(out, a, b, op)

    def ts(self, out, a, s1, op0, s2=None, op1=ALU.bypass, accum_out=None):
        self._pre([out, accum_out], [a, s1, s2])
        return self.e.tensor_scalar(out, a, s1, s2, op0, op1, accum_out)

    def stt(self, out, in0, s, in1, op0, op1):
        self._pre([out], [in0, s, in1])
        return self.e.scalar_tensor_tensor(out, in0, s, in1, op0, op1)

    def scan(self, out, d0, d1, init):
        self._pre([out], [d0, d1, init])
        return self.e.tensor_tensor_scan(out, d0, d1, init, ALU.mult, ALU.add)

    def red(self, out, in_, op):
        self._pre([out], [in_])
        return self.e.tensor_reduce(out, in_, AX.X, op)

    def act(self, out, in_, func, bias=0.0, scale=1.0, accum_out=None):
        self._pre([out, accum_out], [in_, bias, scale])
        return self.e.activation(out, in_, func, bias=bias, scale=scale,
                                 accum_out=accum_out)

    def recip(self, out, in_):
        self._pre([out], [in_])
        return self.e.reciprocal(out, in_)


class PEW:
    def __init__(self, eng):
        self.e = eng

    def wait(self, sem, n=None):
        self.e.wait_ge(sem.h, sem.n if n is None else n)

    def inc(self, sem, k=1):
        self.e.sem_inc(sem.h, k)
        sem.n += k

    def mm(self, out, lhsT, rhs, start, stop):
        return self.e.matmul(out, lhsT, rhs, start=start, stop=stop)

    def tr(self, out, in_, idn):
        return self.e.transpose(out, in_, idn)


class DMAQ:
    def __init__(self, eng):
        self.e = eng

    def wait(self, sem, n=None):
        self.e.wait_ge(sem.h, sem.n if n is None else n)

    def inc(self, sem, k=1):
        self.e.sem_inc(sem.h, k)
        sem.n += k

    def go(self, out, in_, sem=None, **kw):
        i = self.e.dma_start(out=out, in_=in_, **kw)
        if sem is not None:
            sem.bump(i, 16)
        return i


def build(phases=("A1", "A2a", "A2b", "R", "B", "C"), debug=False):
    nc = bass.Bass("TRN2", target_bir_lowering=False)

    X = nc.dram_tensor("x", [NB, L, D], F32, kind="ExternalInput").ap()
    OUT = nc.dram_tensor("out", [NB, L, D], F32, kind="ExternalOutput").ap()
    IDN = nc.dram_tensor("identity", [128, 128], F32, kind="ExternalInput").ap()

    wn = {}
    for nm, shp in [
        ("ln_att_g", [D]), ("ln_att_b", [D]), ("ln_ffn_g", [D]), ("ln_ffn_b", [D]),
        ("att_time_decay", [D]), ("att_time_first", [D]),
        ("att_mix_k", [D]), ("att_mix_v", [D]), ("att_mix_r", [D]),
        ("att_wk", [D, D]), ("att_wv", [D, D]), ("att_wr", [D, D]), ("att_wo", [D, D]),
        ("ffn_mix_k", [D]), ("ffn_mix_r", [D]),
        ("ffn_wk", [D, DFF]), ("ffn_wv", [DFF, D]), ("ffn_wr", [D, D]),
    ]:
        wn[nm] = nc.dram_tensor("L0_attn_" + nm, shp, F32, kind="ExternalInput").ap()
    for nm, shp in [("W1", [D, 256]), ("b1", [256]), ("W2", [256, NZ]), ("b2", [NZ]),
                    ("Wa", [NZ, D, DFF]), ("Wb", [NZ, DFF, D])]:
        wn[nm] = nc.dram_tensor("L0_ffn_" + nm, shp, F32, kind="ExternalInput").ap()

    skind = "ExternalOutput" if debug else "Internal"
    S_XPP = nc.dram_tensor("s_xpp", [NB, L, D], F32, kind=skind).ap()
    S_Y = nc.dram_tensor("s_y", [NB, L, D], F32, kind=skind).ap()
    S_G32 = nc.dram_tensor("s_g32", [NB, L, D], F32, kind=skind).ap()
    S_GBF = nc.dram_tensor("s_gbf", [D, NB * L], BF16, kind=skind).ap()
    S_KF = nc.dram_tensor("s_kf", [DFF, NB * L], BF16, kind=skind).ap()
    S_SR2 = nc.dram_tensor("s_sr2", [D, NB * L], F32, kind=skind).ap()
    S_ZAC = nc.dram_tensor("s_zac", [D, NB * L], F32, kind=skind).ap()

    gbf_r = S_GBF.rearrange("(k p) t -> p k t", p=128)
    kf_r = S_KF.rearrange("(j p) t -> p j t", p=128)
    sr2_r = S_SR2.rearrange("(k p) t -> p k t", p=128)
    zac_r = S_ZAC.rearrange("(k p) t -> p k t", p=128)

    st = contextlib.ExitStack()
    with st:
        E = st.enter_context

        V = Eng(nc.vector)
        A = Eng(nc.scalar)
        G = Eng(nc.gpsimd)
        P = PEW(nc.tensor)
        SY = DMAQ(nc.sync)
        GD = DMAQ(nc.gpsimd)

        semnames = ["ld", "sx", "ssq", "srd", "sst", "snr", "str_", "saf", "smx",
                    "smm", "sek", "sdv", "ssc", "swo", "sad", "sou", "sbw", "sbg",
                    "sbz", "sbm1", "sbm2", "sbev", "sbac", "sfin", "sbar", "srt"]
        S = {n: Sem(nc, st, n) for n in semnames}

        dma_sems = [S[k] for k in ("ld", "sx", "sou", "sbw", "sbg", "sbac",
                                   "srt", "sfin")]

        def barrier():
            for dsem in dma_sems:
                SY.wait(dsem)
            for e in (V, A, G):
                e.fence(S["sbar"])
            P.e.drain()
            P.inc(S["sbar"])
            SY.inc(S["sbar"])
            tgt = S["sbar"].n
            for e in (V, A, G, P, SY):
                e.wait(S["sbar"], tgt)

        FREE = {}

        def consumed(key, sem):
            FREE[key] = (sem, sem.n)

        def wait_free(eng, key):
            if key in FREE:
                sem, n = FREE[key]
                eng.wait(sem, n)

        BANK = {}

        def bank_wait(i):
            if i in BANK:
                sem, n = BANK[i]
                P.wait(sem, n)

        def bank_set(i, sem):
            BANK[i] = (sem, sem.n)

        idn = E(nc.sbuf_tensor("idn", [128, 128], F32))
        pnames = ["lag", "lab", "lfg", "lfb", "mk", "mv", "mr",
                  "fmk", "fmr", "ew", "eu"]
        PB = {n: E(nc.sbuf_tensor("pp_" + n, [128, KT], F32)) for n in pnames}
        carN = E(nc.sbuf_tensor("carN", [128, KT], F32))
        carQ = E(nc.sbuf_tensor("carQ", [128, KT], F32))
        pacc = E(nc.sbuf_tensor("pacc", [128, KT, 16], F32))
        wzbc = E(nc.sbuf_tensor("wzbc", [128, 16], F32))
        onesr = E(nc.sbuf_tensor("onesr", [1, 128], F32))
        stt_ = E(nc.sbuf_tensor("stats", [128, NTA, 32], F32))
        cst = E(nc.sbuf_tensor("cst", [128, 2], F32))
        I32 = mybir.dt.int32
        iota8f = E(nc.sbuf_tensor("iota8f", [2, 8], F32))
        iota8i = E(nc.sbuf_tensor("iota8i", [2, 8], I32))
        jjoff_i = E(nc.sbuf_tensor("jjoff_i", [128, 32], I32))
        jjoff_f = E(nc.sbuf_tensor("jjoff_f", [128, 32], F32))
        idxAi = E(nc.sbuf_tensor("idxAi", [128, 48], I32))
        idxBi = E(nc.sbuf_tensor("idxBi", [128, 192], I32))
        idxTf = E(nc.sbuf_tensor("idxTf", [128, 32], F32))
        rsF = E(nc.sbuf_tensor("rsF", [128, 1], F32))
        nc.const_aps.register(F32, EPS, cst[:, 0:1])
        nc.const_aps.register(F32, 1.0, cst[:, 1:2])
        psum = [E(nc.psum_tensor(f"pb{i}", [128, 512], F32)) for i in range(8)]

        SY.go(idn[:], IDN, sem=S["ld"])
        for nm, src in [("lag", "ln_att_g"), ("lab", "ln_att_b"),
                        ("lfg", "ln_ffn_g"), ("lfb", "ln_ffn_b"),
                        ("mk", "att_mix_k"), ("mv", "att_mix_v"),
                        ("mr", "att_mix_r"), ("fmk", "ffn_mix_k"),
                        ("fmr", "ffn_mix_r"), ("ew", "att_time_decay"),
                        ("eu", "att_time_first")]:
            SY.go(PB[nm][:], wn[src].rearrange("(k p) -> p k", p=128),
                  sem=S["ld"], allow_slow_non_contiguous=True)
        for e in (V, A, G):
            e.wait(S["ld"])
        V.memset(onesr[:], 1.0)
        A.act(PB["ew"][:], PB["ew"][:], AF.Exp)
        A.act(PB["ew"][:], PB["ew"][:], AF.Exp, scale=-1.0)
        A.act(PB["eu"][:], PB["eu"][:], AF.Exp)
        V.fence(S["sst"])
        A.fence(S["sst"])
        G.memset(carN[:], 0.0)
        G.memset(carQ[:], 0.0)
        G.memset(pacc[:], 0.0)
        G.memset(cst[:, 0:1], EPS)
        G.memset(cst[:, 1:2], 1.0)
        G.e.iota(jjoff_i[:], [[128, 32]], base=0, channel_multiplier=1)
        G.e.iota(iota8i[:], [[1, 8]], base=0, channel_multiplier=0)
        G.drain()
        G.copy(jjoff_f[:], jjoff_i[:])
        G.copy(iota8f[:], iota8i[:])
        G.fence(S["sst"])

        n_out_dma = [0]

        def ln_stats(src, sq, n_tiles, mu_c, rstd_c, base=0, sbuf_stats=None):
            stx = stt_ if sbuf_stats is None else sbuf_stats
            o = base
            wait_free(A, ("st", id(stx), o, mu_c))
            for n in range(n_tiles):
                A.act(sq[:, 0, :], src[:, n, :], AF.Square,
                      accum_out=stx[:, n, o + 1:o + 2])
                A.act(sq[:, 0, :], src[:, n, :], AF.Copy,
                      accum_out=stx[:, n, o + 0:o + 1])
            A.fence(S["srd"])
            G.wait(S["srd"])
            c = 1.0 / D
            sl = lambda i: stx[:, 0:n_tiles, o + i:o + i + 1]
            G.ts(sl(2), sl(0), c, ALU.mult)
            G.tt(sl(3), sl(2), sl(2), ALU.mult)
            G.ts(sl(4), sl(1), c, ALU.mult)
            G.tt(sl(4), sl(4), sl(3), ALU.subtract)
            G.fence(S["sst"])
            A.wait(S["sst"])
            A.act(sl(5), sl(4), AF.Ln, bias=EPS)
            A.act(stx[:, 0:n_tiles, o + rstd_c:o + rstd_c + 1], sl(5),
                  AF.Exp, scale=-0.5)
            A.fence(S["sst"])
            G.wait(S["sst"])
            G.tt(sl(mu_c), sl(2), sl(rstd_c), ALU.mult)
            G.ts(sl(mu_c), sl(mu_c), -1.0, ALU.mult)
            G.fence(S["srd"])

        def ln_norm(dst, src, n_tiles, mu_c, rstd_c, base=0, sbuf_stats=None,
                    on=None):
            stx = stt_ if sbuf_stats is None else sbuf_stats
            o = base
            e = V if on is None else on
            e.wait(S["srd"])
            for n in range(n_tiles):
                e.ts(dst[:, n, :], src[:, n, :],
                     stx[:, n, o + rstd_c:o + rstd_c + 1], ALU.mult,
                     stx[:, n, o + mu_c:o + mu_c + 1], ALU.add)
            e.fence(S["snr"])
            consumed(("st", id(stx), o, mu_c), S["snr"])

        def transpose_affine(src_tok, dst_ch, par_key, gcol, bcol, accum_slot=None):
            off = 1 if dst_ch.shape[2] == CHA + 1 else 0
            for kk in range(KT):
                bank = 4 + (kk % 2)
                bank_wait(bank)
                for n in range(NTA):
                    ii = P.tr(psum[bank][:, n * 128:(n + 1) * 128],
                              src_tok[:, n, kk * 128:(kk + 1) * 128], idn[:])
                S["str_"].bump(ii, 1)
                A.wait(S["str_"])
                kwargs = {}
                if accum_slot is not None:
                    kwargs["accum_out"] = accum_slot(kk)
                if gcol is not None:
                    ia = A.act(dst_ch[:, kk, off:off + CHA], psum[bank][:, 0:CHA],
                               AF.Identity, scale=gcol[:, kk:kk + 1],
                               bias=bcol[:, kk:kk + 1], **kwargs)
                else:
                    ia = A.act(dst_ch[:, kk, off:off + CHA], psum[bank][:, 0:CHA],
                               AF.Copy, **kwargs)
                S["saf"].bump(ia, 1)
                bank_set(bank, S["saf"])

        def transpose_add(src_ch, dst_tok, n_tiles):
            for n in range(n_tiles):
                bank_wait(4)
                bank_wait(5)
                for kk in range(KT):
                    bank = 4 + kk // 4
                    ii = P.tr(psum[bank][:, (kk % 4) * 128:(kk % 4 + 1) * 128],
                              src_ch[:, kk, n * 128:(n + 1) * 128], idn[:])
                    if kk == 7:
                        S["str_"].bump(ii, 1)
                V.wait(S["str_"])
                V.tt(dst_tok[:, n, 0:512], dst_tok[:, n, 0:512],
                     psum[4][:, 0:512], ALU.add)
                V.tt(dst_tok[:, n, 512:1024], dst_tok[:, n, 512:1024],
                     psum[5][:, 0:512], ALU.add)
                V.fence(S["sad"])
                bank_set(4, S["sad"])
                bank_set(5, S["sad"])

        barrier()

        if "A1" in phases:
            sc_A1 = st.enter_context(nc.named_scope("PH_A1"))
            with contextlib.ExitStack() as sa:
                EA = sa.enter_context
                wkb = EA(nc.sbuf_tensor("wkb", [128, KT, D], BF16))
                wvb = EA(nc.sbuf_tensor("wvb", [128, KT, D], BF16))
                wrb = EA(nc.sbuf_tensor("wrb", [128, KT, D], BF16))
                wob = EA(nc.sbuf_tensor("wob", [128, KT, D], BF16))
                ewb = EA(nc.sbuf_tensor("ewb", [128, KT, CHA], F32))
                ones2 = EA(nc.sbuf_tensor("ones2", [128, CHA], F32))
                xa = [EA(nc.sbuf_tensor(f"xa{i}", [128, NTA, D], F32)) for i in range(2)]
                st1 = [EA(nc.sbuf_tensor(f"st1_{i}", [128, NTA, D], F32)) for i in range(2)]
                sqb = EA(nc.sbuf_tensor("sqb", [128, 1, D], F32))
                hch = [EA(nc.sbuf_tensor(f"hch{i}", [128, KT, CHA + 1], F32))
                       for i in range(2)]
                dbf = EA(nc.sbuf_tensor("dbf", [128, KT, CHA], F32))
                mixb = {m: [EA(nc.sbuf_tensor(f"mx{m}{i}", [128, KT, CHA], BF16))
                            for i in range(2)] for m in "kvr"}
                eKb = [EA(nc.sbuf_tensor(f"eK{i}", [128, CHA], F32)) for i in range(3)]
                erb = [EA(nc.sbuf_tensor(f"er{i}", [128, CHA], F32)) for i in range(3)]
                vvb = [EA(nc.sbuf_tensor(f"vv{i}", [128, CHA], F32)) for i in range(3)]
                ekvb = [EA(nc.sbuf_tensor(f"ekv{i}", [128, CHA], F32)) for i in range(3)]
                NNb = [EA(nc.sbuf_tensor(f"NN{i}", [128, CHA + 1], F32)) for i in range(3)]
                QQb = [EA(nc.sbuf_tensor(f"QQ{i}", [128, CHA + 1], F32)) for i in range(3)]
                rwk = [EA(nc.sbuf_tensor(f"rwk{i}", [128, KT, CHA], BF16))
                       for i in range(2)]
                attc = EA(nc.sbuf_tensor("attc", [128, KT, CHA], F32))

                for buf, src in [(wkb, "att_wk"), (wvb, "att_wv"),
                                 (wrb, "att_wr"), (wob, "att_wo")]:
                    GD.go(buf[:], wn[src].rearrange("(k p) c -> p k c", p=128),
                          sem=S["sbw"])
                V.memset(ones2[:], 1.0)
                V.fence(S["sst"])
                A.wait(S["sst"])
                for kk in range(KT):
                    A.act(ewb[:, kk, :], ones2[:], AF.Copy,
                          scale=PB["ew"][:, kk:kk + 1])
                A.fence(S["sst"])
                for e in (V, A, G, P):
                    e.wait(S["sbw"])

                chunks = [(b, ci) for b in range(NB) for ci in range(NCA)]
                load_n = {}
                mix_n = {}
                snr_n = {}

                def a1_load(idx):
                    b, ci = chunks[idx]
                    par = idx % 2
                    t0 = ci * CHA
                    wait_free(SY, ("xa", par))
                    SY.go(xa[par][:], X[b, t0:t0 + CHA, :]
                          .rearrange("(n p) c -> p n c", p=128), sem=S["sx"])
                    load_n[idx] = S["sx"].n

                def a1_front(idx):
                    b, ci = chunks[idx]
                    par = idx % 2
                    if idx + 1 < len(chunks):
                        a1_load(idx + 1)
                    sb = par * 16
                    A.wait(S["sx"], load_n[idx])
                    G.wait(S["sx"], load_n[idx])
                    ln_stats(xa[par], sqb, NTA, mu_c=6, rstd_c=7, base=sb)
                    o = sb
                    sl = lambda i: stt_[:, 0:NTA, o + i:o + i + 1]
                    G.wait(S["srd"])
                    G.tt(sl(3), sl(7), sl(7), ALU.mult)
                    G.tt(sl(12), sl(4), sl(3), ALU.mult)
                    G.fence(S["sst"])
                    A.wait(S["sst"])
                    A.act(sl(5), sl(12), AF.Ln, bias=EPS)
                    A.act(sl(13), sl(5), AF.Exp, scale=-0.5)
                    A.fence(S["sst"])
                    G.wait(S["sst"])
                    G.tt(sl(10), sl(7), sl(13), ALU.mult)
                    G.tt(sl(11), sl(6), sl(13), ALU.mult)
                    wait_free(G, ("st1", par))
                    for n in range(NTA):
                        G.ts(st1[par][:, n, :], xa[par][:, n, :],
                             stt_[:, n, o + 7:o + 8], ALU.mult,
                             stt_[:, n, o + 6:o + 7], ALU.add)
                    for n in range(NTA):
                        G.ts(xa[par][:, n, :], xa[par][:, n, :],
                             stt_[:, n, o + 10:o + 11], ALU.mult,
                             stt_[:, n, o + 11:o + 12], ALU.add)
                    G.fence(S["snr"])
                    consumed(("st", id(stt_), o, 6), S["snr"])
                    snr_n[idx] = S["snr"].n
                    P.wait(S["snr"])
                    wait_free(A, ("hch", par))
                    transpose_affine(xa[par], hch[par], ("hch", par),
                                     PB["lag"], PB["lab"])
                    A.fence(S["saf"])
                    consumed(("xa", par), S["saf"])
                    G.wait(S["saf"])
                    if ci == 0:
                        G.memset(hch[par][:, :, 0:1], 0.0)
                    else:
                        G.copy(hch[par][:, :, 0:1],
                               hch[1 - par][:, :, CHA:CHA + 1])
                    h1 = hch[par][:, :, 1:CHA + 1]
                    h0 = hch[par][:, :, 0:CHA]
                    wait_free(G, ("dbf",))
                    G.tt(dbf[:], h1, h0, ALU.subtract)
                    G.fence(S["ssc"])
                    consumed(("hch", 1 - par), S["ssc"])
                    V.wait(S["ssc"])
                    for m, mixp in [("k", "mk"), ("v", "mv"), ("r", "mr")]:
                        wait_free(V, ("mix" + m, par))
                        for kk in range(KT):
                            V.stt(mixb[m][par][:, kk, :], dbf[:, kk, :],
                                  PB[mixp][:, kk:kk + 1], h0[:, kk, :],
                                  ALU.mult, ALU.add)
                    V.fence(S["smx"])
                    mix_n[idx] = S["smx"].n
                    consumed(("dbf",), S["smx"])

                def a1_back(idx):
                    b, ci = chunks[idx]
                    par = idx % 2
                    t0 = ci * CHA
                    if ci == 0:
                        V.memset(carN[:], 0.0)
                        V.memset(carQ[:], 0.0)
                    P.wait(S["smx"], mix_n[idx])
                    ev_n = {}
                    v1_n = {}
                    g2_n = {}
                    a3_n = {}

                    def a1_mm(j):
                        for gi, m in enumerate(("k", "v", "r")):
                            bank = (j * 3 + gi) % 6
                            bank = bank if bank < 4 else bank + 2
                            bank_wait(bank)
                            wsb = {"k": wkb, "v": wvb, "r": wrb}[m]
                            for kk in range(KT):
                                ii = P.mm(psum[bank][:, 0:CHA],
                                          wsb[:, kk, j * 128:(j + 1) * 128],
                                          mixb[m][par][:, kk, :],
                                          start=(kk == 0), stop=(kk == KT - 1))
                            S["smm"].bump(ii, 1)

                    def a1_ev(j):
                        jp = j % 3
                        A.wait(S["smm"])
                        wait_free(A, ("wkv", jp))
                        bks = []
                        for gi in range(3):
                            bank = (j * 3 + gi) % 6
                            bks.append(bank if bank < 4 else bank + 2)
                        A.act(eKb[jp][:], psum[bks[0]][:, 0:CHA], AF.Exp)
                        A.act(vvb[jp][:], psum[bks[1]][:, 0:CHA], AF.Copy)
                        ia = A.act(erb[jp][:], psum[bks[2]][:, 0:CHA],
                                   AF.Exp, scale=-1.0)
                        S["sek"].bump(ia, 1)
                        ev_n[j] = S["sek"].n
                        for bank in bks:
                            bank_set(bank, S["sek"])

                    def a1_g1(j):
                        jp = j % 3
                        G.wait(S["sek"], ev_n[j])
                        ig = G.tt(ekvb[jp][:], eKb[jp][:], vvb[jp][:], ALU.mult)
                        S["ssc"].bump(ig, 1)

                    def a1_v1(j):
                        jp = j % 3
                        V.wait(S["ssc"])
                        V.copy(NNb[jp][:, 0:1], carN[:, j:j + 1])
                        V.copy(QQb[jp][:, 0:1], carQ[:, j:j + 1])
                        V.scan(NNb[jp][:, 1:CHA + 1], ewb[:, j, :],
                               ekvb[jp][:], carN[:, j:j + 1])
                        V.scan(QQb[jp][:, 1:CHA + 1], ewb[:, j, :],
                               eKb[jp][:], carQ[:, j:j + 1])
                        V.copy(carN[:, j:j + 1], NNb[jp][:, CHA:CHA + 1])
                        V.copy(carQ[:, j:j + 1], QQb[jp][:, CHA:CHA + 1])
                        V.stt(ekvb[jp][:], ekvb[jp][:], PB["eu"][:, j:j + 1],
                              NNb[jp][:, 0:CHA], ALU.mult, ALU.add)
                        iv = V.stt(eKb[jp][:], eKb[jp][:],
                                   PB["eu"][:, j:j + 1],
                                   QQb[jp][:, 0:CHA], ALU.mult, ALU.add)
                        S["sdv"].bump(iv, 1)
                        v1_n[j] = S["sdv"].n

                    def a1_gd(j):
                        jp = j % 3
                        G.wait(S["sdv"], v1_n[j])
                        G.ts(erb[jp][:], erb[jp][:], 1.0, ALU.add)
                        ig = G.tt(eKb[jp][:], eKb[jp][:], erb[jp][:], ALU.mult)
                        S["ssc"].bump(ig, 1)

                    def a1_ar(j):
                        jp = j % 3
                        A.wait(S["ssc"])
                        A.act(eKb[jp][:], eKb[jp][:], AF.Ln)
                        ia = A.act(eKb[jp][:], eKb[jp][:], AF.Exp, scale=-1.0)
                        S["sek"].bump(ia, 1)
                        a3_n[j] = S["sek"].n

                    def a1_v2(j):
                        jp = j % 3
                        V.wait(S["sek"], a3_n[j])
                        if j == 0:
                            wait_free(V, ("rwk", par))
                        iv = V.tt(rwk[par][:, j, :], ekvb[jp][:], eKb[jp][:],
                                  ALU.mult)
                        S["sdv"].bump(iv, 1)
                        consumed(("wkv", jp), S["sdv"])

                    for j in range(KT):
                        a1_mm(j)
                        a1_ev(j)
                        a1_g1(j)
                        if j >= 1:
                            a1_gd(j - 1)
                        a1_v1(j)
                        if j >= 1:
                            a1_ar(j - 1)
                        if j >= 2:
                            a1_v2(j - 2)
                    a1_gd(KT - 1)
                    a1_ar(KT - 1)
                    a1_v2(KT - 2)
                    a1_v2(KT - 1)
                    rwk_ready = S["sdv"].n
                    consumed(("mixk", par), S["smm"])
                    consumed(("mixv", par), S["smm"])
                    consumed(("mixr", par), S["smm"])

                    P.wait(S["sdv"], rwk_ready)
                    wait_free(A, ("attc",))
                    for co in range(KT):
                        bank = co % 2
                        bank_wait(bank)
                        for kk in range(KT):
                            ii = P.mm(psum[bank][:, 0:CHA],
                                      wob[:, kk, co * 128:(co + 1) * 128],
                                      rwk[par][:, kk, :],
                                      start=(kk == 0), stop=(kk == KT - 1))
                        S["swo"].bump(ii, 1)
                        A.wait(S["swo"])
                        ia = A.act(attc[:, co, :], psum[bank][:, 0:CHA], AF.Copy)
                        S["sek"].bump(ia, 1)
                        bank_set(bank, S["sek"])
                    consumed(("rwk", par), S["swo"])
                    P.wait(S["sek"])
                    V.wait(S["snr"], snr_n[idx])
                    transpose_add(attc, st1[par], NTA)
                    consumed(("attc",), S["str_"])
                    SY.wait(S["sad"])
                    SY.go(S_XPP[b, t0:t0 + CHA, :]
                          .rearrange("(n p) c -> p n c", p=128),
                          st1[par][:], sem=S["sou"])
                    consumed(("st1", par), S["sou"])

                a1_load(0)
                a1_front(0)
                for idx in range(1, len(chunks)):
                    a1_back(idx - 1)
                    a1_front(idx)
                a1_back(len(chunks) - 1)
            barrier()

        if "A2a" in phases:
            sc_A2a = st.enter_context(nc.named_scope("PH_A2a"))
            with contextlib.ExitStack() as sa:
                EA = sa.enter_context
                fwkb = EA(nc.sbuf_tensor("fwkb", [128, KT, DFF], BF16))
                fwrb = EA(nc.sbuf_tensor("fwrb", [128, KT, D], BF16))
                xa2 = [EA(nc.sbuf_tensor(f"xa2_{i}", [128, NTA, D], F32))
                       for i in range(2)]
                sqb2 = EA(nc.sbuf_tensor("sqb2", [128, 1, D], F32))
                h2pre = EA(nc.sbuf_tensor("h2pre", [128, NTA, D], F32))
                h2ch = [EA(nc.sbuf_tensor(f"h2ch{i}", [128, KT, CHA + 1], F32))
                        for i in range(2)]
                dbf2 = EA(nc.sbuf_tensor("dbf2", [128, KT, CHA], F32))
                mfk = [EA(nc.sbuf_tensor(f"mfk{i}", [128, KT, CHA], BF16))
                       for i in range(2)]
                mfr = [EA(nc.sbuf_tensor(f"mfr{i}", [128, KT, CHA], BF16))
                       for i in range(2)]
                kfb = [EA(nc.sbuf_tensor(f"kfb{i}", [128, JT, CHA], BF16))
                       for i in range(2)]
                trel = EA(nc.sbuf_tensor("trel", [128, 4, CHA], F32))
                sr2b = [EA(nc.sbuf_tensor(f"sr2b{i}", [128, KT, CHA], F32))
                        for i in range(2)]

                GD.go(fwkb[:], wn["ffn_wk"].rearrange("(k p) c -> p k c", p=128),
                      sem=S["sbw"])
                GD.go(fwrb[:], wn["ffn_wr"].rearrange("(k p) c -> p k c", p=128),
                      sem=S["sbw"])
                for e in (V, A, G, P):
                    e.wait(S["sbw"])

                chunks = [(b, ci) for b in range(NB) for ci in range(NCA)]
                load_n = {}
                mix_n = {}

                def a2a_load(idx):
                    b, ci = chunks[idx]
                    par = idx % 2
                    t0 = ci * CHA
                    wait_free(SY, ("xa2", par))
                    SY.go(xa2[par][:], S_XPP[b, t0:t0 + CHA, :]
                          .rearrange("(n p) c -> p n c", p=128), sem=S["sx"])
                    load_n[idx] = S["sx"].n

                def a2a_front(idx):
                    b, ci = chunks[idx]
                    par = idx % 2
                    if idx + 1 < len(chunks):
                        a2a_load(idx + 1)
                    sb = par * 16
                    A.wait(S["sx"], load_n[idx])
                    G.wait(S["sx"], load_n[idx])
                    ln_stats(xa2[par], sqb2, NTA, mu_c=6, rstd_c=7, base=sb)
                    wait_free(G, ("h2pre",))
                    ln_norm(h2pre, xa2[par], NTA, mu_c=6, rstd_c=7,
                            base=sb, on=G)
                    consumed(("xa2", par), S["snr"])
                    P.wait(S["snr"])
                    wait_free(A, ("h2ch", par))
                    transpose_affine(h2pre, h2ch[par], ("h2ch", par),
                                     PB["lfg"], PB["lfb"])
                    A.fence(S["saf"])
                    consumed(("h2pre",), S["saf"])
                    G.wait(S["saf"])
                    if ci == 0:
                        G.memset(h2ch[par][:, :, 0:1], 0.0)
                    else:
                        G.copy(h2ch[par][:, :, 0:1],
                               h2ch[1 - par][:, :, CHA:CHA + 1])
                    h1 = h2ch[par][:, :, 1:CHA + 1]
                    h0 = h2ch[par][:, :, 0:CHA]
                    wait_free(G, ("dbf2",))
                    G.tt(dbf2[:], h1, h0, ALU.subtract)
                    G.fence(S["ssc"])
                    consumed(("h2ch", 1 - par), S["ssc"])
                    V.wait(S["ssc"])
                    wait_free(V, ("mfk", par))
                    for kk in range(KT):
                        V.stt(mfk[par][:, kk, :], dbf2[:, kk, :],
                              PB["fmk"][:, kk:kk + 1], h0[:, kk, :],
                              ALU.mult, ALU.add)
                    wait_free(V, ("mfr", par))
                    for kk in range(KT):
                        V.stt(mfr[par][:, kk, :], dbf2[:, kk, :],
                              PB["fmr"][:, kk:kk + 1], h0[:, kk, :],
                              ALU.mult, ALU.add)
                    V.fence(S["smx"])
                    mix_n[idx] = S["smx"].n
                    consumed(("dbf2",), S["smx"])

                def a2a_back(idx):
                    b, ci = chunks[idx]
                    par = idx % 2
                    t0 = ci * CHA
                    tg = b * L + t0
                    P.wait(S["smx"], mix_n[idx])
                    wait_free(A, ("kfb", par))
                    wait_free(V, ("kfb", par))
                    for j in range(JT):
                        bank = j % 2
                        bank_wait(bank)
                        for kk in range(KT):
                            ii = P.mm(psum[bank][:, 0:CHA],
                                      fwkb[:, kk, j * 128:(j + 1) * 128],
                                      mfk[par][:, kk, :],
                                      start=(kk == 0), stop=(kk == KT - 1))
                        S["smm"].bump(ii, 1)
                        A.wait(S["smm"])
                        if j % 4 == 0 and j > 0:
                            A.wait(S["sdv"])
                        ia = A.act(trel[:, j % 4, :], psum[bank][:, 0:CHA],
                                   AF.Relu)
                        S["sek"].bump(ia, 1)
                        bank_set(bank, S["sek"])
                        V.wait(S["sek"])
                        iv = V.tt(kfb[par][:, j, :], trel[:, j % 4, :],
                                  trel[:, j % 4, :], ALU.mult)
                        if j % 4 == 3 or j == JT - 1:
                            S["sdv"].bump(iv, 1)
                    consumed(("mfk", par), S["smm"])
                    wait_free(A, ("sr2b", par))
                    for co in range(KT):
                        bank = 2 + co % 2
                        bank_wait(bank)
                        for kk in range(KT):
                            ii = P.mm(psum[bank][:, 0:CHA],
                                      fwrb[:, kk, co * 128:(co + 1) * 128],
                                      mfr[par][:, kk, :],
                                      start=(kk == 0), stop=(kk == KT - 1))
                        S["smm"].bump(ii, 1)
                        A.wait(S["smm"])
                        ia = A.act(sr2b[par][:, co, :], psum[bank][:, 0:CHA],
                                   AF.Exp, scale=-1.0)
                        S["sek"].bump(ia, 1)
                        bank_set(bank, S["sek"])
                    consumed(("mfr", par), S["smm"])
                    A.act(sr2b[par][:], sr2b[par][:], AF.Ln, bias=1.0)
                    A.act(sr2b[par][:], sr2b[par][:], AF.Exp, scale=-1.0)
                    A.fence(S["sek"])
                    SY.wait(S["sek"])
                    SY.wait(S["sdv"])
                    SY.go(kf_r[:, :, tg:tg + CHA], kfb[par][:], sem=S["sou"])
                    SY.go(sr2_r[:, :, tg:tg + CHA], sr2b[par][:], sem=S["sou"])
                    consumed(("kfb", par), S["sou"])
                    consumed(("sr2b", par), S["sou"])

                a2a_load(0)
                a2a_front(0)
                for idx in range(1, len(chunks)):
                    a2a_back(idx - 1)
                    a2a_front(idx)
                a2a_back(len(chunks) - 1)
            barrier()

        if "A2b" in phases:
            sc_A2b = st.enter_context(nc.named_scope("PH_A2b"))
            with contextlib.ExitStack() as sa:
                EA = sa.enter_context
                fwvb = EA(nc.sbuf_tensor("fwvb", [128, JT, D], BF16))
                kfw = [EA(nc.sbuf_tensor(f"kfw{i}", [128, JT, CHA], BF16))
                       for i in range(2)]
                sr2w = [EA(nc.sbuf_tensor(f"sr2w{i}", [128, KT, CHA], F32))
                        for i in range(2)]
                xa3 = [EA(nc.sbuf_tensor(f"xa3_{i}", [128, NTA, D], F32))
                       for i in range(2)]
                xr = [EA(nc.sbuf_tensor(f"xr{i}", [128, NTA, D], F32))
                      for i in range(2)]
                ffc = [EA(nc.sbuf_tensor(f"ffc{i}", [128, KT, CHA], F32))
                       for i in range(2)]
                sqb3 = EA(nc.sbuf_tensor("sqb3", [128, 1, D], F32))
                lnb = [EA(nc.sbuf_tensor(f"lnb{i}", [128, NTA, D], F32))
                       for i in range(2)]
                gbuf = [EA(nc.sbuf_tensor(f"gbuf{i}", [128, NTA, D], F32))
                        for i in range(2)]
                gbfo = [EA(nc.sbuf_tensor(f"gbfo{i}", [128, KT, CHA], BF16))
                        for i in range(2)]

                GD.go(fwvb[:], wn["ffn_wv"].rearrange("(j p) c -> p j c", p=128),
                      sem=S["sbw"])
                for e in (V, A, G, P):
                    e.wait(S["sbw"])

                chunks = [(b, ci) for b in range(NB) for ci in range(NCA)]
                load_n = {}
                ffc_n = {}

                def a2b_load(idx):
                    b, ci = chunks[idx]
                    par = idx % 2
                    t0 = ci * CHA
                    tg = b * L + t0
                    wait_free(SY, ("kfw", par))
                    SY.go(kfw[par][:], kf_r[:, :, tg:tg + CHA], sem=S["sx"])
                    wait_free(SY, ("sr2w", par))
                    SY.go(sr2w[par][:], sr2_r[:, :, tg:tg + CHA], sem=S["sx"])
                    wait_free(SY, ("xa3", par))
                    SY.go(xa3[par][:], S_XPP[b, t0:t0 + CHA, :]
                          .rearrange("(n p) c -> p n c", p=128), sem=S["sx"])
                    wait_free(SY, ("xr", par))
                    SY.go(xr[par][:], X[b, t0:t0 + CHA, :]
                          .rearrange("(n p) c -> p n c", p=128), sem=S["sx"])
                    load_n[idx] = S["sx"].n

                def a2b_back1(idx):
                    b, ci = chunks[idx]
                    par = idx % 2
                    P.wait(S["sx"], load_n[idx])
                    V.wait(S["sx"], load_n[idx])
                    wait_free(V, ("ffc", par))
                    for co in range(KT):
                        bank = co % 2
                        bank_wait(bank)
                        for j in range(JT):
                            ii = P.mm(psum[bank][:, 0:CHA],
                                      fwvb[:, j, co * 128:(co + 1) * 128],
                                      kfw[par][:, j, :],
                                      start=(j == 0), stop=(j == JT - 1))
                        S["smm"].bump(ii, 1)
                        V.wait(S["smm"])
                        iv = V.tt(ffc[par][:, co, :], sr2w[par][:, co, :],
                                  psum[bank][:, 0:CHA], ALU.mult)
                        S["sdv"].bump(iv, 1)
                        bank_set(bank, S["sdv"])
                    V.fence(S["sdv"])
                    ffc_n[idx] = S["sdv"].n
                    consumed(("kfw", par), S["smm"])
                    consumed(("sr2w", par), S["sdv"])

                def a2b_back2a(idx):
                    b, ci = chunks[idx]
                    par = idx % 2
                    t0 = ci * CHA
                    sb = par * 16
                    P.wait(S["sdv"], ffc_n[idx])
                    transpose_add(ffc[par], xa3[par], NTA)
                    consumed(("ffc", par), S["str_"])
                    for e in (A, G):
                        e.wait(S["sad"])
                    ln_stats(xa3[par], sqb3, NTA, mu_c=6, rstd_c=7, base=sb)
                    wait_free(V, ("lnb", par))
                    ln_norm(lnb[par], xa3[par], NTA, mu_c=6, rstd_c=7, base=sb)
                    consumed(("xa3", par), S["snr"])
                    V.wait(S["sx"], load_n[idx])
                    V.tt(lnb[par][:], lnb[par][:], xr[par][:], ALU.add)
                    V.fence(S["snr"])
                    consumed(("xr", par), S["snr"])
                    SY.wait(S["snr"])
                    SY.go(S_Y[b, t0:t0 + CHA, :]
                          .rearrange("(n p) c -> p n c", p=128), lnb[par][:],
                          sem=S["sou"])
                    for e in (A, G):
                        e.wait(S["snr"])
                    ln_stats(lnb[par], sqb3, NTA, mu_c=6, rstd_c=7, base=sb)

                def a2b_back2b(idx):
                    b, ci = chunks[idx]
                    par = idx % 2
                    t0 = ci * CHA
                    tg = b * L + t0
                    cg = b * NCA + ci
                    sb = par * 16
                    wait_free(V, ("gbuf", par))
                    ln_norm(gbuf[par], lnb[par], NTA, mu_c=6, rstd_c=7, base=sb)
                    consumed(("lnb", par), S["snr"])
                    SY.wait(S["snr"])
                    SY.go(S_G32[b, t0:t0 + CHA, :]
                          .rearrange("(n p) c -> p n c", p=128),
                          gbuf[par][:], sem=S["sou"])
                    P.wait(S["snr"])
                    wait_free(A, ("gbfo", par))
                    transpose_affine(gbuf[par], gbfo[par], ("gbfo", par),
                                     None, None,
                                     accum_slot=lambda kk, _cg=cg:
                                     pacc[:, kk, _cg:_cg + 1])
                    A.fence(S["saf"])
                    consumed(("gbuf", par), S["saf"])
                    SY.wait(S["saf"])
                    SY.go(gbf_r[:, :, tg:tg + CHA], gbfo[par][:], sem=S["sou"])
                    consumed(("gbfo", par), S["sou"])

                a2b_load(0)
                a2b_load(1)
                a2b_back1(0)
                for idx in range(1, len(chunks)):
                    a2b_back1(idx)
                    if idx >= 2:
                        a2b_back2b(idx - 2)
                    a2b_back2a(idx - 1)
                    if idx + 1 < len(chunks):
                        a2b_load(idx + 1)
                a2b_back2b(len(chunks) - 2)
                a2b_back2a(len(chunks) - 1)
                a2b_back2b(len(chunks) - 1)
            barrier()

        if "R" in phases:
            sc_R = st.enter_context(nc.named_scope("PH_R"))
            with contextlib.ExitStack() as sa:
                EA = sa.enter_context
                W1s = EA(nc.sbuf_tensor("W1s", [128, KT, 256], F32))
                W2s = EA(nc.sbuf_tensor("W2s", [128, 2, NZ], F32))
                b1bc = EA(nc.sbuf_tensor("b1bc", [2, 256], F32))
                b2bc = EA(nc.sbuf_tensor("b2bc", [2, NZ], F32))
                pl2 = EA(nc.sbuf_tensor("pl2", [128, KT, 2], F32))
                hrt = EA(nc.sbuf_tensor("hrt", [2, 256], F32))
                hrtT = EA(nc.sbuf_tensor("hrtT", [128, 2, 2], F32))
                rsc = EA(nc.sbuf_tensor("rsc", [2, 64], F32))
                wzrow = EA(nc.sbuf_tensor("wzrow", [1, 16], F32))

                SY.go(W1s[:], wn["W1"].rearrange("(k p) h -> p k h", p=128),
                      sem=S["srt"])
                SY.go(W2s[:], wn["W2"].rearrange("(k p) z -> p k z", p=128),
                      sem=S["srt"])
                for prt in range(2):
                    SY.go(b1bc[prt:prt + 1, :], wn["b1"].rearrange("(o h) -> o h", o=1),
                          sem=S["srt"])
                    SY.go(b2bc[prt:prt + 1, :], wn["b2"].rearrange("(o z) -> o z", o=1),
                          sem=S["srt"])
                for e in (V, A, G, P):
                    e.wait(S["srt"])
                for b in range(NB):
                    V.red(pl2[:, :, b:b + 1],
                          pacc[:, :, b * NCA:(b + 1) * NCA], ALU.add)
                V.ts(pl2[:], pl2[:], 1.0 / L, ALU.mult)
                V.fence(S["sst"])
                P.wait(S["sst"])
                bank_wait(6)
                for kk in range(KT):
                    ii = P.mm(psum[6][0:2, 0:256], pl2[:, kk, :], W1s[:, kk, :],
                              start=(kk == 0), stop=(kk == KT - 1))
                S["smm"].bump(ii, 1)
                V.wait(S["smm"])
                V.tt(hrt[:], psum[6][0:2, 0:256], b1bc[:], ALU.add)
                V.fence(S["sdv"])
                bank_set(6, S["sdv"])
                A.wait(S["sdv"])
                A.act(hrt[:], hrt[:], AF.Tanh)
                A.fence(S["sek"])
                P.wait(S["sek"])
                bank_wait(4)
                for kk in range(2):
                    ii = P.tr(psum[4][:, kk * 2:kk * 2 + 2],
                              hrt[0:2, kk * 128:(kk + 1) * 128], idn[0:2, 0:2])
                S["str_"].bump(ii, 1)
                A.wait(S["str_"])
                A.act(hrtT[:, 0, :], psum[4][:, 0:2], AF.Copy)
                A.act(hrtT[:, 1, :], psum[4][:, 2:4], AF.Copy)
                A.fence(S["sek"])
                bank_set(4, S["sek"])
                P.wait(S["sek"])
                for kk in range(2):
                    ii = P.mm(psum[6][0:2, 256:256 + NZ], hrtT[:, kk, :],
                              W2s[:, kk, :], start=(kk == 0), stop=(kk == 1))
                S["smm"].bump(ii, 1)
                V.wait(S["smm"])
                lg = rsc[:, 0:8]
                V.tt(lg, psum[6][0:2, 256:256 + NZ], b2bc[:], ALU.add)
                V.red(rsc[:, 8:9], lg, ALU.max)
                V.ts(rsc[:, 8:9], rsc[:, 8:9], -1.0, ALU.mult)
                V.fence(S["sdv"])
                bank_set(6, S["sdv"])
                A.wait(S["sdv"])
                A.act(lg, lg, AF.Exp, bias=rsc[:, 8:9])
                A.fence(S["sek"])
                V.wait(S["sek"])
                V.red(rsc[:, 9:10], lg, ALU.add)
                V.recip(rsc[:, 9:10], rsc[:, 9:10])
                V.ts(lg, lg, rsc[:, 9:10], ALU.mult)
                V.red(rsc[:, 18:19], lg, ALU.max)
                V.ts(rsc[:, 40:48], lg, rsc[:, 18:19], ALU.is_ge)
                V.stt(rsc[:, 10:18], rsc[:, 40:48], -1e30, lg, ALU.mult, ALU.add)
                V.red(rsc[:, 19:20], rsc[:, 10:18], ALU.max)
                V.ts(rsc[:, 48:56], rsc[:, 10:18], rsc[:, 19:20], ALU.is_ge)
                V.stt(rsc[:, 26:34], rsc[:, 48:56], -1e30, rsc[:, 10:18],
                      ALU.mult, ALU.add)
                V.red(rsc[:, 20:21], rsc[:, 26:34], ALU.max)
                V.ts(rsc[:, 56:64], rsc[:, 26:34], rsc[:, 20:21], ALU.is_ge)
                V.tt(rsc[:, 35:36], rsc[:, 18:19], rsc[:, 19:20], ALU.add)
                V.tt(rsc[:, 35:36], rsc[:, 35:36], rsc[:, 20:21], ALU.add)
                V.recip(rsc[:, 36:37], rsc[:, 35:36])
                for r_ in range(3):
                    V.ts(rsc[:, 24 + r_:25 + r_], rsc[:, 18 + r_:19 + r_],
                         rsc[:, 36:37], ALU.mult)
                for r_ in range(3):
                    V.tt(rsc[:, 10:18], rsc[:, 40 + 8 * r_:48 + 8 * r_],
                         iota8f[:], ALU.mult)
                    V.red(rsc[:, 21 + r_:22 + r_], rsc[:, 10:18], ALU.add)
                V.fence(S["sdv"])
                SY.wait(S["sdv"])
                SY.go(wzrow[0:1, 0:3], rsc[0:1, 24:27], sem=S["srt"])
                SY.go(wzrow[0:1, 3:6], rsc[1:2, 24:27], sem=S["srt"])
                SY.go(wzrow[0:1, 8:11], rsc[0:1, 21:24], sem=S["srt"])
                SY.go(wzrow[0:1, 11:14], rsc[1:2, 21:24], sem=S["srt"])
                P.wait(S["srt"])
                bank_wait(6)
                for be in list(range(6)) + list(range(8, 14)):
                    ii = P.mm(psum[6][:, be:be + 1], onesr[0:1, :],
                              wzrow[0:1, be:be + 1], start=True, stop=True)
                S["smm"].bump(ii, 1)
                A.wait(S["smm"])
                A.act(wzbc[:], psum[6][:, 0:16], AF.Copy)
                A.fence(S["sek"])
                bank_set(6, S["sek"])
                G.wait(S["sek"])
                for ps_ in range(6):
                    ecol = wzbc[:, 8 + ps_:9 + ps_]
                    G.ts(rsF[:, 0:1], ecol, 1024.0, ALU.mult)
                    G.ts(idxTf[:, 0:8], jjoff_f[:, 0:8], rsF[:, 0:1], ALU.add)
                    G.copy(idxAi[:, 8 * ps_:8 * (ps_ + 1)], idxTf[:, 0:8])
                    G.ts(rsF[:, 0:1], ecol, 4096.0, ALU.mult)
                    G.ts(idxTf[:], jjoff_f[:], rsF[:, 0:1], ALU.add)
                    G.copy(idxBi[:, 32 * ps_:32 * (ps_ + 1)], idxTf[:])
                G.fence(S["ssc"])
            barrier()

        if "B" in phases:
            sc_B = st.enter_context(nc.named_scope("PH_B"))
            with contextlib.ExitStack() as sa:
                EA = sa.enter_context
                wab = EA(nc.sbuf_tensor("wab", [128, KT, DFF], BF16))
                wbb = EA(nc.sbuf_tensor("wbb", [128, JT, D], BF16))
                gw = [EA(nc.sbuf_tensor(f"gw{i}", [128, KT, TB], BF16))
                      for i in range(2)]
                zb = EA(nc.sbuf_tensor("zb", [128, JT, TB], BF16))
                zcb = EA(nc.sbuf_tensor("zcb", [128, KT, TB], F32))

                WaF = wn["Wa"].rearrange("e r c -> (e r) c")
                WbF = wn["Wb"].rearrange("e r c -> (e r) c")
                NPASS = NB * 3
                NCBP = L // TB

                def load_wa(p):
                    for kk in range(KT):
                        gi = GD.e.indirect_dma_start(
                            wab[:, kk, :], None, WaF,
                            IndirectOffsetOnAxis(
                                idxAi[:, 8 * p + kk:8 * p + kk + 1], axis=0))
                        S["sbw"].bump(gi, 16)
                    return S["sbw"].n

                def load_wb(p):
                    for jj in range(JT):
                        gi = GD.e.indirect_dma_start(
                            wbb[:, jj, :], None, WbF,
                            IndirectOffsetOnAxis(
                                idxBi[:, 32 * p + jj:32 * p + jj + 1], axis=0))
                        S["sbw"].bump(gi, 16)
                    return S["sbw"].n

                wa_ready = load_wa(0)
                wb_ready = load_wb(0)

                m2_done = None
                for p in range(NPASS):
                    b = p // 3
                    r = p % 3
                    for ch in range(NCBP):
                        tg = b * L + ch * TB
                        par = ch % 2
                        wait_free(SY, ("gw", par))
                        SY.go(gw[par][:], gbf_r[:, :, tg:tg + TB], sem=S["sbg"])
                        P.wait(S["sbg"])
                        if ch == 0:
                            P.wait(S["sbw"], wa_ready)
                        if m2_done is not None:
                            A.e.wait_ge(m2_done[0].h, m2_done[1])
                        for j in range(JT):
                            bank = j % 2
                            bank_wait(bank)
                            for kk in range(KT):
                                ii = P.mm(psum[bank][:, 0:TB],
                                          wab[:, kk, j * 128:(j + 1) * 128],
                                          gw[par][:, kk, :],
                                          start=(kk == 0), stop=(kk == KT - 1))
                            S["sbm1"].bump(ii, 1)
                            A.wait(S["sbm1"])
                            ia = A.act(zb[:, j, :], psum[bank][:, 0:TB],
                                       AF.Gelu_apprx_tanh)
                            S["sbz"].bump(ia, 1)
                            bank_set(bank, S["sbz"])
                        consumed(("gw", par), S["sbm1"])
                        if ch == NCBP - 1 and p < NPASS - 1:
                            GD.wait(S["sbm1"])
                            wa_ready = load_wa(p + 1)
                        P.wait(S["sbz"])
                        if p == 0 and ch == 0:
                            P.wait(S["sbw"], wb_ready)
                        for cog in range(4):
                            bk0 = 2 + 2 * (cog % 2)
                            bk1 = 3 + 2 * (cog % 2)
                            bank_wait(bk0)
                            bank_wait(bk1)
                            for j in range(JT):
                                P.mm(psum[bk0][:, 0:TB],
                                     wbb[:, j, (2 * cog) * 128:(2 * cog + 1) * 128],
                                     zb[:, j, :], start=(j == 0), stop=(j == JT - 1))
                                ii = P.mm(psum[bk1][:, 0:TB],
                                          wbb[:, j,
                                              (2 * cog + 1) * 128:(2 * cog + 2) * 128],
                                          zb[:, j, :], start=(j == 0),
                                          stop=(j == JT - 1))
                            S["sbm2"].bump(ii, 1)
                            V.wait(S["sbm2"])
                            wait_free(V, ("zcb", cog))
                            V.ts(zcb[:, 2 * cog, :], psum[bk0][:, 0:TB],
                                 wzbc[:, p:p + 1], ALU.mult)
                            V.ts(zcb[:, 2 * cog + 1, :], psum[bk1][:, 0:TB],
                                 wzbc[:, p:p + 1], ALU.mult)
                            V.fence(S["sbev"])
                            bank_set(bk0, S["sbev"])
                            bank_set(bk1, S["sbev"])
                        m2_done = (S["sbm2"], S["sbm2"].n)
                        GD.wait(S["sbev"])
                        if r > 0:
                            GD.go(zac_r[:, :, tg:tg + TB], zcb[:],
                                  sem=S["sbac"], accum_op=ALU.add)
                        else:
                            GD.go(zac_r[:, :, tg:tg + TB], zcb[:], sem=S["sbac"])
                        for cog in range(4):
                            consumed(("zcb", cog), S["sbac"])
                    if p < NPASS - 1:
                        GD.wait(S["sbm2"])
                        wb_ready = load_wb(p + 1)
            barrier()

        if "C" in phases:
            sc_C = st.enter_context(nc.named_scope("PH_C"))
            with contextlib.ExitStack() as sa:
                EA = sa.enter_context
                zw = [EA(nc.sbuf_tensor(f"zw{i}", [128, KT, TC], F32))
                      for i in range(2)]
                g32w = [EA(nc.sbuf_tensor(f"g32w{i}", [128, NTC, D], F32))
                        for i in range(2)]
                yw = [EA(nc.sbuf_tensor(f"yw{i}", [128, NTC, D], F32))
                      for i in range(2)]
                sqc = EA(nc.sbuf_tensor("sqc", [128, 1, D], F32))
                ow = [EA(nc.sbuf_tensor(f"ow{i}", [128, NTC, D], F32))
                      for i in range(2)]
                stc = EA(nc.sbuf_tensor("stc", [128, NTC, 32], F32))

                c_load_n = {}

                def c_load(ch):
                    tg = ch * TC
                    b = tg // L
                    t0 = tg % L
                    par = ch % 2
                    wait_free(SY, ("zw", par))
                    SY.go(zw[par][:], zac_r[:, :, tg:tg + TC], sem=S["sx"])
                    wait_free(SY, ("g32w", par))
                    SY.go(g32w[par][:], S_G32[b, t0:t0 + TC, :]
                          .rearrange("(n p) c -> p n c", p=128), sem=S["sx"])
                    wait_free(SY, ("yw", par))
                    SY.go(yw[par][:], S_Y[b, t0:t0 + TC, :]
                          .rearrange("(n p) c -> p n c", p=128), sem=S["sx"])
                    c_load_n[ch] = S["sx"].n

                c_load(0)
                if NCC > 1:
                    c_load(1)
                for ch in range(NCC):
                    tg = ch * TC
                    b = tg // L
                    t0 = tg % L
                    par = ch % 2
                    o = par * 16
                    P.wait(S["sx"], c_load_n[ch])
                    V.wait(S["sx"], c_load_n[ch])
                    transpose_add(zw[par], g32w[par], NTC)
                    consumed(("zw", par), S["sad"])
                    for e in (A, G):
                        e.wait(S["sad"])
                    wait_free(A, ("stc", o))
                    for n in range(NTC):
                        A.act(sqc[:, 0, :], g32w[par][:, n, :], AF.Square,
                              accum_out=stc[:, n, o + 1:o + 2])
                        A.act(sqc[:, 0, :], g32w[par][:, n, :], AF.Copy,
                              accum_out=stc[:, n, o + 0:o + 1])
                    A.fence(S["srd"])
                    G.wait(S["srd"])
                    c = 1.0 / D
                    scl = lambda i: stc[:, :, o + i:o + i + 1]
                    G.ts(scl(2), scl(0), c, ALU.mult)
                    G.tt(scl(3), scl(2), scl(2), ALU.mult)
                    G.ts(scl(4), scl(1), c, ALU.mult)
                    G.tt(scl(4), scl(4), scl(3), ALU.subtract)
                    G.fence(S["sst"])
                    A.wait(S["sst"])
                    A.act(scl(5), scl(4), AF.Ln, bias=EPS)
                    A.act(scl(7), scl(5), AF.Exp, scale=-0.5)
                    A.fence(S["sst"])
                    G.wait(S["sst"])
                    G.tt(scl(6), scl(2), scl(7), ALU.mult)
                    G.ts(scl(6), scl(6), -1.0, ALU.mult)
                    G.fence(S["srd"])
                    V.wait(S["srd"])
                    wait_free(V, ("ow", par))
                    for n in range(NTC):
                        V.ts(ow[par][:, n, :], g32w[par][:, n, :],
                             stc[:, n, o + 7:o + 8], ALU.mult,
                             stc[:, n, o + 6:o + 7], ALU.add)
                    V.tt(ow[par][:], ow[par][:], yw[par][:], ALU.add)
                    V.fence(S["snr"])
                    consumed(("stc", o), S["snr"])
                    consumed(("g32w", par), S["snr"])
                    consumed(("yw", par), S["snr"])
                    SY.wait(S["snr"])
                    SY.go(OUT[b, t0:t0 + TC, :]
                          .rearrange("(n p) c -> p n c", p=128),
                          ow[par][:], sem=S["sfin"])
                    consumed(("ow", par), S["sfin"])
                    n_out_dma[0] += 1
                    if ch + 2 < NCC:
                        c_load(ch + 2)
            SY.wait(S["sfin"], n_out_dma[0] * 16)
        else:
            SY.wait(S["sou"], S["sou"].n)

    return nc


def kernel(**inputs):
    xs = np.ascontiguousarray(np.asarray(inputs["L0_attn_x"], dtype=np.float32))
    base = {}
    for k, v in inputs.items():
        if k == "L0_attn_x":
            continue
        base[k] = np.ascontiguousarray(np.asarray(v, dtype=np.float32))
    base["identity"] = np.eye(128, dtype=np.float32)
    in_maps = []
    for c in range(8):
        m = dict(base)
        m["x"] = np.ascontiguousarray(xs[2 * c:2 * c + 2])
        in_maps.append(m)
    nc = build()
    res = run_bass_kernel_spmd(nc, in_maps, core_ids=list(range(8)))
    out = np.concatenate([np.asarray(r["out"], dtype=np.float32)
                          for r in res.results], axis=0)
    return out.reshape(16, L, D)


if __name__ == "__main__":
    nc = build()
    print("instructions:", sum(1 for _ in nc.all_instructions()))


# version 08
# speedup vs baseline: 1.1975x; 1.1975x over previous
import contextlib
import numpy as np
import concourse.bass as bass
import concourse.mybir as mybir
from concourse.bass_utils import run_bass_kernel_spmd
from concourse.bass import IndirectOffsetOnAxis

F32 = mybir.dt.float32
BF16 = mybir.dt.bfloat16
AF = mybir.ActivationFunctionType
ALU = mybir.AluOpType
AX = mybir.AxisListType

NB = 2
L = 2048
D = 1024
KT = 8
DFF = 4096
JT = 32
CHA = 256
NCA = L // CHA
NTA = CHA // 128
TB = 512
NCB = NB * L // TB
TC = 512
NCC = NB * L // TC
NTC = TC // 128
EPS = 1e-5
NZ = 8


def _names(args):
    s = set()
    for ap in args:
        if ap is None or isinstance(ap, (int, float)):
            continue
        t = getattr(ap, "tensor", None)
        n = getattr(t, "name", None)
        if n is not None:
            s.add(n)
    return s


class Sem:
    def __init__(self, nc, stack, name):
        self.h = stack.enter_context(nc.semaphore(name))
        self.n = 0

    def bump(self, instr, k=1):
        instr.then_inc(self.h, k)
        self.n += k
        return self.n


class Eng:

    def __init__(self, eng):
        self.e = eng
        self.dirty = set()

    def wait(self, sem, n=None):
        self.e.wait_ge(sem.h, sem.n if n is None else n)

    def drain(self):
        self.e.drain()
        self.dirty = set()

    def inc(self, sem, k=1):
        self.e.sem_inc(sem.h, k)
        sem.n += k

    def fence(self, sem):
        self.drain()
        self.inc(sem)

    def _pre(self, outs, ins):
        if (_names(outs) | _names(ins)) & self.dirty:
            self.e.drain()
            self.dirty = set()
        self.dirty |= _names(outs)

    def copy(self, out, in_):
        self._pre([out], [in_])
        return self.e.tensor_copy(out, in_)

    def memset(self, out, v):
        self._pre([out], [])
        return self.e.memset(out, v)

    def tt(self, out, a, b, op):
        self._pre([out], [a, b])
        return self.e.tensor_tensor(out, a, b, op)

    def ts(self, out, a, s1, op0, s2=None, op1=ALU.bypass, accum_out=None):
        self._pre([out, accum_out], [a, s1, s2])
        return self.e.tensor_scalar(out, a, s1, s2, op0, op1, accum_out)

    def stt(self, out, in0, s, in1, op0, op1):
        self._pre([out], [in0, s, in1])
        return self.e.scalar_tensor_tensor(out, in0, s, in1, op0, op1)

    def scan(self, out, d0, d1, init):
        self._pre([out], [d0, d1, init])
        return self.e.tensor_tensor_scan(out, d0, d1, init, ALU.mult, ALU.add)

    def red(self, out, in_, op):
        self._pre([out], [in_])
        return self.e.tensor_reduce(out, in_, AX.X, op)

    def act(self, out, in_, func, bias=0.0, scale=1.0, accum_out=None):
        self._pre([out, accum_out], [in_, bias, scale])
        return self.e.activation(out, in_, func, bias=bias, scale=scale,
                                 accum_out=accum_out)

    def recip(self, out, in_):
        self._pre([out], [in_])
        return self.e.reciprocal(out, in_)


class PEW:
    def __init__(self, eng):
        self.e = eng

    def wait(self, sem, n=None):
        self.e.wait_ge(sem.h, sem.n if n is None else n)

    def inc(self, sem, k=1):
        self.e.sem_inc(sem.h, k)
        sem.n += k

    def mm(self, out, lhsT, rhs, start, stop):
        return self.e.matmul(out, lhsT, rhs, start=start, stop=stop)

    def tr(self, out, in_, idn):
        return self.e.transpose(out, in_, idn)


class DMAQ:
    def __init__(self, eng):
        self.e = eng

    def wait(self, sem, n=None):
        self.e.wait_ge(sem.h, sem.n if n is None else n)

    def inc(self, sem, k=1):
        self.e.sem_inc(sem.h, k)
        sem.n += k

    def go(self, out, in_, sem=None, **kw):
        i = self.e.dma_start(out=out, in_=in_, **kw)
        if sem is not None:
            sem.bump(i, 16)
        return i


def build(phases=("A1", "A2a", "A2b", "R", "B", "C"), debug=False):
    nc = bass.Bass("TRN2", target_bir_lowering=False)

    X = nc.dram_tensor("x", [NB, L, D], F32, kind="ExternalInput").ap()
    OUT = nc.dram_tensor("out", [NB, L, D], F32, kind="ExternalOutput").ap()
    IDN = nc.dram_tensor("identity", [128, 128], F32, kind="ExternalInput").ap()

    wn = {}
    for nm, shp in [
        ("ln_att_g", [D]), ("ln_att_b", [D]), ("ln_ffn_g", [D]), ("ln_ffn_b", [D]),
        ("att_time_decay", [D]), ("att_time_first", [D]),
        ("att_mix_k", [D]), ("att_mix_v", [D]), ("att_mix_r", [D]),
        ("att_wk", [D, D]), ("att_wv", [D, D]), ("att_wr", [D, D]), ("att_wo", [D, D]),
        ("ffn_mix_k", [D]), ("ffn_mix_r", [D]),
        ("ffn_wk", [D, DFF]), ("ffn_wv", [DFF, D]), ("ffn_wr", [D, D]),
    ]:
        wn[nm] = nc.dram_tensor("L0_attn_" + nm, shp, F32, kind="ExternalInput").ap()
    for nm, shp in [("W1", [D, 256]), ("b1", [256]), ("W2", [256, NZ]), ("b2", [NZ]),
                    ("Wa", [NZ, D, DFF]), ("Wb", [NZ, DFF, D])]:
        wn[nm] = nc.dram_tensor("L0_ffn_" + nm, shp, F32, kind="ExternalInput").ap()

    skind = "ExternalOutput" if debug else "Internal"
    S_XPP = nc.dram_tensor("s_xpp", [NB, L, D], F32, kind=skind).ap()
    S_Y = nc.dram_tensor("s_y", [NB, L, D], F32, kind=skind).ap()
    S_G32 = nc.dram_tensor("s_g32", [NB, L, D], F32, kind=skind).ap()
    S_GBF = nc.dram_tensor("s_gbf", [D, NB * L], BF16, kind=skind).ap()
    S_KF = nc.dram_tensor("s_kf", [DFF, NB * L], BF16, kind=skind).ap()
    S_SR2 = nc.dram_tensor("s_sr2", [D, NB * L], F32, kind=skind).ap()
    S_ZAC = nc.dram_tensor("s_zac", [D, NB * L], F32, kind=skind).ap()

    gbf_r = S_GBF.rearrange("(k p) t -> p k t", p=128)
    kf_r = S_KF.rearrange("(j p) t -> p j t", p=128)
    sr2_r = S_SR2.rearrange("(k p) t -> p k t", p=128)
    zac_r = S_ZAC.rearrange("(k p) t -> p k t", p=128)

    st = contextlib.ExitStack()
    with st:
        E = st.enter_context

        V = Eng(nc.vector)
        A = Eng(nc.scalar)
        G = Eng(nc.gpsimd)
        P = PEW(nc.tensor)
        SY = DMAQ(nc.sync)
        GD = DMAQ(nc.gpsimd)

        semnames = ["ld", "sx", "ssq", "srd", "sst", "snr", "str_", "saf", "smx",
                    "smm", "sek", "sdv", "ssc", "swo", "sad", "sou", "sbw", "sbg",
                    "sbz", "sbm1", "sbm2", "sbev", "sbac", "sfin", "sbar", "srt"]
        S = {n: Sem(nc, st, n) for n in semnames}

        dma_sems = [S[k] for k in ("ld", "sx", "sou", "sbw", "sbg", "sbac",
                                   "srt", "sfin")]

        def barrier():
            for dsem in dma_sems:
                SY.wait(dsem)
            for e in (V, A, G):
                e.fence(S["sbar"])
            P.e.drain()
            P.inc(S["sbar"])
            SY.inc(S["sbar"])
            tgt = S["sbar"].n
            for e in (V, A, G, P, SY):
                e.wait(S["sbar"], tgt)

        FREE = {}

        def consumed(key, sem):
            FREE[key] = (sem, sem.n)

        def wait_free(eng, key):
            if key in FREE:
                sem, n = FREE[key]
                eng.wait(sem, n)

        BANK = {}

        def bank_wait(i):
            if i in BANK:
                sem, n = BANK[i]
                P.wait(sem, n)

        def bank_set(i, sem):
            BANK[i] = (sem, sem.n)

        idn = E(nc.sbuf_tensor("idn", [128, 128], F32))
        pnames = ["lag", "lab", "lfg", "lfb", "mk", "mv", "mr",
                  "fmk", "fmr", "ew", "eu"]
        PB = {n: E(nc.sbuf_tensor("pp_" + n, [128, KT], F32)) for n in pnames}
        carN = E(nc.sbuf_tensor("carN", [128, NB, KT], F32))
        carQ = E(nc.sbuf_tensor("carQ", [128, NB, KT], F32))
        pacc = E(nc.sbuf_tensor("pacc", [128, KT, 16], F32))
        RNAMES = ["ssum", "ssum2", "snmu", "sex2", "snv", "sL", "srstd",
                  "sMn", "sT", "sU", "sR2", "sR2b", "sM2n", "sIvr"]
        RS = {n: E(nc.sbuf_tensor("r_" + n, [128, NTA, 2], F32)) for n in RNAMES}
        sqA = E(nc.sbuf_tensor("sqA", [128, D], F32))
        wzbc = E(nc.sbuf_tensor("wzbc", [128, 16], F32))
        onesr = E(nc.sbuf_tensor("onesr", [1, 128], F32))
        cst = E(nc.sbuf_tensor("cst", [128, 4], F32))
        I32 = mybir.dt.int32
        iota8f = E(nc.sbuf_tensor("iota8f", [2, 8], F32))
        iota8i = E(nc.sbuf_tensor("iota8i", [2, 8], I32))
        jjoff_i = E(nc.sbuf_tensor("jjoff_i", [128, 32], I32))
        jjoff_f = E(nc.sbuf_tensor("jjoff_f", [128, 32], F32))
        idxAi = E(nc.sbuf_tensor("idxAi", [128, 48], I32))
        idxBi = E(nc.sbuf_tensor("idxBi", [128, 192], I32))
        idxTf = E(nc.sbuf_tensor("idxTf", [128, 32], F32))
        rsF = E(nc.sbuf_tensor("rsF", [128, 1], F32))
        nc.const_aps.register(F32, EPS, cst[:, 0:1])
        nc.const_aps.register(F32, 1.0, cst[:, 1:2])
        nc.const_aps.register(F32, 1.0 + EPS, cst[:, 2:3])
        nc.const_aps.register(F32, -EPS, cst[:, 3:4])
        psum = [E(nc.psum_tensor(f"pb{i}", [128, 512], F32)) for i in range(8)]

        SY.go(idn[:], IDN, sem=S["ld"])
        for nm, src in [("lag", "ln_att_g"), ("lab", "ln_att_b"),
                        ("lfg", "ln_ffn_g"), ("lfb", "ln_ffn_b"),
                        ("mk", "att_mix_k"), ("mv", "att_mix_v"),
                        ("mr", "att_mix_r"), ("fmk", "ffn_mix_k"),
                        ("fmr", "ffn_mix_r"), ("ew", "att_time_decay"),
                        ("eu", "att_time_first")]:
            SY.go(PB[nm][:], wn[src].rearrange("(k p) -> p k", p=128),
                  sem=S["ld"], allow_slow_non_contiguous=True)
        for e in (V, A, G):
            e.wait(S["ld"])
        V.memset(onesr[:], 1.0)
        A.act(PB["ew"][:], PB["ew"][:], AF.Exp)
        A.act(PB["ew"][:], PB["ew"][:], AF.Exp, scale=-1.0)
        A.act(PB["eu"][:], PB["eu"][:], AF.Exp)
        V.fence(S["sst"])
        A.fence(S["sst"])
        G.memset(carN[:], 0.0)
        G.memset(carQ[:], 0.0)
        G.memset(pacc[:], 0.0)
        G.memset(cst[:, 0:1], EPS)
        G.memset(cst[:, 1:2], 1.0)
        G.memset(cst[:, 2:3], 1.0 + EPS)
        G.memset(cst[:, 3:4], -EPS)
        G.e.iota(jjoff_i[:], [[128, 32]], base=0, channel_multiplier=1)
        G.e.iota(iota8i[:], [[1, 8]], base=0, channel_multiplier=0)
        G.drain()
        G.copy(jjoff_f[:], jjoff_i[:])
        G.copy(iota8f[:], iota8i[:])
        G.fence(S["sst"])

        n_out_dma = [0]

        def rsl(nm, n_tiles, par):
            return RS[nm][:, 0:n_tiles, par:par + 1]

        def ln_sums(src, n_tiles, par):
            wait_free(A, ("stats", par))
            wait_free(G, ("stats", par))
            for n in range(n_tiles):
                A.e.activation(sqA[:], src[:, n, :], AF.Square,
                               accum_out=RS["ssum2"][:, n, par:par + 1])
                A.e.activation(sqA[:], src[:, n, :], AF.Copy,
                               accum_out=RS["ssum"][:, n, par:par + 1])
            A.fence(S["srd"])
            G.wait(S["srd"])
            c = 1.0 / D
            G.ts(rsl("snmu", n_tiles, par), rsl("ssum", n_tiles, par),
                 -c, ALU.mult)
            G.ts(rsl("sex2", n_tiles, par), rsl("ssum2", n_tiles, par),
                 c, ALU.mult)
            G.tt(rsl("snv", n_tiles, par), rsl("snmu", n_tiles, par),
                 rsl("snmu", n_tiles, par), ALU.mult)
            G.tt(rsl("snv", n_tiles, par), rsl("snv", n_tiles, par),
                 rsl("sex2", n_tiles, par), ALU.subtract)
            G.fence(S["sst"])
            A.wait(S["sst"])
            A.act(rsl("sL", n_tiles, par), rsl("snv", n_tiles, par),
                  AF.Ln, scale=-1.0, bias=EPS)

        def ln_stats(src, n_tiles, par):
            ln_sums(src, n_tiles, par)
            A.act(rsl("srstd", n_tiles, par), rsl("sL", n_tiles, par),
                  AF.Exp, scale=-0.5)
            A.fence(S["sst"])
            G.wait(S["sst"])
            G.tt(rsl("sMn", n_tiles, par), rsl("snmu", n_tiles, par),
                 rsl("srstd", n_tiles, par), ALU.mult)
            G.fence(S["srd"])

        def ln_stats_fused(src, n_tiles, par):
            ln_sums(src, n_tiles, par)
            A.act(rsl("srstd", n_tiles, par), rsl("sL", n_tiles, par),
                  AF.Exp, scale=-0.5)
            A.act(rsl("sT", n_tiles, par), rsl("sL", n_tiles, par),
                  AF.Exp, scale=-1.0)
            A.act(rsl("sU", n_tiles, par), rsl("sT", n_tiles, par),
                  AF.Ln, scale=-EPS, bias=1.0 + EPS)
            A.act(rsl("sIvr", n_tiles, par), rsl("sU", n_tiles, par),
                  AF.Exp, scale=0.5)
            A.act(rsl("sR2b", n_tiles, par), rsl("sU", n_tiles, par),
                  AF.Exp, scale=-0.5)
            A.fence(S["sst"])
            G.wait(S["sst"])
            G.tt(rsl("sR2", n_tiles, par), rsl("srstd", n_tiles, par),
                 rsl("sR2b", n_tiles, par), ALU.mult)
            G.tt(rsl("sM2n", n_tiles, par), rsl("snmu", n_tiles, par),
                 rsl("sR2", n_tiles, par), ALU.mult)
            G.fence(S["srd"])

        def ln_norm(dst, src, n_tiles, par, on=None, scale_nm="srstd",
                    bias_nm="sMn"):
            e = V if on is None else on
            e.wait(S["srd"])
            for n in range(n_tiles):
                if e is A:
                    e.act(dst[:, n, :], src[:, n, :], AF.Identity,
                          scale=RS[scale_nm][:, n, par:par + 1],
                          bias=RS[bias_nm][:, n, par:par + 1])
                else:
                    e.ts(dst[:, n, :], src[:, n, :],
                         RS[scale_nm][:, n, par:par + 1], ALU.mult,
                         RS[bias_nm][:, n, par:par + 1], ALU.add)
            e.fence(S["snr"])
            consumed(("stats", par), S["snr"])

        def transpose_affine(src_tok, dst_ch, par_key, gcol, bcol, accum_slot=None):
            off = 1 if dst_ch.shape[2] == CHA + 1 else 0
            for kk in range(KT):
                bank = 4 + (kk % 2)
                bank_wait(bank)
                for n in range(NTA):
                    ii = P.tr(psum[bank][:, n * 128:(n + 1) * 128],
                              src_tok[:, n, kk * 128:(kk + 1) * 128], idn[:])
                S["str_"].bump(ii, 1)
                A.wait(S["str_"])
                kwargs = {}
                if accum_slot is not None:
                    kwargs["accum_out"] = accum_slot(kk)
                if gcol is not None:
                    ia = A.act(dst_ch[:, kk, off:off + CHA], psum[bank][:, 0:CHA],
                               AF.Identity, scale=gcol[:, kk:kk + 1],
                               bias=bcol[:, kk:kk + 1], **kwargs)
                else:
                    ia = A.act(dst_ch[:, kk, off:off + CHA], psum[bank][:, 0:CHA],
                               AF.Copy, **kwargs)
                S["saf"].bump(ia, 1)
                bank_set(bank, S["saf"])

        def transpose_add(src_ch, dst_tok, n_tiles):
            for n in range(n_tiles):
                bank_wait(4)
                bank_wait(5)
                for kk in range(KT):
                    bank = 4 + kk // 4
                    ii = P.tr(psum[bank][:, (kk % 4) * 128:(kk % 4 + 1) * 128],
                              src_ch[:, kk, n * 128:(n + 1) * 128], idn[:])
                    if kk == 7:
                        S["str_"].bump(ii, 1)
                V.wait(S["str_"])
                V.tt(dst_tok[:, n, 0:512], dst_tok[:, n, 0:512],
                     psum[4][:, 0:512], ALU.add)
                V.tt(dst_tok[:, n, 512:1024], dst_tok[:, n, 512:1024],
                     psum[5][:, 0:512], ALU.add)
                V.fence(S["sad"])
                bank_set(4, S["sad"])
                bank_set(5, S["sad"])

        barrier()

        if "A1" in phases:
            sc_A1 = st.enter_context(nc.named_scope("PH_A1"))
            with contextlib.ExitStack() as sa:
                EA = sa.enter_context
                wkb = EA(nc.sbuf_tensor("wkb", [128, KT, D], BF16))
                wvb = EA(nc.sbuf_tensor("wvb", [128, KT, D], BF16))
                wrb = EA(nc.sbuf_tensor("wrb", [128, KT, D], BF16))
                wob = EA(nc.sbuf_tensor("wob", [128, KT, D], BF16))
                ewb = EA(nc.sbuf_tensor("ewb", [128, KT, CHA], F32))
                ones2 = EA(nc.sbuf_tensor("ones2", [128, CHA], F32))
                xa = [EA(nc.sbuf_tensor(f"xa{i}", [128, NTA, D], F32)) for i in range(2)]
                hch = [EA(nc.sbuf_tensor(f"hch{i}", [128, KT, CHA + 1], F32))
                       for i in range(2)]
                dbf = EA(nc.sbuf_tensor("dbf", [128, KT, CHA], F32))
                mixb = {m: [EA(nc.sbuf_tensor(f"mx{m}{i}", [128, KT, CHA], BF16))
                            for i in range(2)] for m in "kvr"}
                eKb = [EA(nc.sbuf_tensor(f"eK{i}", [128, CHA], F32)) for i in range(3)]
                erb = [EA(nc.sbuf_tensor(f"er{i}", [128, CHA], F32)) for i in range(3)]
                vvb = [EA(nc.sbuf_tensor(f"vv{i}", [128, CHA], F32)) for i in range(3)]
                ekvb = [EA(nc.sbuf_tensor(f"ekv{i}", [128, CHA], F32)) for i in range(3)]
                NNb = [EA(nc.sbuf_tensor(f"NN{i}", [128, CHA + 1], F32)) for i in range(3)]
                QQb = [EA(nc.sbuf_tensor(f"QQ{i}", [128, CHA + 1], F32)) for i in range(3)]
                rwk = [EA(nc.sbuf_tensor(f"rwk{i}", [128, KT, CHA], BF16))
                       for i in range(2)]
                attc = EA(nc.sbuf_tensor("attc", [128, KT, CHA], F32))

                for buf, src in [(wkb, "att_wk"), (wvb, "att_wv"),
                                 (wrb, "att_wr"), (wob, "att_wo")]:
                    GD.go(buf[:], wn[src].rearrange("(k p) c -> p k c", p=128),
                          sem=S["sbw"])
                V.memset(ones2[:], 1.0)
                V.fence(S["sst"])
                A.wait(S["sst"])
                for kk in range(KT):
                    A.act(ewb[:, kk, :], ones2[:], AF.Copy,
                          scale=PB["ew"][:, kk:kk + 1])
                A.fence(S["sst"])
                for e in (V, A, G, P):
                    e.wait(S["sbw"])

                chunks = [(b, ci) for ci in range(NCA) for b in range(NB)]
                load_n = {}
                mix_n = {}
                snr_n = {}
                rwk_n = {}

                def a1_load(idx):
                    b, ci = chunks[idx]
                    par = b
                    t0 = ci * CHA
                    wait_free(SY, ("xa", par))
                    SY.go(xa[par][:], X[b, t0:t0 + CHA, :]
                          .rearrange("(n p) c -> p n c", p=128), sem=S["sx"])
                    load_n[idx] = S["sx"].n

                def a1_front(idx):
                    b, ci = chunks[idx]
                    par = b
                    A.wait(S["sx"], load_n[idx])
                    G.wait(S["sx"], load_n[idx])
                    ln_stats_fused(xa[par], NTA, par)
                    eng = V if par == 0 else A
                    eng.wait(S["srd"])
                    for n in range(NTA):
                        if eng is A:
                            A.act(xa[par][:, n, :], xa[par][:, n, :],
                                  AF.Identity,
                                  scale=RS["sR2"][:, n, par:par + 1],
                                  bias=RS["sM2n"][:, n, par:par + 1])
                        else:
                            V.ts(xa[par][:, n, :], xa[par][:, n, :],
                                 RS["sR2"][:, n, par:par + 1], ALU.mult,
                                 RS["sM2n"][:, n, par:par + 1], ALU.add)
                    eng.fence(S["snr"])
                    snr_n[idx] = S["snr"].n
                    wait_free(G, ("hchcol", par))
                    if ci == 0:
                        G.memset(hch[par][:, :, 0:1], 0.0)
                    else:
                        G.copy(hch[par][:, :, 0:1],
                               hch[par][:, :, CHA:CHA + 1])
                    G.fence(S["ssq"])
                    P.wait(S["snr"], snr_n[idx])
                    P.wait(S["ssq"])
                    wait_free(A, ("hch", par))
                    transpose_affine(xa[par], hch[par], ("hch", par),
                                     PB["lag"], PB["lab"])
                    A.fence(S["saf"])
                    h1 = hch[par][:, :, 1:CHA + 1]
                    h0 = hch[par][:, :, 0:CHA]
                    V.wait(S["saf"])
                    wait_free(V, ("dbf",))
                    V.tt(dbf[:], h1, h0, ALU.subtract)
                    for m, mixp in [("k", "mk"), ("v", "mv"), ("r", "mr")]:
                        wait_free(V, ("mix" + m, par))
                        for kk in range(KT):
                            V.stt(mixb[m][par][:, kk, :], dbf[:, kk, :],
                                  PB[mixp][:, kk:kk + 1], h0[:, kk, :],
                                  ALU.mult, ALU.add)
                    V.fence(S["smx"])
                    mix_n[idx] = S["smx"].n
                    consumed(("dbf",), S["smx"])
                    consumed(("hchcol", par), S["smx"])
                    consumed(("hch", par), S["smx"])

                def a1_back_kvr(idx):
                    b, ci = chunks[idx]
                    par = b
                    if ci == 0:
                        V.memset(carN[:, b, :], 0.0)
                        V.memset(carQ[:, b, :], 0.0)
                    P.wait(S["smx"], mix_n[idx])
                    ev_n = {}
                    v1_n = {}
                    g2_n = {}
                    a3_n = {}

                    def a1_mm(j):
                        for gi, m in enumerate(("k", "v", "r")):
                            bank = (j * 3 + gi) % 6
                            bank = bank if bank < 4 else bank + 2
                            bank_wait(bank)
                            wsb = {"k": wkb, "v": wvb, "r": wrb}[m]
                            for kk in range(KT):
                                ii = P.mm(psum[bank][:, 0:CHA],
                                          wsb[:, kk, j * 128:(j + 1) * 128],
                                          mixb[m][par][:, kk, :],
                                          start=(kk == 0), stop=(kk == KT - 1))
                            S["smm"].bump(ii, 1)

                    def a1_ev(j):
                        jp = j % 3
                        A.wait(S["smm"])
                        wait_free(A, ("wkv", jp))
                        bks = []
                        for gi in range(3):
                            bank = (j * 3 + gi) % 6
                            bks.append(bank if bank < 4 else bank + 2)
                        A.act(eKb[jp][:], psum[bks[0]][:, 0:CHA], AF.Exp)
                        A.act(vvb[jp][:], psum[bks[1]][:, 0:CHA], AF.Copy)
                        ia = A.act(erb[jp][:], psum[bks[2]][:, 0:CHA],
                                   AF.Exp, scale=-1.0)
                        S["sek"].bump(ia, 1)
                        ev_n[j] = S["sek"].n
                        for bank in bks:
                            bank_set(bank, S["sek"])

                    def a1_g1(j):
                        jp = j % 3
                        G.wait(S["sek"], ev_n[j])
                        ig = G.tt(ekvb[jp][:], eKb[jp][:], vvb[jp][:], ALU.mult)
                        S["ssc"].bump(ig, 1)

                    def a1_v1(j):
                        jp = j % 3
                        V.wait(S["ssc"])
                        V.copy(NNb[jp][:, 0:1], carN[:, b, j:j + 1])
                        V.copy(QQb[jp][:, 0:1], carQ[:, b, j:j + 1])
                        V.scan(NNb[jp][:, 1:CHA + 1], ewb[:, j, :],
                               ekvb[jp][:], carN[:, b, j:j + 1])
                        V.scan(QQb[jp][:, 1:CHA + 1], ewb[:, j, :],
                               eKb[jp][:], carQ[:, b, j:j + 1])
                        V.copy(carN[:, b, j:j + 1], NNb[jp][:, CHA:CHA + 1])
                        V.copy(carQ[:, b, j:j + 1], QQb[jp][:, CHA:CHA + 1])
                        V.stt(ekvb[jp][:], ekvb[jp][:], PB["eu"][:, j:j + 1],
                              NNb[jp][:, 0:CHA], ALU.mult, ALU.add)
                        iv = V.stt(eKb[jp][:], eKb[jp][:],
                                   PB["eu"][:, j:j + 1],
                                   QQb[jp][:, 0:CHA], ALU.mult, ALU.add)
                        S["sdv"].bump(iv, 1)
                        v1_n[j] = S["sdv"].n

                    def a1_gd(j):
                        jp = j % 3
                        G.wait(S["sdv"], v1_n[j])
                        G.ts(erb[jp][:], erb[jp][:], 1.0, ALU.add)
                        ig = G.tt(eKb[jp][:], eKb[jp][:], erb[jp][:], ALU.mult)
                        S["ssc"].bump(ig, 1)

                    def a1_ar(j):
                        jp = j % 3
                        A.wait(S["ssc"])
                        A.act(eKb[jp][:], eKb[jp][:], AF.Ln)
                        ia = A.act(eKb[jp][:], eKb[jp][:], AF.Exp, scale=-1.0)
                        S["sek"].bump(ia, 1)
                        a3_n[j] = S["sek"].n

                    def a1_v2(j):
                        jp = j % 3
                        V.wait(S["sek"], a3_n[j])
                        if j == 0:
                            wait_free(V, ("rwk", par))
                        iv = V.tt(rwk[par][:, j, :], ekvb[jp][:], eKb[jp][:],
                                  ALU.mult)
                        S["sdv"].bump(iv, 1)
                        consumed(("wkv", jp), S["sdv"])

                    for j in range(KT):
                        a1_mm(j)
                        a1_ev(j)
                        a1_g1(j)
                        if j >= 1:
                            a1_gd(j - 1)
                        a1_v1(j)
                        if j >= 1:
                            a1_ar(j - 1)
                        if j >= 2:
                            a1_v2(j - 2)
                    a1_gd(KT - 1)
                    a1_ar(KT - 1)
                    a1_v2(KT - 2)
                    a1_v2(KT - 1)
                    rwk_n[idx] = S["sdv"].n
                    consumed(("mixk", par), S["smm"])
                    consumed(("mixv", par), S["smm"])
                    consumed(("mixr", par), S["smm"])

                def a1_back_wo(idx):
                    b, ci = chunks[idx]
                    par = b
                    t0 = ci * CHA
                    P.wait(S["sdv"], rwk_n[idx])
                    wait_free(A, ("attc",))
                    for co in range(KT):
                        bank = co % 2
                        bank_wait(bank)
                        for kk in range(KT):
                            ii = P.mm(psum[bank][:, 0:CHA],
                                      wob[:, kk, co * 128:(co + 1) * 128],
                                      rwk[par][:, kk, :],
                                      start=(kk == 0), stop=(kk == KT - 1))
                        S["swo"].bump(ii, 1)
                        A.wait(S["swo"])
                        ia = A.act(attc[:, co, :], psum[bank][:, 0:CHA], AF.Copy)
                        S["sek"].bump(ia, 1)
                        bank_set(bank, S["sek"])
                    consumed(("rwk", par), S["swo"])
                    P.wait(S["sek"])
                    V.wait(S["snr"], snr_n[idx])
                    for n in range(NTA):
                        bank_wait(4)
                        bank_wait(5)
                        for kk in range(KT):
                            bank = 4 + kk // 4
                            ii = P.tr(psum[bank][:, (kk % 4) * 128:(kk % 4 + 1) * 128],
                                      attc[:, kk, n * 128:(n + 1) * 128], idn[:])
                            if kk == 7:
                                S["str_"].bump(ii, 1)
                        V.wait(S["str_"])
                        V.stt(xa[par][:, n, 0:512], xa[par][:, n, 0:512],
                              RS["sIvr"][:, n, par:par + 1],
                              psum[4][:, 0:512], ALU.mult, ALU.add)
                        V.stt(xa[par][:, n, 512:1024], xa[par][:, n, 512:1024],
                              RS["sIvr"][:, n, par:par + 1],
                              psum[5][:, 0:512], ALU.mult, ALU.add)
                        V.fence(S["sad"])
                        bank_set(4, S["sad"])
                        bank_set(5, S["sad"])
                    consumed(("attc",), S["str_"])
                    consumed(("stats", par), S["sad"])
                    SY.wait(S["sad"])
                    SY.go(S_XPP[b, t0:t0 + CHA, :]
                          .rearrange("(n p) c -> p n c", p=128),
                          xa[par][:], sem=S["sou"])
                    consumed(("xa", par), S["sou"])

                a1_load(0)
                a1_load(1)
                a1_front(0)
                a1_back_kvr(0)
                for idx in range(1, len(chunks)):
                    a1_front(idx)
                    a1_back_wo(idx - 1)
                    if idx + 1 < len(chunks):
                        a1_load(idx + 1)
                    a1_back_kvr(idx)
                a1_back_wo(len(chunks) - 1)
            barrier()

        if "A2a" in phases:
            sc_A2a = st.enter_context(nc.named_scope("PH_A2a"))
            with contextlib.ExitStack() as sa:
                EA = sa.enter_context
                fwkb = EA(nc.sbuf_tensor("fwkb", [128, KT, DFF], BF16))
                fwrb = EA(nc.sbuf_tensor("fwrb", [128, KT, D], BF16))
                xa2 = [EA(nc.sbuf_tensor(f"xa2_{i}", [128, NTA, D], F32))
                       for i in range(2)]
                h2pre = EA(nc.sbuf_tensor("h2pre", [128, NTA, D], F32))
                h2ch = [EA(nc.sbuf_tensor(f"h2ch{i}", [128, KT, CHA + 1], F32))
                        for i in range(2)]
                dbf2 = EA(nc.sbuf_tensor("dbf2", [128, KT, CHA], BF16))
                mfk = [EA(nc.sbuf_tensor(f"mfk{i}", [128, KT, CHA], BF16))
                       for i in range(2)]
                mfr = [EA(nc.sbuf_tensor(f"mfr{i}", [128, KT, CHA], BF16))
                       for i in range(2)]
                kfb = [EA(nc.sbuf_tensor(f"kfb{i}", [128, JT, CHA], BF16))
                       for i in range(2)]
                trel = EA(nc.sbuf_tensor("trel", [128, 4, CHA], F32))
                sr2b = [EA(nc.sbuf_tensor(f"sr2b{i}", [128, KT, CHA], F32))
                        for i in range(2)]

                GD.go(fwkb[:], wn["ffn_wk"].rearrange("(k p) c -> p k c", p=128),
                      sem=S["sbw"])
                GD.go(fwrb[:], wn["ffn_wr"].rearrange("(k p) c -> p k c", p=128),
                      sem=S["sbw"])
                for e in (V, A, G, P):
                    e.wait(S["sbw"])

                chunks = [(b, ci) for ci in range(NCA) for b in range(NB)]
                load_n = {}
                mix_n = {}

                def a2a_load(idx):
                    b, ci = chunks[idx]
                    par = b
                    t0 = ci * CHA
                    wait_free(SY, ("xa2", par))
                    SY.go(xa2[par][:], S_XPP[b, t0:t0 + CHA, :]
                          .rearrange("(n p) c -> p n c", p=128), sem=S["sx"])
                    load_n[idx] = S["sx"].n

                def a2a_front(idx):
                    b, ci = chunks[idx]
                    par = b
                    A.wait(S["sx"], load_n[idx])
                    G.wait(S["sx"], load_n[idx])
                    ln_stats(xa2[par], NTA, par)
                    eng = V if par == 0 else A
                    wait_free(eng, ("h2pre",))
                    ln_norm(h2pre, xa2[par], NTA, par, on=eng)
                    consumed(("xa2", par), S["snr"])
                    wait_free(G, ("h2col", par))
                    if ci == 0:
                        G.memset(h2ch[par][:, :, 0:1], 0.0)
                    else:
                        G.copy(h2ch[par][:, :, 0:1],
                               h2ch[par][:, :, CHA:CHA + 1])
                    G.fence(S["ssq"])
                    P.wait(S["snr"])
                    P.wait(S["ssq"])
                    wait_free(A, ("h2ch", par))
                    transpose_affine(h2pre, h2ch[par], ("h2ch", par),
                                     PB["lfg"], PB["lfb"])
                    A.fence(S["saf"])
                    consumed(("h2pre",), S["saf"])
                    h1 = h2ch[par][:, :, 1:CHA + 1]
                    h0 = h2ch[par][:, :, 0:CHA]
                    V.wait(S["saf"])
                    wait_free(V, ("dbf2",))
                    V.tt(dbf2[:], h1, h0, ALU.subtract)
                    wait_free(V, ("mfk", par))
                    for kk in range(KT):
                        V.stt(mfk[par][:, kk, :], dbf2[:, kk, :],
                              PB["fmk"][:, kk:kk + 1], h0[:, kk, :],
                              ALU.mult, ALU.add)
                    wait_free(V, ("mfr", par))
                    for kk in range(KT):
                        V.stt(mfr[par][:, kk, :], dbf2[:, kk, :],
                              PB["fmr"][:, kk:kk + 1], h0[:, kk, :],
                              ALU.mult, ALU.add)
                    V.fence(S["smx"])
                    mix_n[idx] = S["smx"].n
                    consumed(("dbf2",), S["smx"])
                    consumed(("h2col", par), S["smx"])
                    consumed(("h2ch", par), S["smx"])

                def a2a_back_kf(idx):
                    b, ci = chunks[idx]
                    par = b
                    P.wait(S["smx"], mix_n[idx])
                    wait_free(A, ("kfb", par))
                    wait_free(V, ("kfb", par))
                    for j in range(JT):
                        bank = j % 2
                        bank_wait(bank)
                        for kk in range(KT):
                            ii = P.mm(psum[bank][:, 0:CHA],
                                      fwkb[:, kk, j * 128:(j + 1) * 128],
                                      mfk[par][:, kk, :],
                                      start=(kk == 0), stop=(kk == KT - 1))
                        S["smm"].bump(ii, 1)
                        A.wait(S["smm"])
                        if j % 4 == 0:
                            A.wait(S["sdv"])
                        ia = A.act(trel[:, j % 4, :], psum[bank][:, 0:CHA],
                                   AF.Relu)
                        S["sek"].bump(ia, 1)
                        bank_set(bank, S["sek"])
                        V.wait(S["sek"])
                        iv = V.tt(kfb[par][:, j, :], trel[:, j % 4, :],
                                  trel[:, j % 4, :], ALU.mult)
                        if j % 4 == 3 or j == JT - 1:
                            S["sdv"].bump(iv, 1)
                    consumed(("mfk", par), S["smm"])

                def a2a_back_fr(idx):
                    b, ci = chunks[idx]
                    par = b
                    t0 = ci * CHA
                    tg = b * L + t0
                    wait_free(A, ("sr2b", par))
                    for co in range(KT):
                        bank = 2 + co % 2
                        bank_wait(bank)
                        for kk in range(KT):
                            ii = P.mm(psum[bank][:, 0:CHA],
                                      fwrb[:, kk, co * 128:(co + 1) * 128],
                                      mfr[par][:, kk, :],
                                      start=(kk == 0), stop=(kk == KT - 1))
                        S["smm"].bump(ii, 1)
                        A.wait(S["smm"])
                        ia = A.act(sr2b[par][:, co, :], psum[bank][:, 0:CHA],
                                   AF.Exp, scale=-1.0)
                        S["sek"].bump(ia, 1)
                        bank_set(bank, S["sek"])
                    consumed(("mfr", par), S["smm"])
                    A.act(sr2b[par][:], sr2b[par][:], AF.Ln, bias=1.0)
                    A.act(sr2b[par][:], sr2b[par][:], AF.Exp, scale=-1.0)
                    A.fence(S["sek"])
                    SY.wait(S["sek"])
                    SY.wait(S["sdv"])
                    SY.go(kf_r[:, :, tg:tg + CHA], kfb[par][:], sem=S["sou"])
                    SY.go(sr2_r[:, :, tg:tg + CHA], sr2b[par][:], sem=S["sou"])
                    consumed(("kfb", par), S["sou"])
                    consumed(("sr2b", par), S["sou"])

                a2a_load(0)
                a2a_load(1)
                a2a_front(0)
                for idx in range(len(chunks)):
                    a2a_back_kf(idx)
                    if idx + 1 < len(chunks):
                        a2a_front(idx + 1)
                    a2a_back_fr(idx)
                    if idx + 2 < len(chunks):
                        a2a_load(idx + 2)
            barrier()

        if "A2b" in phases:
            sc_A2b = st.enter_context(nc.named_scope("PH_A2b"))
            with contextlib.ExitStack() as sa:
                EA = sa.enter_context
                fwvb = EA(nc.sbuf_tensor("fwvb", [128, JT, D], BF16))
                kfw = [EA(nc.sbuf_tensor(f"kfw{i}", [128, JT, CHA], BF16))
                       for i in range(2)]
                sr2w = [EA(nc.sbuf_tensor(f"sr2w{i}", [128, KT, CHA], F32))
                        for i in range(2)]
                xa3 = [EA(nc.sbuf_tensor(f"xa3_{i}", [128, NTA, D], F32))
                       for i in range(2)]
                xr = [EA(nc.sbuf_tensor(f"xr{i}", [128, NTA, D], F32))
                      for i in range(2)]
                ffc = [EA(nc.sbuf_tensor(f"ffc{i}", [128, KT, CHA], F32))
                       for i in range(2)]
                lnb = [EA(nc.sbuf_tensor(f"lnb{i}", [128, NTA, D], F32))
                       for i in range(2)]
                gbuf = [EA(nc.sbuf_tensor(f"gbuf{i}", [128, NTA, D], F32))
                        for i in range(2)]
                gbfo = [EA(nc.sbuf_tensor("gbfo0", [128, KT, CHA], BF16))] * 2

                GD.go(fwvb[:], wn["ffn_wv"].rearrange("(j p) c -> p j c", p=128),
                      sem=S["sbw"])
                for e in (V, A, G, P):
                    e.wait(S["sbw"])

                chunks = [(b, ci) for b in range(NB) for ci in range(NCA)]
                load_n = {}
                ffc_n = {}

                def a2b_load(idx):
                    b, ci = chunks[idx]
                    par = idx % 2
                    t0 = ci * CHA
                    tg = b * L + t0
                    wait_free(SY, ("kfw", par))
                    SY.go(kfw[par][:], kf_r[:, :, tg:tg + CHA], sem=S["sx"])
                    wait_free(SY, ("sr2w", par))
                    SY.go(sr2w[par][:], sr2_r[:, :, tg:tg + CHA], sem=S["sx"])
                    wait_free(SY, ("xa3", par))
                    SY.go(xa3[par][:], S_XPP[b, t0:t0 + CHA, :]
                          .rearrange("(n p) c -> p n c", p=128), sem=S["sx"])
                    wait_free(SY, ("xr", par))
                    SY.go(xr[par][:], X[b, t0:t0 + CHA, :]
                          .rearrange("(n p) c -> p n c", p=128), sem=S["sx"])
                    load_n[idx] = S["sx"].n

                def a2b_back1(idx):
                    b, ci = chunks[idx]
                    par = idx % 2
                    P.wait(S["sx"], load_n[idx])
                    V.wait(S["sx"], load_n[idx])
                    wait_free(V, ("ffc", par))
                    for co in range(KT):
                        bank = co % 2
                        bank_wait(bank)
                        for j in range(JT):
                            ii = P.mm(psum[bank][:, 0:CHA],
                                      fwvb[:, j, co * 128:(co + 1) * 128],
                                      kfw[par][:, j, :],
                                      start=(j == 0), stop=(j == JT - 1))
                        S["smm"].bump(ii, 1)
                        V.wait(S["smm"])
                        iv = V.tt(ffc[par][:, co, :], sr2w[par][:, co, :],
                                  psum[bank][:, 0:CHA], ALU.mult)
                        S["sdv"].bump(iv, 1)
                        bank_set(bank, S["sdv"])
                    V.fence(S["sdv"])
                    ffc_n[idx] = S["sdv"].n
                    consumed(("kfw", par), S["smm"])
                    consumed(("sr2w", par), S["sdv"])

                def a2b_back2a(idx):
                    b, ci = chunks[idx]
                    par = idx % 2
                    t0 = ci * CHA
                    sb = par * 16
                    P.wait(S["sdv"], ffc_n[idx])
                    transpose_add(ffc[par], xa3[par], NTA)
                    consumed(("ffc", par), S["str_"])
                    for e in (A, G):
                        e.wait(S["sad"])
                    ln_stats(xa3[par], NTA, par)
                    wait_free(V, ("lnb", par))
                    ln_norm(lnb[par], xa3[par], NTA, par)
                    consumed(("xa3", par), S["snr"])
                    V.wait(S["sx"], load_n[idx])
                    V.tt(lnb[par][:], lnb[par][:], xr[par][:], ALU.add)
                    V.fence(S["snr"])
                    consumed(("xr", par), S["snr"])
                    SY.wait(S["snr"])
                    SY.go(S_Y[b, t0:t0 + CHA, :]
                          .rearrange("(n p) c -> p n c", p=128), lnb[par][:],
                          sem=S["sou"])
                    for e in (A, G):
                        e.wait(S["snr"])
                    ln_stats(lnb[par], NTA, par)

                def a2b_back2b(idx):
                    b, ci = chunks[idx]
                    par = idx % 2
                    t0 = ci * CHA
                    tg = b * L + t0
                    cg = b * NCA + ci
                    sb = par * 16
                    wait_free(V, ("gbuf", par))
                    ln_norm(gbuf[par], lnb[par], NTA, par)
                    consumed(("lnb", par), S["snr"])
                    SY.wait(S["snr"])
                    SY.go(S_G32[b, t0:t0 + CHA, :]
                          .rearrange("(n p) c -> p n c", p=128),
                          gbuf[par][:], sem=S["sou"])
                    P.wait(S["snr"])
                    wait_free(A, ("gbfo", 0))
                    transpose_affine(gbuf[par], gbfo[par], ("gbfo", 0),
                                     None, None,
                                     accum_slot=lambda kk, _cg=cg:
                                     pacc[:, kk, _cg:_cg + 1])
                    A.fence(S["saf"])
                    consumed(("gbuf", par), S["saf"])
                    SY.wait(S["saf"])
                    SY.go(gbf_r[:, :, tg:tg + CHA], gbfo[par][:], sem=S["sou"])
                    consumed(("gbfo", 0), S["sou"])

                a2b_load(0)
                a2b_load(1)
                a2b_back1(0)
                for idx in range(1, len(chunks)):
                    a2b_back1(idx)
                    if idx >= 2:
                        a2b_back2b(idx - 2)
                    a2b_back2a(idx - 1)
                    if idx + 1 < len(chunks):
                        a2b_load(idx + 1)
                a2b_back2b(len(chunks) - 2)
                a2b_back2a(len(chunks) - 1)
                a2b_back2b(len(chunks) - 1)
            barrier()

        if "R" in phases:
            sc_R = st.enter_context(nc.named_scope("PH_R"))
            with contextlib.ExitStack() as sa:
                EA = sa.enter_context
                W1s = EA(nc.sbuf_tensor("W1s", [128, KT, 256], F32))
                W2s = EA(nc.sbuf_tensor("W2s", [128, 2, NZ], F32))
                b1bc = EA(nc.sbuf_tensor("b1bc", [2, 256], F32))
                b2bc = EA(nc.sbuf_tensor("b2bc", [2, NZ], F32))
                pl2 = EA(nc.sbuf_tensor("pl2", [128, KT, 2], F32))
                hrt = EA(nc.sbuf_tensor("hrt", [2, 256], F32))
                hrtT = EA(nc.sbuf_tensor("hrtT", [128, 2, 2], F32))
                rsc = EA(nc.sbuf_tensor("rsc", [2, 64], F32))
                wzrow = EA(nc.sbuf_tensor("wzrow", [1, 16], F32))

                SY.go(W1s[:], wn["W1"].rearrange("(k p) h -> p k h", p=128),
                      sem=S["srt"])
                SY.go(W2s[:], wn["W2"].rearrange("(k p) z -> p k z", p=128),
                      sem=S["srt"])
                for prt in range(2):
                    SY.go(b1bc[prt:prt + 1, :], wn["b1"].rearrange("(o h) -> o h", o=1),
                          sem=S["srt"])
                    SY.go(b2bc[prt:prt + 1, :], wn["b2"].rearrange("(o z) -> o z", o=1),
                          sem=S["srt"])
                for e in (V, A, G, P):
                    e.wait(S["srt"])
                for b in range(NB):
                    V.red(pl2[:, :, b:b + 1],
                          pacc[:, :, b * NCA:(b + 1) * NCA], ALU.add)
                V.ts(pl2[:], pl2[:], 1.0 / L, ALU.mult)
                V.fence(S["sst"])
                P.wait(S["sst"])
                bank_wait(6)
                for kk in range(KT):
                    ii = P.mm(psum[6][0:2, 0:256], pl2[:, kk, :], W1s[:, kk, :],
                              start=(kk == 0), stop=(kk == KT - 1))
                S["smm"].bump(ii, 1)
                V.wait(S["smm"])
                V.tt(hrt[:], psum[6][0:2, 0:256], b1bc[:], ALU.add)
                V.fence(S["sdv"])
                bank_set(6, S["sdv"])
                A.wait(S["sdv"])
                A.act(hrt[:], hrt[:], AF.Tanh)
                A.fence(S["sek"])
                P.wait(S["sek"])
                bank_wait(4)
                for kk in range(2):
                    ii = P.tr(psum[4][:, kk * 2:kk * 2 + 2],
                              hrt[0:2, kk * 128:(kk + 1) * 128], idn[0:2, 0:2])
                S["str_"].bump(ii, 1)
                A.wait(S["str_"])
                A.act(hrtT[:, 0, :], psum[4][:, 0:2], AF.Copy)
                A.act(hrtT[:, 1, :], psum[4][:, 2:4], AF.Copy)
                A.fence(S["sek"])
                bank_set(4, S["sek"])
                P.wait(S["sek"])
                for kk in range(2):
                    ii = P.mm(psum[6][0:2, 256:256 + NZ], hrtT[:, kk, :],
                              W2s[:, kk, :], start=(kk == 0), stop=(kk == 1))
                S["smm"].bump(ii, 1)
                V.wait(S["smm"])
                lg = rsc[:, 0:8]
                V.tt(lg, psum[6][0:2, 256:256 + NZ], b2bc[:], ALU.add)
                V.red(rsc[:, 8:9], lg, ALU.max)
                V.ts(rsc[:, 8:9], rsc[:, 8:9], -1.0, ALU.mult)
                V.fence(S["sdv"])
                bank_set(6, S["sdv"])
                A.wait(S["sdv"])
                A.act(lg, lg, AF.Exp, bias=rsc[:, 8:9])
                A.fence(S["sek"])
                V.wait(S["sek"])
                V.red(rsc[:, 9:10], lg, ALU.add)
                V.recip(rsc[:, 9:10], rsc[:, 9:10])
                V.ts(lg, lg, rsc[:, 9:10], ALU.mult)
                V.red(rsc[:, 18:19], lg, ALU.max)
                V.ts(rsc[:, 40:48], lg, rsc[:, 18:19], ALU.is_ge)
                V.stt(rsc[:, 10:18], rsc[:, 40:48], -1e30, lg, ALU.mult, ALU.add)
                V.red(rsc[:, 19:20], rsc[:, 10:18], ALU.max)
                V.ts(rsc[:, 48:56], rsc[:, 10:18], rsc[:, 19:20], ALU.is_ge)
                V.stt(rsc[:, 26:34], rsc[:, 48:56], -1e30, rsc[:, 10:18],
                      ALU.mult, ALU.add)
                V.red(rsc[:, 20:21], rsc[:, 26:34], ALU.max)
                V.ts(rsc[:, 56:64], rsc[:, 26:34], rsc[:, 20:21], ALU.is_ge)
                V.tt(rsc[:, 35:36], rsc[:, 18:19], rsc[:, 19:20], ALU.add)
                V.tt(rsc[:, 35:36], rsc[:, 35:36], rsc[:, 20:21], ALU.add)
                V.recip(rsc[:, 36:37], rsc[:, 35:36])
                for r_ in range(3):
                    V.ts(rsc[:, 24 + r_:25 + r_], rsc[:, 18 + r_:19 + r_],
                         rsc[:, 36:37], ALU.mult)
                for r_ in range(3):
                    V.tt(rsc[:, 10:18], rsc[:, 40 + 8 * r_:48 + 8 * r_],
                         iota8f[:], ALU.mult)
                    V.red(rsc[:, 21 + r_:22 + r_], rsc[:, 10:18], ALU.add)
                V.fence(S["sdv"])
                SY.wait(S["sdv"])
                SY.go(wzrow[0:1, 0:3], rsc[0:1, 24:27], sem=S["srt"])
                SY.go(wzrow[0:1, 3:6], rsc[1:2, 24:27], sem=S["srt"])
                SY.go(wzrow[0:1, 8:11], rsc[0:1, 21:24], sem=S["srt"])
                SY.go(wzrow[0:1, 11:14], rsc[1:2, 21:24], sem=S["srt"])
                P.wait(S["srt"])
                bank_wait(6)
                for be in list(range(6)) + list(range(8, 14)):
                    ii = P.mm(psum[6][:, be:be + 1], onesr[0:1, :],
                              wzrow[0:1, be:be + 1], start=True, stop=True)
                S["smm"].bump(ii, 1)
                A.wait(S["smm"])
                A.act(wzbc[:], psum[6][:, 0:16], AF.Copy)
                A.fence(S["sek"])
                bank_set(6, S["sek"])
                G.wait(S["sek"])
                for ps_ in range(6):
                    ecol = wzbc[:, 8 + ps_:9 + ps_]
                    G.ts(rsF[:, 0:1], ecol, 1024.0, ALU.mult)
                    G.ts(idxTf[:, 0:8], jjoff_f[:, 0:8], rsF[:, 0:1], ALU.add)
                    G.copy(idxAi[:, 8 * ps_:8 * (ps_ + 1)], idxTf[:, 0:8])
                    G.ts(rsF[:, 0:1], ecol, 4096.0, ALU.mult)
                    G.ts(idxTf[:], jjoff_f[:], rsF[:, 0:1], ALU.add)
                    G.copy(idxBi[:, 32 * ps_:32 * (ps_ + 1)], idxTf[:])
                G.fence(S["ssc"])
            barrier()

        if "B" in phases:
            sc_B = st.enter_context(nc.named_scope("PH_B"))
            with contextlib.ExitStack() as sa:
                EA = sa.enter_context
                wab = EA(nc.sbuf_tensor("wab", [128, KT, DFF], BF16))
                wbb = EA(nc.sbuf_tensor("wbb", [128, JT, D], BF16))
                gw = [EA(nc.sbuf_tensor(f"gw{i}", [128, KT, TB], BF16))
                      for i in range(2)]
                zb = EA(nc.sbuf_tensor("zb", [128, JT, TB], BF16))
                zcb = EA(nc.sbuf_tensor("zcb", [128, KT, TB], F32))

                WaF = wn["Wa"].rearrange("e r c -> (e r) c")
                WbF = wn["Wb"].rearrange("e r c -> (e r) c")
                NPASS = NB * 3
                NCBP = L // TB

                def load_wa(p):
                    for kk in range(KT):
                        gi = GD.e.indirect_dma_start(
                            wab[:, kk, :], None, WaF,
                            IndirectOffsetOnAxis(
                                idxAi[:, 8 * p + kk:8 * p + kk + 1], axis=0))
                        S["sbw"].bump(gi, 16)
                    return S["sbw"].n

                def load_wb(p):
                    for jj in range(JT):
                        gi = GD.e.indirect_dma_start(
                            wbb[:, jj, :], None, WbF,
                            IndirectOffsetOnAxis(
                                idxBi[:, 32 * p + jj:32 * p + jj + 1], axis=0))
                        S["sbw"].bump(gi, 16)
                    return S["sbw"].n

                wa_ready = load_wa(0)
                wb_ready = load_wb(0)

                m2_done = None
                for p in range(NPASS):
                    b = p // 3
                    r = p % 3
                    for ch in range(NCBP):
                        tg = b * L + ch * TB
                        par = ch % 2
                        wait_free(SY, ("gw", par))
                        SY.go(gw[par][:], gbf_r[:, :, tg:tg + TB], sem=S["sbg"])
                        P.wait(S["sbg"])
                        if ch == 0:
                            P.wait(S["sbw"], wa_ready)
                        if m2_done is not None:
                            A.e.wait_ge(m2_done[0].h, m2_done[1])
                        for j in range(JT):
                            bank = j % 2
                            bank_wait(bank)
                            for kk in range(KT):
                                ii = P.mm(psum[bank][:, 0:TB],
                                          wab[:, kk, j * 128:(j + 1) * 128],
                                          gw[par][:, kk, :],
                                          start=(kk == 0), stop=(kk == KT - 1))
                            S["sbm1"].bump(ii, 1)
                            A.wait(S["sbm1"])
                            ia = A.act(zb[:, j, :], psum[bank][:, 0:TB],
                                       AF.Gelu_apprx_tanh)
                            S["sbz"].bump(ia, 1)
                            bank_set(bank, S["sbz"])
                        consumed(("gw", par), S["sbm1"])
                        if ch == NCBP - 1 and p < NPASS - 1:
                            GD.wait(S["sbm1"])
                            wa_ready = load_wa(p + 1)
                        P.wait(S["sbz"])
                        if p == 0 and ch == 0:
                            P.wait(S["sbw"], wb_ready)
                        for cog in range(4):
                            bk0 = 2 + 2 * (cog % 2)
                            bk1 = 3 + 2 * (cog % 2)
                            bank_wait(bk0)
                            bank_wait(bk1)
                            for j in range(JT):
                                P.mm(psum[bk0][:, 0:TB],
                                     wbb[:, j, (2 * cog) * 128:(2 * cog + 1) * 128],
                                     zb[:, j, :], start=(j == 0), stop=(j == JT - 1))
                                ii = P.mm(psum[bk1][:, 0:TB],
                                          wbb[:, j,
                                              (2 * cog + 1) * 128:(2 * cog + 2) * 128],
                                          zb[:, j, :], start=(j == 0),
                                          stop=(j == JT - 1))
                            S["sbm2"].bump(ii, 1)
                            V.wait(S["sbm2"])
                            wait_free(V, ("zcb", cog))
                            V.ts(zcb[:, 2 * cog, :], psum[bk0][:, 0:TB],
                                 wzbc[:, p:p + 1], ALU.mult)
                            V.ts(zcb[:, 2 * cog + 1, :], psum[bk1][:, 0:TB],
                                 wzbc[:, p:p + 1], ALU.mult)
                            V.fence(S["sbev"])
                            bank_set(bk0, S["sbev"])
                            bank_set(bk1, S["sbev"])
                        m2_done = (S["sbm2"], S["sbm2"].n)
                        GD.wait(S["sbev"])
                        if r > 0:
                            GD.go(zac_r[:, :, tg:tg + TB], zcb[:],
                                  sem=S["sbac"], accum_op=ALU.add)
                        else:
                            GD.go(zac_r[:, :, tg:tg + TB], zcb[:], sem=S["sbac"])
                        for cog in range(4):
                            consumed(("zcb", cog), S["sbac"])
                    if p < NPASS - 1:
                        GD.wait(S["sbm2"])
                        wb_ready = load_wb(p + 1)
            barrier()

        if "C" in phases:
            sc_C = st.enter_context(nc.named_scope("PH_C"))
            with contextlib.ExitStack() as sa:
                EA = sa.enter_context
                zw = [EA(nc.sbuf_tensor(f"zw{i}", [128, KT, TC], F32))
                      for i in range(2)]
                g32w = [EA(nc.sbuf_tensor(f"g32w{i}", [128, NTC, D], F32))
                        for i in range(2)]
                yw = [EA(nc.sbuf_tensor(f"yw{i}", [128, NTC, D], F32))
                      for i in range(2)]
                sqc = EA(nc.sbuf_tensor("sqc", [128, 1, D], F32))
                ow = [EA(nc.sbuf_tensor(f"ow{i}", [128, NTC, D], F32))
                      for i in range(2)]
                stc = EA(nc.sbuf_tensor("stc", [128, NTC, 32], F32))

                c_load_n = {}

                def c_load(ch):
                    tg = ch * TC
                    b = tg // L
                    t0 = tg % L
                    par = ch % 2
                    wait_free(SY, ("zw", par))
                    SY.go(zw[par][:], zac_r[:, :, tg:tg + TC], sem=S["sx"])
                    wait_free(SY, ("g32w", par))
                    SY.go(g32w[par][:], S_G32[b, t0:t0 + TC, :]
                          .rearrange("(n p) c -> p n c", p=128), sem=S["sx"])
                    wait_free(SY, ("yw", par))
                    SY.go(yw[par][:], S_Y[b, t0:t0 + TC, :]
                          .rearrange("(n p) c -> p n c", p=128), sem=S["sx"])
                    c_load_n[ch] = S["sx"].n

                c_load(0)
                if NCC > 1:
                    c_load(1)
                for ch in range(NCC):
                    tg = ch * TC
                    b = tg // L
                    t0 = tg % L
                    par = ch % 2
                    o = par * 16
                    P.wait(S["sx"], c_load_n[ch])
                    V.wait(S["sx"], c_load_n[ch])
                    transpose_add(zw[par], g32w[par], NTC)
                    consumed(("zw", par), S["sad"])
                    for e in (A, G):
                        e.wait(S["sad"])
                    wait_free(A, ("stc", o))
                    for n in range(NTC):
                        A.act(sqc[:, 0, :], g32w[par][:, n, :], AF.Square,
                              accum_out=stc[:, n, o + 1:o + 2])
                        A.act(sqc[:, 0, :], g32w[par][:, n, :], AF.Copy,
                              accum_out=stc[:, n, o + 0:o + 1])
                    A.fence(S["srd"])
                    G.wait(S["srd"])
                    c = 1.0 / D
                    scl = lambda i: stc[:, :, o + i:o + i + 1]
                    G.ts(scl(2), scl(0), c, ALU.mult)
                    G.tt(scl(3), scl(2), scl(2), ALU.mult)
                    G.ts(scl(4), scl(1), c, ALU.mult)
                    G.tt(scl(4), scl(4), scl(3), ALU.subtract)
                    G.fence(S["sst"])
                    A.wait(S["sst"])
                    A.act(scl(5), scl(4), AF.Ln, bias=EPS)
                    A.act(scl(7), scl(5), AF.Exp, scale=-0.5)
                    A.fence(S["sst"])
                    G.wait(S["sst"])
                    G.tt(scl(6), scl(2), scl(7), ALU.mult)
                    G.ts(scl(6), scl(6), -1.0, ALU.mult)
                    G.fence(S["srd"])
                    V.wait(S["srd"])
                    wait_free(V, ("ow", par))
                    for n in range(NTC):
                        V.ts(ow[par][:, n, :], g32w[par][:, n, :],
                             stc[:, n, o + 7:o + 8], ALU.mult,
                             stc[:, n, o + 6:o + 7], ALU.add)
                    V.tt(ow[par][:], ow[par][:], yw[par][:], ALU.add)
                    V.fence(S["snr"])
                    consumed(("stc", o), S["snr"])
                    consumed(("g32w", par), S["snr"])
                    consumed(("yw", par), S["snr"])
                    SY.wait(S["snr"])
                    SY.go(OUT[b, t0:t0 + TC, :]
                          .rearrange("(n p) c -> p n c", p=128),
                          ow[par][:], sem=S["sfin"])
                    consumed(("ow", par), S["sfin"])
                    n_out_dma[0] += 1
                    if ch + 2 < NCC:
                        c_load(ch + 2)
            SY.wait(S["sfin"], n_out_dma[0] * 16)
        else:
            SY.wait(S["sou"], S["sou"].n)

    return nc


def kernel(**inputs):
    xs = np.ascontiguousarray(np.asarray(inputs["L0_attn_x"], dtype=np.float32))
    base = {}
    for k, v in inputs.items():
        if k == "L0_attn_x":
            continue
        base[k] = np.ascontiguousarray(np.asarray(v, dtype=np.float32))
    base["identity"] = np.eye(128, dtype=np.float32)
    in_maps = []
    for c in range(8):
        m = dict(base)
        m["x"] = np.ascontiguousarray(xs[2 * c:2 * c + 2])
        in_maps.append(m)
    nc = build()
    res = run_bass_kernel_spmd(nc, in_maps, core_ids=list(range(8)))
    out = np.concatenate([np.asarray(r["out"], dtype=np.float32)
                          for r in res.results], axis=0)
    return out.reshape(16, L, D)


if __name__ == "__main__":
    nc = build()
    print("instructions:", sum(1 for _ in nc.all_instructions()))


# version 09
# speedup vs baseline: 1.2114x; 1.0116x over previous
import contextlib
import numpy as np
import concourse.bass as bass
import concourse.mybir as mybir
from concourse.bass_utils import run_bass_kernel_spmd
from concourse.bass import IndirectOffsetOnAxis

F32 = mybir.dt.float32
BF16 = mybir.dt.bfloat16
AF = mybir.ActivationFunctionType
ALU = mybir.AluOpType
AX = mybir.AxisListType

NB = 2
L = 2048
D = 1024
KT = 8
DFF = 4096
JT = 32
CHA = 256
NCA = L // CHA
NTA = CHA // 128
TB = 512
NCB = NB * L // TB
TC = 512
NCC = NB * L // TC
NTC = TC // 128
EPS = 1e-5
NZ = 8


def _names(args):
    s = set()
    for ap in args:
        if ap is None or isinstance(ap, (int, float)):
            continue
        t = getattr(ap, "tensor", None)
        n = getattr(t, "name", None)
        if n is not None:
            s.add(n)
    return s


class Sem:
    def __init__(self, nc, stack, name):
        self.h = stack.enter_context(nc.semaphore(name))
        self.n = 0

    def bump(self, instr, k=1):
        instr.then_inc(self.h, k)
        self.n += k
        return self.n


class Eng:

    def __init__(self, eng):
        self.e = eng
        self.dirty = set()

    def wait(self, sem, n=None):
        self.e.wait_ge(sem.h, sem.n if n is None else n)

    def drain(self):
        self.e.drain()
        self.dirty = set()

    def inc(self, sem, k=1):
        self.e.sem_inc(sem.h, k)
        sem.n += k

    def fence(self, sem):
        self.drain()
        self.inc(sem)

    def _pre(self, outs, ins):
        if (_names(outs) | _names(ins)) & self.dirty:
            self.e.drain()
            self.dirty = set()
        self.dirty |= _names(outs)

    def copy(self, out, in_):
        self._pre([out], [in_])
        return self.e.tensor_copy(out, in_)

    def memset(self, out, v):
        self._pre([out], [])
        return self.e.memset(out, v)

    def tt(self, out, a, b, op):
        self._pre([out], [a, b])
        return self.e.tensor_tensor(out, a, b, op)

    def ts(self, out, a, s1, op0, s2=None, op1=ALU.bypass, accum_out=None):
        self._pre([out, accum_out], [a, s1, s2])
        return self.e.tensor_scalar(out, a, s1, s2, op0, op1, accum_out)

    def stt(self, out, in0, s, in1, op0, op1):
        self._pre([out], [in0, s, in1])
        return self.e.scalar_tensor_tensor(out, in0, s, in1, op0, op1)

    def scan(self, out, d0, d1, init):
        self._pre([out], [d0, d1, init])
        return self.e.tensor_tensor_scan(out, d0, d1, init, ALU.mult, ALU.add)

    def red(self, out, in_, op):
        self._pre([out], [in_])
        return self.e.tensor_reduce(out, in_, AX.X, op)

    def act(self, out, in_, func, bias=0.0, scale=1.0, accum_out=None):
        self._pre([out, accum_out], [in_, bias, scale])
        return self.e.activation(out, in_, func, bias=bias, scale=scale,
                                 accum_out=accum_out)

    def recip(self, out, in_):
        self._pre([out], [in_])
        return self.e.reciprocal(out, in_)


class PEW:
    def __init__(self, eng):
        self.e = eng

    def wait(self, sem, n=None):
        self.e.wait_ge(sem.h, sem.n if n is None else n)

    def inc(self, sem, k=1):
        self.e.sem_inc(sem.h, k)
        sem.n += k

    def mm(self, out, lhsT, rhs, start, stop):
        return self.e.matmul(out, lhsT, rhs, start=start, stop=stop)

    def tr(self, out, in_, idn):
        return self.e.transpose(out, in_, idn)


class DMAQ:
    def __init__(self, eng):
        self.e = eng

    def wait(self, sem, n=None):
        self.e.wait_ge(sem.h, sem.n if n is None else n)

    def inc(self, sem, k=1):
        self.e.sem_inc(sem.h, k)
        sem.n += k

    def go(self, out, in_, sem=None, **kw):
        i = self.e.dma_start(out=out, in_=in_, **kw)
        if sem is not None:
            sem.bump(i, 16)
        return i


def build(phases=("A1", "A2a", "A2b", "R", "B", "C"), debug=False):
    nc = bass.Bass("TRN2", target_bir_lowering=False)

    X = nc.dram_tensor("x", [NB, L, D], F32, kind="ExternalInput").ap()
    OUT = nc.dram_tensor("out", [NB, L, D], F32, kind="ExternalOutput").ap()
    IDN = nc.dram_tensor("identity", [128, 128], F32, kind="ExternalInput").ap()

    wn = {}
    for nm, shp in [
        ("ln_att_g", [D]), ("ln_att_b", [D]), ("ln_ffn_g", [D]), ("ln_ffn_b", [D]),
        ("att_time_decay", [D]), ("att_time_first", [D]),
        ("att_mix_k", [D]), ("att_mix_v", [D]), ("att_mix_r", [D]),
        ("att_wk", [D, D]), ("att_wv", [D, D]), ("att_wr", [D, D]), ("att_wo", [D, D]),
        ("ffn_mix_k", [D]), ("ffn_mix_r", [D]),
        ("ffn_wk", [D, DFF]), ("ffn_wv", [DFF, D]), ("ffn_wr", [D, D]),
    ]:
        wn[nm] = nc.dram_tensor("L0_attn_" + nm, shp, F32, kind="ExternalInput").ap()
    for nm, shp in [("W1", [D, 256]), ("b1", [256]), ("W2", [256, NZ]), ("b2", [NZ]),
                    ("Wa", [NZ, D, DFF]), ("Wb", [NZ, DFF, D])]:
        wn[nm] = nc.dram_tensor("L0_ffn_" + nm, shp, F32, kind="ExternalInput").ap()

    skind = "ExternalOutput" if debug else "Internal"
    S_XPP = nc.dram_tensor("s_xpp", [NB, L, D], F32, kind=skind).ap()
    S_Y = nc.dram_tensor("s_y", [NB, L, D], F32, kind=skind).ap()
    S_G32 = nc.dram_tensor("s_g32", [NB, L, D], F32, kind=skind).ap()
    S_GBF = nc.dram_tensor("s_gbf", [D, NB * L], BF16, kind=skind).ap()
    S_KF = nc.dram_tensor("s_kf", [DFF, NB * L], BF16, kind=skind).ap()
    S_SR2 = nc.dram_tensor("s_sr2", [D, NB * L], F32, kind=skind).ap()
    S_ZAC = nc.dram_tensor("s_zac", [D, NB * L], F32, kind=skind).ap()

    gbf_r = S_GBF.rearrange("(k p) t -> p k t", p=128)
    kf_r = S_KF.rearrange("(j p) t -> p j t", p=128)
    sr2_r = S_SR2.rearrange("(k p) t -> p k t", p=128)
    zac_r = S_ZAC.rearrange("(k p) t -> p k t", p=128)

    st = contextlib.ExitStack()
    with st:
        E = st.enter_context

        V = Eng(nc.vector)
        A = Eng(nc.scalar)
        G = Eng(nc.gpsimd)
        P = PEW(nc.tensor)
        SY = DMAQ(nc.sync)
        GD = DMAQ(nc.gpsimd)

        semnames = ["ld", "sx", "ssq", "srd", "sst", "snr", "str_", "saf", "smx",
                    "smm", "sek", "sdv", "ssc", "swo", "sad", "sou", "sbw", "sbg",
                    "sbz", "sbm1", "sbm2", "sbev", "sbac", "sfin", "sbar", "srt"]
        S = {n: Sem(nc, st, n) for n in semnames}

        dma_sems = [S[k] for k in ("ld", "sx", "sou", "sbw", "sbg", "sbac",
                                   "srt", "sfin")]

        def barrier():
            for dsem in dma_sems:
                SY.wait(dsem)
            for e in (V, A, G):
                e.fence(S["sbar"])
            P.e.drain()
            P.inc(S["sbar"])
            SY.inc(S["sbar"])
            tgt = S["sbar"].n
            for e in (V, A, G, P, SY):
                e.wait(S["sbar"], tgt)

        FREE = {}

        def consumed(key, sem):
            FREE[key] = (sem, sem.n)

        def wait_free(eng, key):
            if key in FREE:
                sem, n = FREE[key]
                eng.wait(sem, n)

        BANK = {}

        def bank_wait(i):
            if i in BANK:
                sem, n = BANK[i]
                P.wait(sem, n)

        def bank_set(i, sem):
            BANK[i] = (sem, sem.n)

        idn = E(nc.sbuf_tensor("idn", [128, 128], F32))
        pnames = ["lag", "lab", "lfg", "lfb", "mk", "mv", "mr",
                  "fmk", "fmr", "ew", "eu"]
        PB = {n: E(nc.sbuf_tensor("pp_" + n, [128, KT], F32)) for n in pnames}
        carN = E(nc.sbuf_tensor("carN", [128, NB, KT], F32))
        carQ = E(nc.sbuf_tensor("carQ", [128, NB, KT], F32))
        pacc = E(nc.sbuf_tensor("pacc", [128, KT, 16], F32))
        RNAMES = ["ssum", "ssum2", "snmu", "sex2", "snv", "sL", "srstd",
                  "sMn", "sT", "sU", "sR2", "sR2b", "sM2n", "sIvr"]
        RS = {n: E(nc.sbuf_tensor("r_" + n, [128, NTA, 2], F32)) for n in RNAMES}
        sqA = E(nc.sbuf_tensor("sqA", [128, D], F32))
        wzbc = E(nc.sbuf_tensor("wzbc", [128, 16], F32))
        onesr = E(nc.sbuf_tensor("onesr", [1, 128], F32))
        cst = E(nc.sbuf_tensor("cst", [128, 4], F32))
        I32 = mybir.dt.int32
        iota8f = E(nc.sbuf_tensor("iota8f", [2, 8], F32))
        iota8i = E(nc.sbuf_tensor("iota8i", [2, 8], I32))
        jjoff_i = E(nc.sbuf_tensor("jjoff_i", [128, 32], I32))
        jjoff_f = E(nc.sbuf_tensor("jjoff_f", [128, 32], F32))
        idxAi = E(nc.sbuf_tensor("idxAi", [128, 48], I32))
        idxBi = E(nc.sbuf_tensor("idxBi", [128, 192], I32))
        idxTf = E(nc.sbuf_tensor("idxTf", [128, 32], F32))
        rsF = E(nc.sbuf_tensor("rsF", [128, 1], F32))
        nc.const_aps.register(F32, EPS, cst[:, 0:1])
        nc.const_aps.register(F32, 1.0, cst[:, 1:2])
        nc.const_aps.register(F32, 1.0 + EPS, cst[:, 2:3])
        nc.const_aps.register(F32, -EPS, cst[:, 3:4])
        psum = [E(nc.psum_tensor(f"pb{i}", [128, 512], F32)) for i in range(8)]

        SY.go(idn[:], IDN, sem=S["ld"])
        for nm, src in [("lag", "ln_att_g"), ("lab", "ln_att_b"),
                        ("lfg", "ln_ffn_g"), ("lfb", "ln_ffn_b"),
                        ("mk", "att_mix_k"), ("mv", "att_mix_v"),
                        ("mr", "att_mix_r"), ("fmk", "ffn_mix_k"),
                        ("fmr", "ffn_mix_r"), ("ew", "att_time_decay"),
                        ("eu", "att_time_first")]:
            SY.go(PB[nm][:], wn[src].rearrange("(k p) -> p k", p=128),
                  sem=S["ld"], allow_slow_non_contiguous=True)
        for e in (V, A, G):
            e.wait(S["ld"])
        V.memset(onesr[:], 1.0)
        A.act(PB["ew"][:], PB["ew"][:], AF.Exp)
        A.act(PB["ew"][:], PB["ew"][:], AF.Exp, scale=-1.0)
        A.act(PB["eu"][:], PB["eu"][:], AF.Exp)
        V.fence(S["sst"])
        A.fence(S["sst"])
        G.memset(carN[:], 0.0)
        G.memset(carQ[:], 0.0)
        G.memset(pacc[:], 0.0)
        G.memset(cst[:, 0:1], EPS)
        G.memset(cst[:, 1:2], 1.0)
        G.memset(cst[:, 2:3], 1.0 + EPS)
        G.memset(cst[:, 3:4], -EPS)
        G.e.iota(jjoff_i[:], [[128, 32]], base=0, channel_multiplier=1)
        G.e.iota(iota8i[:], [[1, 8]], base=0, channel_multiplier=0)
        G.drain()
        G.copy(jjoff_f[:], jjoff_i[:])
        G.copy(iota8f[:], iota8i[:])
        G.fence(S["sst"])

        n_out_dma = [0]

        def rsl(nm, n_tiles, par):
            return RS[nm][:, 0:n_tiles, par:par + 1]

        def ln_sums(src, n_tiles, par):
            wait_free(A, ("stats", par))
            wait_free(G, ("stats", par))
            for n in range(n_tiles):
                A.e.activation(sqA[:], src[:, n, :], AF.Square,
                               accum_out=RS["ssum2"][:, n, par:par + 1])
                A.e.activation(sqA[:], src[:, n, :], AF.Copy,
                               accum_out=RS["ssum"][:, n, par:par + 1])
            A.fence(S["srd"])
            G.wait(S["srd"])
            c = 1.0 / D
            G.ts(rsl("snmu", n_tiles, par), rsl("ssum", n_tiles, par),
                 -c, ALU.mult)
            G.ts(rsl("sex2", n_tiles, par), rsl("ssum2", n_tiles, par),
                 c, ALU.mult)
            G.tt(rsl("snv", n_tiles, par), rsl("snmu", n_tiles, par),
                 rsl("snmu", n_tiles, par), ALU.mult)
            G.tt(rsl("snv", n_tiles, par), rsl("snv", n_tiles, par),
                 rsl("sex2", n_tiles, par), ALU.subtract)
            G.fence(S["sst"])
            A.wait(S["sst"])
            A.act(rsl("sL", n_tiles, par), rsl("snv", n_tiles, par),
                  AF.Ln, scale=-1.0, bias=EPS)

        def ln_stats(src, n_tiles, par):
            ln_sums(src, n_tiles, par)
            A.act(rsl("srstd", n_tiles, par), rsl("sL", n_tiles, par),
                  AF.Exp, scale=-0.5)
            A.fence(S["sst"])
            G.wait(S["sst"])
            G.tt(rsl("sMn", n_tiles, par), rsl("snmu", n_tiles, par),
                 rsl("srstd", n_tiles, par), ALU.mult)
            G.fence(S["srd"])

        def ln_stats_fused(src, n_tiles, par):
            ln_sums(src, n_tiles, par)
            A.act(rsl("srstd", n_tiles, par), rsl("sL", n_tiles, par),
                  AF.Exp, scale=-0.5)
            A.act(rsl("sT", n_tiles, par), rsl("sL", n_tiles, par),
                  AF.Exp, scale=-1.0)
            A.act(rsl("sU", n_tiles, par), rsl("sT", n_tiles, par),
                  AF.Ln, scale=-EPS, bias=1.0 + EPS)
            A.act(rsl("sIvr", n_tiles, par), rsl("sU", n_tiles, par),
                  AF.Exp, scale=0.5)
            A.act(rsl("sR2b", n_tiles, par), rsl("sU", n_tiles, par),
                  AF.Exp, scale=-0.5)
            A.fence(S["sst"])
            G.wait(S["sst"])
            G.tt(rsl("sR2", n_tiles, par), rsl("srstd", n_tiles, par),
                 rsl("sR2b", n_tiles, par), ALU.mult)
            G.tt(rsl("sM2n", n_tiles, par), rsl("snmu", n_tiles, par),
                 rsl("sR2", n_tiles, par), ALU.mult)
            G.fence(S["srd"])

        def ln_norm(dst, src, n_tiles, par, on=None, scale_nm="srstd",
                    bias_nm="sMn"):
            e = V if on is None else on
            e.wait(S["srd"])
            for n in range(n_tiles):
                if e is A:
                    e.act(dst[:, n, :], src[:, n, :], AF.Identity,
                          scale=RS[scale_nm][:, n, par:par + 1],
                          bias=RS[bias_nm][:, n, par:par + 1])
                else:
                    e.ts(dst[:, n, :], src[:, n, :],
                         RS[scale_nm][:, n, par:par + 1], ALU.mult,
                         RS[bias_nm][:, n, par:par + 1], ALU.add)
            e.fence(S["snr"])
            consumed(("stats", par), S["snr"])

        def transpose_affine(src_tok, dst_ch, par_key, gcol, bcol, accum_slot=None):
            off = 1 if dst_ch.shape[2] == CHA + 1 else 0
            for kk in range(KT):
                bank = 4 + (kk % 2)
                bank_wait(bank)
                for n in range(NTA):
                    ii = P.tr(psum[bank][:, n * 128:(n + 1) * 128],
                              src_tok[:, n, kk * 128:(kk + 1) * 128], idn[:])
                S["str_"].bump(ii, 1)
                A.wait(S["str_"])
                kwargs = {}
                if accum_slot is not None:
                    kwargs["accum_out"] = accum_slot(kk)
                if gcol is not None:
                    ia = A.act(dst_ch[:, kk, off:off + CHA], psum[bank][:, 0:CHA],
                               AF.Identity, scale=gcol[:, kk:kk + 1],
                               bias=bcol[:, kk:kk + 1], **kwargs)
                else:
                    ia = A.act(dst_ch[:, kk, off:off + CHA], psum[bank][:, 0:CHA],
                               AF.Copy, **kwargs)
                S["saf"].bump(ia, 1)
                bank_set(bank, S["saf"])

        def transpose_add(src_ch, dst_tok, n_tiles):
            for n in range(n_tiles):
                bank_wait(4)
                bank_wait(5)
                for kk in range(KT):
                    bank = 4 + kk // 4
                    ii = P.tr(psum[bank][:, (kk % 4) * 128:(kk % 4 + 1) * 128],
                              src_ch[:, kk, n * 128:(n + 1) * 128], idn[:])
                    if kk == 7:
                        S["str_"].bump(ii, 1)
                V.wait(S["str_"])
                V.tt(dst_tok[:, n, 0:512], dst_tok[:, n, 0:512],
                     psum[4][:, 0:512], ALU.add)
                V.tt(dst_tok[:, n, 512:1024], dst_tok[:, n, 512:1024],
                     psum[5][:, 0:512], ALU.add)
                V.fence(S["sad"])
                bank_set(4, S["sad"])
                bank_set(5, S["sad"])

        barrier()

        if "A1" in phases:
            sc_A1 = st.enter_context(nc.named_scope("PH_A1"))
            with contextlib.ExitStack() as sa:
                EA = sa.enter_context
                wkb = EA(nc.sbuf_tensor("wkb", [128, KT, D], BF16))
                wvb = EA(nc.sbuf_tensor("wvb", [128, KT, D], BF16))
                wrb = EA(nc.sbuf_tensor("wrb", [128, KT, D], BF16))
                wob = EA(nc.sbuf_tensor("wob", [128, KT, D], BF16))
                ewb = EA(nc.sbuf_tensor("ewb", [128, KT, CHA], F32))
                ones2 = EA(nc.sbuf_tensor("ones2", [128, CHA], F32))
                xa = [EA(nc.sbuf_tensor(f"xa{i}", [128, NTA, D], F32)) for i in range(2)]
                hch = [EA(nc.sbuf_tensor(f"hch{i}", [128, KT, CHA + 1], F32))
                       for i in range(2)]
                dbf = EA(nc.sbuf_tensor("dbf", [128, KT, CHA], F32))
                mixb = {m: [EA(nc.sbuf_tensor(f"mx{m}{i}", [128, KT, CHA], BF16))
                            for i in range(2)] for m in "kvr"}
                eKb2 = [[EA(nc.sbuf_tensor(f"eK{s}_{i}", [128, CHA], F32))
                         for i in range(3)] for s in range(2)]
                erb2 = [[EA(nc.sbuf_tensor(f"er{s}_{i}", [128, CHA], F32))
                         for i in range(3)] for s in range(2)]
                vvb2 = [[EA(nc.sbuf_tensor(f"vv{s}_{i}", [128, CHA], F32))
                         for i in range(3)] for s in range(2)]
                ekvb2 = [[EA(nc.sbuf_tensor(f"ekv{s}_{i}", [128, CHA], F32))
                          for i in range(3)] for s in range(2)]
                NNb2 = [[EA(nc.sbuf_tensor(f"NN{s}_{i}", [128, CHA + 1], F32))
                         for i in range(3)] for s in range(2)]
                QQb2 = [[EA(nc.sbuf_tensor(f"QQ{s}_{i}", [128, CHA + 1], F32))
                         for i in range(3)] for s in range(2)]
                rwk = [EA(nc.sbuf_tensor(f"rwk{i}", [128, KT, CHA], BF16))
                       for i in range(2)]
                attc = EA(nc.sbuf_tensor("attc", [128, KT, CHA], F32))

                for buf, src in [(wkb, "att_wk"), (wvb, "att_wv"),
                                 (wrb, "att_wr"), (wob, "att_wo")]:
                    GD.go(buf[:], wn[src].rearrange("(k p) c -> p k c", p=128),
                          sem=S["sbw"])
                V.memset(ones2[:], 1.0)
                V.fence(S["sst"])
                A.wait(S["sst"])
                for kk in range(KT):
                    A.act(ewb[:, kk, :], ones2[:], AF.Copy,
                          scale=PB["ew"][:, kk:kk + 1])
                A.fence(S["sst"])
                for e in (V, A, G, P):
                    e.wait(S["sbw"])

                chunks = [(b, ci) for ci in range(NCA) for b in range(NB)]
                load_n = {}
                mix_n = {}
                snr_n = {}
                rwk_n = {}

                def a1_load(idx):
                    b, ci = chunks[idx]
                    par = b
                    t0 = ci * CHA
                    wait_free(SY, ("xa", par))
                    SY.go(xa[par][:], X[b, t0:t0 + CHA, :]
                          .rearrange("(n p) c -> p n c", p=128), sem=S["sx"])
                    load_n[idx] = S["sx"].n

                def a1_front(idx):
                    b, ci = chunks[idx]
                    par = b
                    A.wait(S["sx"], load_n[idx])
                    G.wait(S["sx"], load_n[idx])
                    ln_stats_fused(xa[par], NTA, par)
                    eng = V if par == 0 else A
                    eng.wait(S["srd"])
                    for n in range(NTA):
                        if eng is A:
                            A.act(xa[par][:, n, :], xa[par][:, n, :],
                                  AF.Identity,
                                  scale=RS["sR2"][:, n, par:par + 1],
                                  bias=RS["sM2n"][:, n, par:par + 1])
                        else:
                            V.ts(xa[par][:, n, :], xa[par][:, n, :],
                                 RS["sR2"][:, n, par:par + 1], ALU.mult,
                                 RS["sM2n"][:, n, par:par + 1], ALU.add)
                    eng.fence(S["snr"])
                    snr_n[idx] = S["snr"].n
                    wait_free(G, ("hchcol", par))
                    if ci == 0:
                        G.memset(hch[par][:, :, 0:1], 0.0)
                    else:
                        G.copy(hch[par][:, :, 0:1],
                               hch[par][:, :, CHA:CHA + 1])
                    G.fence(S["ssq"])
                    P.wait(S["snr"], snr_n[idx])
                    P.wait(S["ssq"])
                    wait_free(A, ("hch", par))
                    transpose_affine(xa[par], hch[par], ("hch", par),
                                     PB["lag"], PB["lab"])
                    A.fence(S["saf"])
                    h1 = hch[par][:, :, 1:CHA + 1]
                    h0 = hch[par][:, :, 0:CHA]
                    V.wait(S["saf"])
                    wait_free(V, ("dbf",))
                    V.tt(dbf[:], h1, h0, ALU.subtract)
                    for m, mixp in [("k", "mk"), ("v", "mv"), ("r", "mr")]:
                        wait_free(V, ("mix" + m, par))
                        for kk in range(KT):
                            V.stt(mixb[m][par][:, kk, :], dbf[:, kk, :],
                                  PB[mixp][:, kk:kk + 1], h0[:, kk, :],
                                  ALU.mult, ALU.add)
                    V.fence(S["smx"])
                    mix_n[idx] = S["smx"].n
                    consumed(("dbf",), S["smx"])
                    consumed(("hchcol", par), S["smx"])
                    consumed(("hch", par), S["smx"])

                def a1_back_kvr(idx):
                    b, ci = chunks[idx]
                    par = b
                    if ci == 0:
                        V.memset(carN[:, b, :], 0.0)
                        V.memset(carQ[:, b, :], 0.0)
                    eKb, erb, vvb = eKb2[par], erb2[par], vvb2[par]
                    ekvb, NNb, QQb = ekvb2[par], NNb2[par], QQb2[par]
                    P.wait(S["smx"], mix_n[idx])
                    ev_n = {}
                    v1_n = {}
                    g2_n = {}
                    a3_n = {}

                    def a1_mm(j):
                        for gi, m in enumerate(("k", "v", "r")):
                            bank = (j * 3 + gi) % 6
                            bank = bank if bank < 4 else bank + 2
                            bank_wait(bank)
                            wsb = {"k": wkb, "v": wvb, "r": wrb}[m]
                            for kk in range(KT):
                                ii = P.mm(psum[bank][:, 0:CHA],
                                          wsb[:, kk, j * 128:(j + 1) * 128],
                                          mixb[m][par][:, kk, :],
                                          start=(kk == 0), stop=(kk == KT - 1))
                            S["smm"].bump(ii, 1)

                    def a1_ev(j):
                        jp = j % 3
                        A.wait(S["smm"])
                        wait_free(A, ("wkv", par, jp))
                        bks = []
                        for gi in range(3):
                            bank = (j * 3 + gi) % 6
                            bks.append(bank if bank < 4 else bank + 2)
                        A.act(eKb[jp][:], psum[bks[0]][:, 0:CHA], AF.Exp)
                        A.act(vvb[jp][:], psum[bks[1]][:, 0:CHA], AF.Copy)
                        ia = A.act(erb[jp][:], psum[bks[2]][:, 0:CHA],
                                   AF.Exp, scale=-1.0)
                        S["sek"].bump(ia, 1)
                        ev_n[j] = S["sek"].n
                        for bank in bks:
                            bank_set(bank, S["sek"])

                    def a1_g1(j):
                        jp = j % 3
                        G.wait(S["sek"], ev_n[j])
                        ig = G.tt(ekvb[jp][:], eKb[jp][:], vvb[jp][:], ALU.mult)
                        S["ssc"].bump(ig, 1)

                    def a1_v1(j):
                        jp = j % 3
                        V.wait(S["ssc"])
                        V.copy(NNb[jp][:, 0:1], carN[:, b, j:j + 1])
                        V.copy(QQb[jp][:, 0:1], carQ[:, b, j:j + 1])
                        V.scan(NNb[jp][:, 1:CHA + 1], ewb[:, j, :],
                               ekvb[jp][:], carN[:, b, j:j + 1])
                        V.scan(QQb[jp][:, 1:CHA + 1], ewb[:, j, :],
                               eKb[jp][:], carQ[:, b, j:j + 1])
                        V.copy(carN[:, b, j:j + 1], NNb[jp][:, CHA:CHA + 1])
                        V.copy(carQ[:, b, j:j + 1], QQb[jp][:, CHA:CHA + 1])
                        V.stt(ekvb[jp][:], ekvb[jp][:], PB["eu"][:, j:j + 1],
                              NNb[jp][:, 0:CHA], ALU.mult, ALU.add)
                        iv = V.stt(eKb[jp][:], eKb[jp][:],
                                   PB["eu"][:, j:j + 1],
                                   QQb[jp][:, 0:CHA], ALU.mult, ALU.add)
                        S["sdv"].bump(iv, 1)
                        v1_n[j] = S["sdv"].n

                    def a1_gd(j):
                        jp = j % 3
                        G.wait(S["sdv"], v1_n[j])
                        G.ts(erb[jp][:], erb[jp][:], 1.0, ALU.add)
                        ig = G.tt(eKb[jp][:], eKb[jp][:], erb[jp][:], ALU.mult)
                        S["ssc"].bump(ig, 1)

                    def a1_ar(j):
                        jp = j % 3
                        A.wait(S["ssc"])
                        A.act(eKb[jp][:], eKb[jp][:], AF.Ln)
                        ia = A.act(eKb[jp][:], eKb[jp][:], AF.Exp, scale=-1.0)
                        S["sek"].bump(ia, 1)
                        a3_n[j] = S["sek"].n

                    def a1_v2(j):
                        jp = j % 3
                        V.wait(S["sek"], a3_n[j])
                        if j == 0:
                            wait_free(V, ("rwk", par))
                        iv = V.tt(rwk[par][:, j, :], ekvb[jp][:], eKb[jp][:],
                                  ALU.mult)
                        S["sdv"].bump(iv, 1)
                        consumed(("wkv", par, jp), S["sdv"])

                    for j in range(KT):
                        a1_mm(j)
                        a1_ev(j)
                        a1_g1(j)
                        if j >= 1:
                            a1_gd(j - 1)
                        a1_v1(j)
                        if j >= 1:
                            a1_ar(j - 1)
                        if j >= 2:
                            a1_v2(j - 2)
                    a1_gd(KT - 1)
                    a1_ar(KT - 1)
                    a1_v2(KT - 2)
                    a1_v2(KT - 1)
                    rwk_n[idx] = S["sdv"].n
                    consumed(("mixk", par), S["smm"])
                    consumed(("mixv", par), S["smm"])
                    consumed(("mixr", par), S["smm"])

                def a1_back_wo(idx):
                    b, ci = chunks[idx]
                    par = b
                    t0 = ci * CHA
                    P.wait(S["sdv"], rwk_n[idx])
                    wait_free(A, ("attc",))
                    for co in range(KT):
                        bank = co % 2
                        bank_wait(bank)
                        for kk in range(KT):
                            ii = P.mm(psum[bank][:, 0:CHA],
                                      wob[:, kk, co * 128:(co + 1) * 128],
                                      rwk[par][:, kk, :],
                                      start=(kk == 0), stop=(kk == KT - 1))
                        S["swo"].bump(ii, 1)
                        A.wait(S["swo"])
                        ia = A.act(attc[:, co, :], psum[bank][:, 0:CHA], AF.Copy)
                        S["sek"].bump(ia, 1)
                        bank_set(bank, S["sek"])
                    consumed(("rwk", par), S["swo"])
                    P.wait(S["sek"])
                    V.wait(S["snr"], snr_n[idx])
                    for n in range(NTA):
                        bank_wait(4)
                        bank_wait(5)
                        for kk in range(KT):
                            bank = 4 + kk // 4
                            ii = P.tr(psum[bank][:, (kk % 4) * 128:(kk % 4 + 1) * 128],
                                      attc[:, kk, n * 128:(n + 1) * 128], idn[:])
                            if kk == 7:
                                S["str_"].bump(ii, 1)
                        V.wait(S["str_"])
                        V.stt(xa[par][:, n, 0:512], xa[par][:, n, 0:512],
                              RS["sIvr"][:, n, par:par + 1],
                              psum[4][:, 0:512], ALU.mult, ALU.add)
                        V.stt(xa[par][:, n, 512:1024], xa[par][:, n, 512:1024],
                              RS["sIvr"][:, n, par:par + 1],
                              psum[5][:, 0:512], ALU.mult, ALU.add)
                        V.fence(S["sad"])
                        bank_set(4, S["sad"])
                        bank_set(5, S["sad"])
                    consumed(("attc",), S["str_"])
                    consumed(("stats", par), S["sad"])
                    SY.wait(S["sad"])
                    SY.go(S_XPP[b, t0:t0 + CHA, :]
                          .rearrange("(n p) c -> p n c", p=128),
                          xa[par][:], sem=S["sou"])
                    consumed(("xa", par), S["sou"])

                a1_load(0)
                a1_load(1)
                a1_front(0)
                a1_back_kvr(0)
                for idx in range(1, len(chunks)):
                    a1_front(idx)
                    a1_back_wo(idx - 1)
                    if idx + 1 < len(chunks):
                        a1_load(idx + 1)
                    a1_back_kvr(idx)
                a1_back_wo(len(chunks) - 1)
            barrier()

        if "A2a" in phases:
            sc_A2a = st.enter_context(nc.named_scope("PH_A2a"))
            with contextlib.ExitStack() as sa:
                EA = sa.enter_context
                fwkb = EA(nc.sbuf_tensor("fwkb", [128, KT, DFF], BF16))
                fwrb = EA(nc.sbuf_tensor("fwrb", [128, KT, D], BF16))
                xa2 = [EA(nc.sbuf_tensor(f"xa2_{i}", [128, NTA, D], F32))
                       for i in range(2)]
                h2pre = EA(nc.sbuf_tensor("h2pre", [128, NTA, D], F32))
                h2ch = [EA(nc.sbuf_tensor(f"h2ch{i}", [128, KT, CHA + 1], F32))
                        for i in range(2)]
                dbf2 = EA(nc.sbuf_tensor("dbf2", [128, KT, CHA], BF16))
                mfk = [EA(nc.sbuf_tensor(f"mfk{i}", [128, KT, CHA], BF16))
                       for i in range(2)]
                mfr = [EA(nc.sbuf_tensor(f"mfr{i}", [128, KT, CHA], BF16))
                       for i in range(2)]
                kfb = [EA(nc.sbuf_tensor(f"kfb{i}", [128, JT, CHA], BF16))
                       for i in range(2)]
                trel = EA(nc.sbuf_tensor("trel", [128, 4, CHA], F32))
                sr2b = [EA(nc.sbuf_tensor(f"sr2b{i}", [128, KT, CHA], F32))
                        for i in range(2)]

                GD.go(fwkb[:], wn["ffn_wk"].rearrange("(k p) c -> p k c", p=128),
                      sem=S["sbw"])
                GD.go(fwrb[:], wn["ffn_wr"].rearrange("(k p) c -> p k c", p=128),
                      sem=S["sbw"])
                for e in (V, A, G, P):
                    e.wait(S["sbw"])

                chunks = [(b, ci) for ci in range(NCA) for b in range(NB)]
                load_n = {}
                mix_n = {}

                def a2a_load(idx):
                    b, ci = chunks[idx]
                    par = b
                    t0 = ci * CHA
                    wait_free(SY, ("xa2", par))
                    SY.go(xa2[par][:], S_XPP[b, t0:t0 + CHA, :]
                          .rearrange("(n p) c -> p n c", p=128), sem=S["sx"])
                    load_n[idx] = S["sx"].n

                def a2a_front(idx):
                    b, ci = chunks[idx]
                    par = b
                    A.wait(S["sx"], load_n[idx])
                    G.wait(S["sx"], load_n[idx])
                    ln_stats(xa2[par], NTA, par)
                    eng = V if par == 0 else A
                    wait_free(eng, ("h2pre",))
                    ln_norm(h2pre, xa2[par], NTA, par, on=eng)
                    consumed(("xa2", par), S["snr"])
                    wait_free(G, ("h2col", par))
                    if ci == 0:
                        G.memset(h2ch[par][:, :, 0:1], 0.0)
                    else:
                        G.copy(h2ch[par][:, :, 0:1],
                               h2ch[par][:, :, CHA:CHA + 1])
                    G.fence(S["ssq"])
                    P.wait(S["snr"])
                    P.wait(S["ssq"])
                    wait_free(A, ("h2ch", par))
                    transpose_affine(h2pre, h2ch[par], ("h2ch", par),
                                     PB["lfg"], PB["lfb"])
                    A.fence(S["saf"])
                    consumed(("h2pre",), S["saf"])
                    h1 = h2ch[par][:, :, 1:CHA + 1]
                    h0 = h2ch[par][:, :, 0:CHA]
                    V.wait(S["saf"])
                    wait_free(V, ("dbf2",))
                    V.tt(dbf2[:], h1, h0, ALU.subtract)
                    wait_free(V, ("mfk", par))
                    for kk in range(KT):
                        V.stt(mfk[par][:, kk, :], dbf2[:, kk, :],
                              PB["fmk"][:, kk:kk + 1], h0[:, kk, :],
                              ALU.mult, ALU.add)
                    wait_free(V, ("mfr", par))
                    for kk in range(KT):
                        V.stt(mfr[par][:, kk, :], dbf2[:, kk, :],
                              PB["fmr"][:, kk:kk + 1], h0[:, kk, :],
                              ALU.mult, ALU.add)
                    V.fence(S["smx"])
                    mix_n[idx] = S["smx"].n
                    consumed(("dbf2",), S["smx"])
                    consumed(("h2col", par), S["smx"])
                    consumed(("h2ch", par), S["smx"])

                def a2a_back_kf(idx):
                    b, ci = chunks[idx]
                    par = b
                    P.wait(S["smx"], mix_n[idx])
                    wait_free(A, ("kfb", par))
                    wait_free(V, ("kfb", par))
                    for j in range(JT):
                        bank = j % 2
                        bank_wait(bank)
                        for kk in range(KT):
                            ii = P.mm(psum[bank][:, 0:CHA],
                                      fwkb[:, kk, j * 128:(j + 1) * 128],
                                      mfk[par][:, kk, :],
                                      start=(kk == 0), stop=(kk == KT - 1))
                        S["smm"].bump(ii, 1)
                        A.wait(S["smm"])
                        if j % 4 == 0:
                            A.wait(S["sdv"])
                        ia = A.act(trel[:, j % 4, :], psum[bank][:, 0:CHA],
                                   AF.Relu)
                        S["sek"].bump(ia, 1)
                        bank_set(bank, S["sek"])
                        V.wait(S["sek"])
                        iv = V.tt(kfb[par][:, j, :], trel[:, j % 4, :],
                                  trel[:, j % 4, :], ALU.mult)
                        if j % 4 == 3 or j == JT - 1:
                            S["sdv"].bump(iv, 1)
                    consumed(("mfk", par), S["smm"])

                def a2a_back_fr(idx):
                    b, ci = chunks[idx]
                    par = b
                    t0 = ci * CHA
                    tg = b * L + t0
                    wait_free(A, ("sr2b", par))
                    for co in range(KT):
                        bank = 2 + co % 2
                        bank_wait(bank)
                        for kk in range(KT):
                            ii = P.mm(psum[bank][:, 0:CHA],
                                      fwrb[:, kk, co * 128:(co + 1) * 128],
                                      mfr[par][:, kk, :],
                                      start=(kk == 0), stop=(kk == KT - 1))
                        S["smm"].bump(ii, 1)
                        A.wait(S["smm"])
                        ia = A.act(sr2b[par][:, co, :], psum[bank][:, 0:CHA],
                                   AF.Exp, scale=-1.0)
                        S["sek"].bump(ia, 1)
                        bank_set(bank, S["sek"])
                    consumed(("mfr", par), S["smm"])
                    A.act(sr2b[par][:], sr2b[par][:], AF.Ln, bias=1.0)
                    A.act(sr2b[par][:], sr2b[par][:], AF.Exp, scale=-1.0)
                    A.fence(S["sek"])
                    SY.wait(S["sek"])
                    SY.wait(S["sdv"])
                    SY.go(kf_r[:, :, tg:tg + CHA], kfb[par][:], sem=S["sou"])
                    SY.go(sr2_r[:, :, tg:tg + CHA], sr2b[par][:], sem=S["sou"])
                    consumed(("kfb", par), S["sou"])
                    consumed(("sr2b", par), S["sou"])

                a2a_load(0)
                a2a_load(1)
                a2a_front(0)
                for idx in range(len(chunks)):
                    a2a_back_kf(idx)
                    if idx + 1 < len(chunks):
                        a2a_front(idx + 1)
                    a2a_back_fr(idx)
                    if idx + 2 < len(chunks):
                        a2a_load(idx + 2)
            barrier()

        if "A2b" in phases:
            sc_A2b = st.enter_context(nc.named_scope("PH_A2b"))
            with contextlib.ExitStack() as sa:
                EA = sa.enter_context
                fwvb = EA(nc.sbuf_tensor("fwvb", [128, JT, D], BF16))
                kfw = [EA(nc.sbuf_tensor(f"kfw{i}", [128, JT, CHA], BF16))
                       for i in range(2)]
                sr2w = [EA(nc.sbuf_tensor(f"sr2w{i}", [128, KT, CHA], F32))
                        for i in range(2)]
                xa3 = [EA(nc.sbuf_tensor(f"xa3_{i}", [128, NTA, D], F32))
                       for i in range(2)]
                xr = [EA(nc.sbuf_tensor(f"xr{i}", [128, NTA, D], F32))
                      for i in range(2)]
                ffc = [EA(nc.sbuf_tensor(f"ffc{i}", [128, KT, CHA], F32))
                       for i in range(2)]
                lnb = [EA(nc.sbuf_tensor(f"lnb{i}", [128, NTA, D], F32))
                       for i in range(2)]
                gbuf = [EA(nc.sbuf_tensor(f"gbuf{i}", [128, NTA, D], F32))
                        for i in range(2)]
                gbfo = [EA(nc.sbuf_tensor("gbfo0", [128, KT, CHA], BF16))] * 2

                GD.go(fwvb[:], wn["ffn_wv"].rearrange("(j p) c -> p j c", p=128),
                      sem=S["sbw"])
                for e in (V, A, G, P):
                    e.wait(S["sbw"])

                chunks = [(b, ci) for b in range(NB) for ci in range(NCA)]
                load_n = {}
                ffc_n = {}

                def a2b_load(idx):
                    b, ci = chunks[idx]
                    par = idx % 2
                    t0 = ci * CHA
                    tg = b * L + t0
                    wait_free(SY, ("kfw", par))
                    SY.go(kfw[par][:], kf_r[:, :, tg:tg + CHA], sem=S["sx"])
                    wait_free(SY, ("sr2w", par))
                    SY.go(sr2w[par][:], sr2_r[:, :, tg:tg + CHA], sem=S["sx"])
                    wait_free(SY, ("xa3", par))
                    SY.go(xa3[par][:], S_XPP[b, t0:t0 + CHA, :]
                          .rearrange("(n p) c -> p n c", p=128), sem=S["sx"])
                    wait_free(SY, ("xr", par))
                    SY.go(xr[par][:], X[b, t0:t0 + CHA, :]
                          .rearrange("(n p) c -> p n c", p=128), sem=S["sx"])
                    load_n[idx] = S["sx"].n

                def a2b_back1(idx):
                    b, ci = chunks[idx]
                    par = idx % 2
                    P.wait(S["sx"], load_n[idx])
                    V.wait(S["sx"], load_n[idx])
                    wait_free(V, ("ffc", par))
                    for co in range(KT):
                        bank = co % 2
                        bank_wait(bank)
                        for j in range(JT):
                            ii = P.mm(psum[bank][:, 0:CHA],
                                      fwvb[:, j, co * 128:(co + 1) * 128],
                                      kfw[par][:, j, :],
                                      start=(j == 0), stop=(j == JT - 1))
                        S["smm"].bump(ii, 1)
                        V.wait(S["smm"])
                        iv = V.tt(ffc[par][:, co, :], sr2w[par][:, co, :],
                                  psum[bank][:, 0:CHA], ALU.mult)
                        S["sdv"].bump(iv, 1)
                        bank_set(bank, S["sdv"])
                    V.fence(S["sdv"])
                    ffc_n[idx] = S["sdv"].n
                    consumed(("kfw", par), S["smm"])
                    consumed(("sr2w", par), S["sdv"])

                def a2b_back2a(idx):
                    b, ci = chunks[idx]
                    par = idx % 2
                    t0 = ci * CHA
                    sb = par * 16
                    P.wait(S["sdv"], ffc_n[idx])
                    transpose_add(ffc[par], xa3[par], NTA)
                    consumed(("ffc", par), S["str_"])
                    for e in (A, G):
                        e.wait(S["sad"])
                    ln_stats(xa3[par], NTA, par)
                    wait_free(V, ("lnb", par))
                    ln_norm(lnb[par], xa3[par], NTA, par)
                    consumed(("xa3", par), S["snr"])
                    V.wait(S["sx"], load_n[idx])
                    V.tt(lnb[par][:], lnb[par][:], xr[par][:], ALU.add)
                    V.fence(S["snr"])
                    consumed(("xr", par), S["snr"])
                    SY.wait(S["snr"])
                    SY.go(S_Y[b, t0:t0 + CHA, :]
                          .rearrange("(n p) c -> p n c", p=128), lnb[par][:],
                          sem=S["sou"])
                    for e in (A, G):
                        e.wait(S["snr"])
                    ln_stats(lnb[par], NTA, par)

                def a2b_back2b(idx):
                    b, ci = chunks[idx]
                    par = idx % 2
                    t0 = ci * CHA
                    tg = b * L + t0
                    cg = b * NCA + ci
                    sb = par * 16
                    wait_free(V, ("gbuf", par))
                    ln_norm(gbuf[par], lnb[par], NTA, par)
                    consumed(("lnb", par), S["snr"])
                    SY.wait(S["snr"])
                    SY.go(S_G32[b, t0:t0 + CHA, :]
                          .rearrange("(n p) c -> p n c", p=128),
                          gbuf[par][:], sem=S["sou"])
                    P.wait(S["snr"])
                    wait_free(A, ("gbfo", 0))
                    transpose_affine(gbuf[par], gbfo[par], ("gbfo", 0),
                                     None, None,
                                     accum_slot=lambda kk, _cg=cg:
                                     pacc[:, kk, _cg:_cg + 1])
                    A.fence(S["saf"])
                    consumed(("gbuf", par), S["saf"])
                    SY.wait(S["saf"])
                    SY.go(gbf_r[:, :, tg:tg + CHA], gbfo[par][:], sem=S["sou"])
                    consumed(("gbfo", 0), S["sou"])

                a2b_load(0)
                a2b_load(1)
                a2b_back1(0)
                for idx in range(1, len(chunks)):
                    a2b_back1(idx)
                    if idx >= 2:
                        a2b_back2b(idx - 2)
                    a2b_back2a(idx - 1)
                    if idx + 1 < len(chunks):
                        a2b_load(idx + 1)
                a2b_back2b(len(chunks) - 2)
                a2b_back2a(len(chunks) - 1)
                a2b_back2b(len(chunks) - 1)
            barrier()

        if "R" in phases:
            sc_R = st.enter_context(nc.named_scope("PH_R"))
            with contextlib.ExitStack() as sa:
                EA = sa.enter_context
                W1s = EA(nc.sbuf_tensor("W1s", [128, KT, 256], F32))
                W2s = EA(nc.sbuf_tensor("W2s", [128, 2, NZ], F32))
                b1bc = EA(nc.sbuf_tensor("b1bc", [2, 256], F32))
                b2bc = EA(nc.sbuf_tensor("b2bc", [2, NZ], F32))
                pl2 = EA(nc.sbuf_tensor("pl2", [128, KT, 2], F32))
                hrt = EA(nc.sbuf_tensor("hrt", [2, 256], F32))
                hrtT = EA(nc.sbuf_tensor("hrtT", [128, 2, 2], F32))
                rsc = EA(nc.sbuf_tensor("rsc", [2, 64], F32))
                wzrow = EA(nc.sbuf_tensor("wzrow", [1, 16], F32))

                SY.go(W1s[:], wn["W1"].rearrange("(k p) h -> p k h", p=128),
                      sem=S["srt"])
                SY.go(W2s[:], wn["W2"].rearrange("(k p) z -> p k z", p=128),
                      sem=S["srt"])
                for prt in range(2):
                    SY.go(b1bc[prt:prt + 1, :], wn["b1"].rearrange("(o h) -> o h", o=1),
                          sem=S["srt"])
                    SY.go(b2bc[prt:prt + 1, :], wn["b2"].rearrange("(o z) -> o z", o=1),
                          sem=S["srt"])
                for e in (V, A, G, P):
                    e.wait(S["srt"])
                for b in range(NB):
                    V.red(pl2[:, :, b:b + 1],
                          pacc[:, :, b * NCA:(b + 1) * NCA], ALU.add)
                V.ts(pl2[:], pl2[:], 1.0 / L, ALU.mult)
                V.fence(S["sst"])
                P.wait(S["sst"])
                bank_wait(6)
                for kk in range(KT):
                    ii = P.mm(psum[6][0:2, 0:256], pl2[:, kk, :], W1s[:, kk, :],
                              start=(kk == 0), stop=(kk == KT - 1))
                S["smm"].bump(ii, 1)
                V.wait(S["smm"])
                V.tt(hrt[:], psum[6][0:2, 0:256], b1bc[:], ALU.add)
                V.fence(S["sdv"])
                bank_set(6, S["sdv"])
                A.wait(S["sdv"])
                A.act(hrt[:], hrt[:], AF.Tanh)
                A.fence(S["sek"])
                P.wait(S["sek"])
                bank_wait(4)
                for kk in range(2):
                    ii = P.tr(psum[4][:, kk * 2:kk * 2 + 2],
                              hrt[0:2, kk * 128:(kk + 1) * 128], idn[0:2, 0:2])
                S["str_"].bump(ii, 1)
                A.wait(S["str_"])
                A.act(hrtT[:, 0, :], psum[4][:, 0:2], AF.Copy)
                A.act(hrtT[:, 1, :], psum[4][:, 2:4], AF.Copy)
                A.fence(S["sek"])
                bank_set(4, S["sek"])
                P.wait(S["sek"])
                for kk in range(2):
                    ii = P.mm(psum[6][0:2, 256:256 + NZ], hrtT[:, kk, :],
                              W2s[:, kk, :], start=(kk == 0), stop=(kk == 1))
                S["smm"].bump(ii, 1)
                V.wait(S["smm"])
                lg = rsc[:, 0:8]
                V.tt(lg, psum[6][0:2, 256:256 + NZ], b2bc[:], ALU.add)
                V.red(rsc[:, 8:9], lg, ALU.max)
                V.ts(rsc[:, 8:9], rsc[:, 8:9], -1.0, ALU.mult)
                V.fence(S["sdv"])
                bank_set(6, S["sdv"])
                A.wait(S["sdv"])
                A.act(lg, lg, AF.Exp, bias=rsc[:, 8:9])
                A.fence(S["sek"])
                V.wait(S["sek"])
                V.red(rsc[:, 9:10], lg, ALU.add)
                V.recip(rsc[:, 9:10], rsc[:, 9:10])
                V.ts(lg, lg, rsc[:, 9:10], ALU.mult)
                V.red(rsc[:, 18:19], lg, ALU.max)
                V.ts(rsc[:, 40:48], lg, rsc[:, 18:19], ALU.is_ge)
                V.stt(rsc[:, 10:18], rsc[:, 40:48], -1e30, lg, ALU.mult, ALU.add)
                V.red(rsc[:, 19:20], rsc[:, 10:18], ALU.max)
                V.ts(rsc[:, 48:56], rsc[:, 10:18], rsc[:, 19:20], ALU.is_ge)
                V.stt(rsc[:, 26:34], rsc[:, 48:56], -1e30, rsc[:, 10:18],
                      ALU.mult, ALU.add)
                V.red(rsc[:, 20:21], rsc[:, 26:34], ALU.max)
                V.ts(rsc[:, 56:64], rsc[:, 26:34], rsc[:, 20:21], ALU.is_ge)
                V.tt(rsc[:, 35:36], rsc[:, 18:19], rsc[:, 19:20], ALU.add)
                V.tt(rsc[:, 35:36], rsc[:, 35:36], rsc[:, 20:21], ALU.add)
                V.recip(rsc[:, 36:37], rsc[:, 35:36])
                for r_ in range(3):
                    V.ts(rsc[:, 24 + r_:25 + r_], rsc[:, 18 + r_:19 + r_],
                         rsc[:, 36:37], ALU.mult)
                for r_ in range(3):
                    V.tt(rsc[:, 10:18], rsc[:, 40 + 8 * r_:48 + 8 * r_],
                         iota8f[:], ALU.mult)
                    V.red(rsc[:, 21 + r_:22 + r_], rsc[:, 10:18], ALU.add)
                V.fence(S["sdv"])
                SY.wait(S["sdv"])
                SY.go(wzrow[0:1, 0:3], rsc[0:1, 24:27], sem=S["srt"])
                SY.go(wzrow[0:1, 3:6], rsc[1:2, 24:27], sem=S["srt"])
                SY.go(wzrow[0:1, 8:11], rsc[0:1, 21:24], sem=S["srt"])
                SY.go(wzrow[0:1, 11:14], rsc[1:2, 21:24], sem=S["srt"])
                P.wait(S["srt"])
                bank_wait(6)
                for be in list(range(6)) + list(range(8, 14)):
                    ii = P.mm(psum[6][:, be:be + 1], onesr[0:1, :],
                              wzrow[0:1, be:be + 1], start=True, stop=True)
                S["smm"].bump(ii, 1)
                A.wait(S["smm"])
                A.act(wzbc[:], psum[6][:, 0:16], AF.Copy)
                A.fence(S["sek"])
                bank_set(6, S["sek"])
                G.wait(S["sek"])
                for ps_ in range(6):
                    ecol = wzbc[:, 8 + ps_:9 + ps_]
                    G.ts(rsF[:, 0:1], ecol, 1024.0, ALU.mult)
                    G.ts(idxTf[:, 0:8], jjoff_f[:, 0:8], rsF[:, 0:1], ALU.add)
                    G.copy(idxAi[:, 8 * ps_:8 * (ps_ + 1)], idxTf[:, 0:8])
                    G.ts(rsF[:, 0:1], ecol, 4096.0, ALU.mult)
                    G.ts(idxTf[:], jjoff_f[:], rsF[:, 0:1], ALU.add)
                    G.copy(idxBi[:, 32 * ps_:32 * (ps_ + 1)], idxTf[:])
                G.fence(S["ssc"])
            barrier()

        if "B" in phases:
            sc_B = st.enter_context(nc.named_scope("PH_B"))
            with contextlib.ExitStack() as sa:
                EA = sa.enter_context
                wab = EA(nc.sbuf_tensor("wab", [128, KT, DFF], BF16))
                wbb = EA(nc.sbuf_tensor("wbb", [128, JT, D], BF16))
                gw = [EA(nc.sbuf_tensor(f"gw{i}", [128, KT, TB], BF16))
                      for i in range(2)]
                zb = EA(nc.sbuf_tensor("zb", [128, JT, TB], BF16))
                zcb = EA(nc.sbuf_tensor("zcb", [128, KT, TB], F32))

                WaF = wn["Wa"].rearrange("e r c -> (e r) c")
                WbF = wn["Wb"].rearrange("e r c -> (e r) c")
                NPASS = NB * 3
                NCBP = L // TB

                def load_wa(p):
                    for kk in range(KT):
                        gi = GD.e.indirect_dma_start(
                            wab[:, kk, :], None, WaF,
                            IndirectOffsetOnAxis(
                                idxAi[:, 8 * p + kk:8 * p + kk + 1], axis=0))
                        S["sbw"].bump(gi, 16)
                    return S["sbw"].n

                def load_wb(p):
                    for jj in range(JT):
                        gi = GD.e.indirect_dma_start(
                            wbb[:, jj, :], None, WbF,
                            IndirectOffsetOnAxis(
                                idxBi[:, 32 * p + jj:32 * p + jj + 1], axis=0))
                        S["sbw"].bump(gi, 16)
                    return S["sbw"].n

                wa_ready = load_wa(0)
                wb_ready = load_wb(0)

                m2_done = None
                for p in range(NPASS):
                    b = p // 3
                    r = p % 3
                    for ch in range(NCBP):
                        tg = b * L + ch * TB
                        par = ch % 2
                        wait_free(SY, ("gw", par))
                        SY.go(gw[par][:], gbf_r[:, :, tg:tg + TB], sem=S["sbg"])
                        P.wait(S["sbg"])
                        if ch == 0:
                            P.wait(S["sbw"], wa_ready)
                        if m2_done is not None:
                            A.e.wait_ge(m2_done[0].h, m2_done[1])
                        for j in range(JT):
                            bank = j % 2
                            bank_wait(bank)
                            for kk in range(KT):
                                ii = P.mm(psum[bank][:, 0:TB],
                                          wab[:, kk, j * 128:(j + 1) * 128],
                                          gw[par][:, kk, :],
                                          start=(kk == 0), stop=(kk == KT - 1))
                            S["sbm1"].bump(ii, 1)
                            A.wait(S["sbm1"])
                            ia = A.act(zb[:, j, :], psum[bank][:, 0:TB],
                                       AF.Gelu_apprx_tanh)
                            S["sbz"].bump(ia, 1)
                            bank_set(bank, S["sbz"])
                        consumed(("gw", par), S["sbm1"])
                        if ch == NCBP - 1 and p < NPASS - 1:
                            GD.wait(S["sbm1"])
                            wa_ready = load_wa(p + 1)
                        P.wait(S["sbz"])
                        if p == 0 and ch == 0:
                            P.wait(S["sbw"], wb_ready)
                        for cog in range(4):
                            bk0 = 2 + 2 * (cog % 2)
                            bk1 = 3 + 2 * (cog % 2)
                            bank_wait(bk0)
                            bank_wait(bk1)
                            for j in range(JT):
                                P.mm(psum[bk0][:, 0:TB],
                                     wbb[:, j, (2 * cog) * 128:(2 * cog + 1) * 128],
                                     zb[:, j, :], start=(j == 0), stop=(j == JT - 1))
                                ii = P.mm(psum[bk1][:, 0:TB],
                                          wbb[:, j,
                                              (2 * cog + 1) * 128:(2 * cog + 2) * 128],
                                          zb[:, j, :], start=(j == 0),
                                          stop=(j == JT - 1))
                            S["sbm2"].bump(ii, 1)
                            V.wait(S["sbm2"])
                            wait_free(V, ("zcb", cog))
                            V.ts(zcb[:, 2 * cog, :], psum[bk0][:, 0:TB],
                                 wzbc[:, p:p + 1], ALU.mult)
                            V.ts(zcb[:, 2 * cog + 1, :], psum[bk1][:, 0:TB],
                                 wzbc[:, p:p + 1], ALU.mult)
                            V.fence(S["sbev"])
                            bank_set(bk0, S["sbev"])
                            bank_set(bk1, S["sbev"])
                        m2_done = (S["sbm2"], S["sbm2"].n)
                        GD.wait(S["sbev"])
                        if r > 0:
                            GD.go(zac_r[:, :, tg:tg + TB], zcb[:],
                                  sem=S["sbac"], accum_op=ALU.add)
                        else:
                            GD.go(zac_r[:, :, tg:tg + TB], zcb[:], sem=S["sbac"])
                        for cog in range(4):
                            consumed(("zcb", cog), S["sbac"])
                    if p < NPASS - 1:
                        GD.wait(S["sbm2"])
                        wb_ready = load_wb(p + 1)
            barrier()

        if "C" in phases:
            sc_C = st.enter_context(nc.named_scope("PH_C"))
            with contextlib.ExitStack() as sa:
                EA = sa.enter_context
                zw = [EA(nc.sbuf_tensor(f"zw{i}", [128, KT, TC], F32))
                      for i in range(2)]
                g32w = [EA(nc.sbuf_tensor(f"g32w{i}", [128, NTC, D], F32))
                        for i in range(2)]
                yw = [EA(nc.sbuf_tensor(f"yw{i}", [128, NTC, D], F32))
                      for i in range(2)]
                sqc = EA(nc.sbuf_tensor("sqc", [128, 1, D], F32))
                ow = [EA(nc.sbuf_tensor(f"ow{i}", [128, NTC, D], F32))
                      for i in range(2)]
                stc = EA(nc.sbuf_tensor("stc", [128, NTC, 32], F32))

                c_load_n = {}

                def c_load(ch):
                    tg = ch * TC
                    b = tg // L
                    t0 = tg % L
                    par = ch % 2
                    wait_free(SY, ("zw", par))
                    SY.go(zw[par][:], zac_r[:, :, tg:tg + TC], sem=S["sx"])
                    wait_free(SY, ("g32w", par))
                    SY.go(g32w[par][:], S_G32[b, t0:t0 + TC, :]
                          .rearrange("(n p) c -> p n c", p=128), sem=S["sx"])
                    wait_free(SY, ("yw", par))
                    SY.go(yw[par][:], S_Y[b, t0:t0 + TC, :]
                          .rearrange("(n p) c -> p n c", p=128), sem=S["sx"])
                    c_load_n[ch] = S["sx"].n

                c_load(0)
                if NCC > 1:
                    c_load(1)
                for ch in range(NCC):
                    tg = ch * TC
                    b = tg // L
                    t0 = tg % L
                    par = ch % 2
                    o = par * 16
                    P.wait(S["sx"], c_load_n[ch])
                    V.wait(S["sx"], c_load_n[ch])
                    transpose_add(zw[par], g32w[par], NTC)
                    consumed(("zw", par), S["sad"])
                    for e in (A, G):
                        e.wait(S["sad"])
                    wait_free(A, ("stc", o))
                    for n in range(NTC):
                        A.act(sqc[:, 0, :], g32w[par][:, n, :], AF.Square,
                              accum_out=stc[:, n, o + 1:o + 2])
                        A.act(sqc[:, 0, :], g32w[par][:, n, :], AF.Copy,
                              accum_out=stc[:, n, o + 0:o + 1])
                    A.fence(S["srd"])
                    G.wait(S["srd"])
                    c = 1.0 / D
                    scl = lambda i: stc[:, :, o + i:o + i + 1]
                    G.ts(scl(2), scl(0), c, ALU.mult)
                    G.tt(scl(3), scl(2), scl(2), ALU.mult)
                    G.ts(scl(4), scl(1), c, ALU.mult)
                    G.tt(scl(4), scl(4), scl(3), ALU.subtract)
                    G.fence(S["sst"])
                    A.wait(S["sst"])
                    A.act(scl(5), scl(4), AF.Ln, bias=EPS)
                    A.act(scl(7), scl(5), AF.Exp, scale=-0.5)
                    A.fence(S["sst"])
                    G.wait(S["sst"])
                    G.tt(scl(6), scl(2), scl(7), ALU.mult)
                    G.ts(scl(6), scl(6), -1.0, ALU.mult)
                    G.fence(S["srd"])
                    V.wait(S["srd"])
                    wait_free(V, ("ow", par))
                    for n in range(NTC):
                        V.ts(ow[par][:, n, :], g32w[par][:, n, :],
                             stc[:, n, o + 7:o + 8], ALU.mult,
                             stc[:, n, o + 6:o + 7], ALU.add)
                    V.tt(ow[par][:], ow[par][:], yw[par][:], ALU.add)
                    V.fence(S["snr"])
                    consumed(("stc", o), S["snr"])
                    consumed(("g32w", par), S["snr"])
                    consumed(("yw", par), S["snr"])
                    SY.wait(S["snr"])
                    SY.go(OUT[b, t0:t0 + TC, :]
                          .rearrange("(n p) c -> p n c", p=128),
                          ow[par][:], sem=S["sfin"])
                    consumed(("ow", par), S["sfin"])
                    n_out_dma[0] += 1
                    if ch + 2 < NCC:
                        c_load(ch + 2)
            SY.wait(S["sfin"], n_out_dma[0] * 16)
        else:
            SY.wait(S["sou"], S["sou"].n)

    return nc


def kernel(**inputs):
    xs = np.ascontiguousarray(np.asarray(inputs["L0_attn_x"], dtype=np.float32))
    base = {}
    for k, v in inputs.items():
        if k == "L0_attn_x":
            continue
        base[k] = np.ascontiguousarray(np.asarray(v, dtype=np.float32))
    base["identity"] = np.eye(128, dtype=np.float32)
    in_maps = []
    for c in range(8):
        m = dict(base)
        m["x"] = np.ascontiguousarray(xs[2 * c:2 * c + 2])
        in_maps.append(m)
    nc = build()
    res = run_bass_kernel_spmd(nc, in_maps, core_ids=list(range(8)))
    out = np.concatenate([np.asarray(r["out"], dtype=np.float32)
                          for r in res.results], axis=0)
    return out.reshape(16, L, D)


if __name__ == "__main__":
    nc = build()
    print("instructions:", sum(1 for _ in nc.all_instructions()))


# version 11
# speedup vs baseline: 1.3323x; 1.0998x over previous
import contextlib
import numpy as np
import concourse.bass as bass
import concourse.mybir as mybir
from concourse.bass_utils import run_bass_kernel_spmd
from concourse.bass import IndirectOffsetOnAxis

F32 = mybir.dt.float32
BF16 = mybir.dt.bfloat16
AF = mybir.ActivationFunctionType
ALU = mybir.AluOpType
AX = mybir.AxisListType

NB = 2
L = 2048
D = 1024
KT = 8
DFF = 4096
JT = 32
CHA = 256
NCA = L // CHA
NTA = CHA // 128
TB = 512
NCB = NB * L // TB
TC = 512
NCC = NB * L // TC
NTC = TC // 128
EPS = 1e-5
NZ = 8


def _names(args):
    s = set()
    for ap in args:
        if ap is None or isinstance(ap, (int, float)):
            continue
        t = getattr(ap, "tensor", None)
        n = getattr(t, "name", None)
        if n is not None:
            s.add(n)
    return s


class Sem:
    def __init__(self, nc, stack, name):
        self.h = stack.enter_context(nc.semaphore(name))
        self.n = 0

    def bump(self, instr, k=1):
        instr.then_inc(self.h, k)
        self.n += k
        return self.n


class Eng:

    def __init__(self, eng):
        self.e = eng
        self.dirty = set()

    def wait(self, sem, n=None):
        self.e.wait_ge(sem.h, sem.n if n is None else n)

    def drain(self):
        self.e.drain()
        self.dirty = set()

    def inc(self, sem, k=1):
        self.e.sem_inc(sem.h, k)
        sem.n += k

    def fence(self, sem):
        self.drain()
        self.inc(sem)

    def _pre(self, outs, ins):
        if (_names(outs) | _names(ins)) & self.dirty:
            self.e.drain()
            self.dirty = set()
        self.dirty |= _names(outs)

    def copy(self, out, in_):
        self._pre([out], [in_])
        return self.e.tensor_copy(out, in_)

    def memset(self, out, v):
        self._pre([out], [])
        return self.e.memset(out, v)

    def tt(self, out, a, b, op):
        self._pre([out], [a, b])
        return self.e.tensor_tensor(out, a, b, op)

    def ts(self, out, a, s1, op0, s2=None, op1=ALU.bypass, accum_out=None):
        self._pre([out, accum_out], [a, s1, s2])
        return self.e.tensor_scalar(out, a, s1, s2, op0, op1, accum_out)

    def stt(self, out, in0, s, in1, op0, op1):
        self._pre([out], [in0, s, in1])
        return self.e.scalar_tensor_tensor(out, in0, s, in1, op0, op1)

    def scan(self, out, d0, d1, init):
        self._pre([out], [d0, d1, init])
        return self.e.tensor_tensor_scan(out, d0, d1, init, ALU.mult, ALU.add)

    def red(self, out, in_, op):
        self._pre([out], [in_])
        return self.e.tensor_reduce(out, in_, AX.X, op)

    def act(self, out, in_, func, bias=0.0, scale=1.0, accum_out=None):
        self._pre([out, accum_out], [in_, bias, scale])
        return self.e.activation(out, in_, func, bias=bias, scale=scale,
                                 accum_out=accum_out)

    def recip(self, out, in_):
        self._pre([out], [in_])
        return self.e.reciprocal(out, in_)


class PEW:
    def __init__(self, eng):
        self.e = eng

    def wait(self, sem, n=None):
        self.e.wait_ge(sem.h, sem.n if n is None else n)

    def inc(self, sem, k=1):
        self.e.sem_inc(sem.h, k)
        sem.n += k

    def mm(self, out, lhsT, rhs, start, stop):
        return self.e.matmul(out, lhsT, rhs, start=start, stop=stop)

    def tr(self, out, in_, idn):
        return self.e.transpose(out, in_, idn)


class DMAQ:
    def __init__(self, eng):
        self.e = eng

    def wait(self, sem, n=None):
        self.e.wait_ge(sem.h, sem.n if n is None else n)

    def inc(self, sem, k=1):
        self.e.sem_inc(sem.h, k)
        sem.n += k

    def go(self, out, in_, sem=None, **kw):
        i = self.e.dma_start(out=out, in_=in_, **kw)
        if sem is not None:
            sem.bump(i, 16)
        return i


def build(phases=("A1", "A2a", "A2b", "R", "B", "C"), debug=False):
    nc = bass.Bass("TRN2", target_bir_lowering=False)

    X = nc.dram_tensor("x", [NB, L, D], F32, kind="ExternalInput").ap()
    OUT = nc.dram_tensor("out", [NB, L, D], F32, kind="ExternalOutput").ap()
    IDN = nc.dram_tensor("identity", [128, 128], F32, kind="ExternalInput").ap()

    wn = {}
    for nm, shp in [
        ("ln_att_g", [D]), ("ln_att_b", [D]), ("ln_ffn_g", [D]), ("ln_ffn_b", [D]),
        ("att_time_decay", [D]), ("att_time_first", [D]),
        ("att_mix_k", [D]), ("att_mix_v", [D]), ("att_mix_r", [D]),
        ("att_wk", [D, D]), ("att_wv", [D, D]), ("att_wr", [D, D]), ("att_wo", [D, D]),
        ("ffn_mix_k", [D]), ("ffn_mix_r", [D]),
        ("ffn_wk", [D, DFF]), ("ffn_wv", [DFF, D]), ("ffn_wr", [D, D]),
    ]:
        wn[nm] = nc.dram_tensor("L0_attn_" + nm, shp, F32, kind="ExternalInput").ap()
    for nm, shp in [("W1", [D, 256]), ("b1", [256]), ("W2", [256, NZ]), ("b2", [NZ]),
                    ("Wa", [NZ, D, DFF]), ("Wb", [NZ, DFF, D])]:
        wn[nm] = nc.dram_tensor("L0_ffn_" + nm, shp, F32, kind="ExternalInput").ap()

    skind = "ExternalOutput" if debug else "Internal"
    S_XPP = nc.dram_tensor("s_xpp", [NB, L, D], F32, kind=skind).ap()
    S_Y = nc.dram_tensor("s_y", [NB, L, D], F32, kind=skind).ap()
    S_G32 = nc.dram_tensor("s_g32", [NB, L, D], F32, kind=skind).ap()
    S_GBF = nc.dram_tensor("s_gbf", [D, NB * L], BF16, kind=skind).ap()
    S_KF = nc.dram_tensor("s_kf", [DFF, NB * L], BF16, kind=skind).ap()
    S_SR2 = nc.dram_tensor("s_sr2", [D, NB * L], F32, kind=skind).ap()
    S_ZAC = nc.dram_tensor("s_zac", [D, NB * L], F32, kind=skind).ap()

    gbf_r = S_GBF.rearrange("(k p) t -> p k t", p=128)
    kf_r = S_KF.rearrange("(j p) t -> p j t", p=128)
    sr2_r = S_SR2.rearrange("(k p) t -> p k t", p=128)
    zac_r = S_ZAC.rearrange("(k p) t -> p k t", p=128)

    st = contextlib.ExitStack()
    with st:
        E = st.enter_context

        V = Eng(nc.vector)
        A = Eng(nc.scalar)
        G = Eng(nc.gpsimd)
        P = PEW(nc.tensor)
        SY = DMAQ(nc.sync)
        GD = DMAQ(nc.gpsimd)

        semnames = ["ld", "sx", "ssq", "srd", "sst", "snr", "str_", "saf", "smx",
                    "smm", "sek", "sdv", "ssc", "swo", "sad", "sou", "sbw", "sbg",
                    "sbz", "sbm1", "sbm2", "sbev", "sbac", "sfin", "sbar", "srt"]
        S = {n: Sem(nc, st, n) for n in semnames}

        dma_sems = [S[k] for k in ("ld", "sx", "sou", "sbw", "sbg", "sbac",
                                   "srt", "sfin")]

        def barrier():
            for dsem in dma_sems:
                SY.wait(dsem)
            for e in (V, A, G):
                e.fence(S["sbar"])
            P.e.drain()
            P.inc(S["sbar"])
            SY.inc(S["sbar"])
            tgt = S["sbar"].n
            for e in (V, A, G, P, SY):
                e.wait(S["sbar"], tgt)

        FREE = {}

        def consumed(key, sem):
            FREE[key] = (sem, sem.n)

        def wait_free(eng, key):
            if key in FREE:
                sem, n = FREE[key]
                eng.wait(sem, n)

        BANK = {}

        def bank_wait(i):
            if i in BANK:
                sem, n = BANK[i]
                P.wait(sem, n)

        def bank_set(i, sem):
            BANK[i] = (sem, sem.n)

        idn = E(nc.sbuf_tensor("idn", [128, 128], F32))
        pnames = ["lag", "lab", "lfg", "lfb", "mk", "mv", "mr",
                  "fmk", "fmr", "ew", "eu"]
        PB = {n: E(nc.sbuf_tensor("pp_" + n, [128, KT], F32)) for n in pnames}
        carN = E(nc.sbuf_tensor("carN", [128, NB, KT], F32))
        carQ = E(nc.sbuf_tensor("carQ", [128, NB, KT], F32))
        pacc = E(nc.sbuf_tensor("pacc", [128, KT, 16], F32))
        RNAMES = ["ssum", "ssum2", "snmu", "sex2", "snv", "sL", "srstd",
                  "sMn", "sT", "sU", "sR2", "sR2b", "sM2n", "sIvr"]
        RS = {n: E(nc.sbuf_tensor("r_" + n, [128, NTA, 2], F32)) for n in RNAMES}
        sqA = E(nc.sbuf_tensor("sqA", [128, D], F32))
        wzbc = E(nc.sbuf_tensor("wzbc", [128, 16], F32))
        onesr = E(nc.sbuf_tensor("onesr", [1, 128], F32))
        cst = E(nc.sbuf_tensor("cst", [128, 4], F32))
        I32 = mybir.dt.int32
        iota8f = E(nc.sbuf_tensor("iota8f", [2, 8], F32))
        iota8i = E(nc.sbuf_tensor("iota8i", [2, 8], I32))
        jjoff_i = E(nc.sbuf_tensor("jjoff_i", [128, 32], I32))
        jjoff_f = E(nc.sbuf_tensor("jjoff_f", [128, 32], F32))
        idxAi = E(nc.sbuf_tensor("idxAi", [128, 48], I32))
        idxBi = E(nc.sbuf_tensor("idxBi", [128, 192], I32))
        idxTf = E(nc.sbuf_tensor("idxTf", [128, 32], F32))
        rsF = E(nc.sbuf_tensor("rsF", [128, 1], F32))
        nc.const_aps.register(F32, EPS, cst[:, 0:1])
        nc.const_aps.register(F32, 1.0, cst[:, 1:2])
        nc.const_aps.register(F32, 1.0 + EPS, cst[:, 2:3])
        nc.const_aps.register(F32, -EPS, cst[:, 3:4])
        psum = [E(nc.psum_tensor(f"pb{i}", [128, 512], F32)) for i in range(8)]

        SY.go(idn[:], IDN, sem=S["ld"])
        for nm, src in [("lag", "ln_att_g"), ("lab", "ln_att_b"),
                        ("lfg", "ln_ffn_g"), ("lfb", "ln_ffn_b"),
                        ("mk", "att_mix_k"), ("mv", "att_mix_v"),
                        ("mr", "att_mix_r"), ("fmk", "ffn_mix_k"),
                        ("fmr", "ffn_mix_r"), ("ew", "att_time_decay"),
                        ("eu", "att_time_first")]:
            SY.go(PB[nm][:], wn[src].rearrange("(k p) -> p k", p=128),
                  sem=S["ld"], allow_slow_non_contiguous=True)
        for e in (V, A, G):
            e.wait(S["ld"])
        V.memset(onesr[:], 1.0)
        A.act(PB["ew"][:], PB["ew"][:], AF.Exp)
        A.act(PB["ew"][:], PB["ew"][:], AF.Exp, scale=-1.0)
        A.act(PB["eu"][:], PB["eu"][:], AF.Exp)
        V.fence(S["sst"])
        A.fence(S["sst"])
        G.memset(carN[:], 0.0)
        G.memset(carQ[:], 0.0)
        G.memset(pacc[:], 0.0)
        G.memset(cst[:, 0:1], EPS)
        G.memset(cst[:, 1:2], 1.0)
        G.memset(cst[:, 2:3], 1.0 + EPS)
        G.memset(cst[:, 3:4], -EPS)
        G.e.iota(jjoff_i[:], [[128, 32]], base=0, channel_multiplier=1)
        G.e.iota(iota8i[:], [[1, 8]], base=0, channel_multiplier=0)
        G.drain()
        G.copy(jjoff_f[:], jjoff_i[:])
        G.copy(iota8f[:], iota8i[:])
        G.fence(S["sst"])

        n_out_dma = [0]

        def rsl(nm, n_tiles, par):
            return RS[nm][:, 0:n_tiles, par:par + 1]

        def ln_sums(src, n_tiles, par):
            wait_free(A, ("stats", par))
            wait_free(G, ("stats", par))
            for n in range(n_tiles):
                A.e.activation(sqA[:], src[:, n, :], AF.Square,
                               accum_out=RS["ssum2"][:, n, par:par + 1])
                A.e.activation(sqA[:], src[:, n, :], AF.Copy,
                               accum_out=RS["ssum"][:, n, par:par + 1])
            A.fence(S["srd"])
            G.wait(S["srd"])
            c = 1.0 / D
            G.ts(rsl("snmu", n_tiles, par), rsl("ssum", n_tiles, par),
                 -c, ALU.mult)
            G.ts(rsl("sex2", n_tiles, par), rsl("ssum2", n_tiles, par),
                 c, ALU.mult)
            G.tt(rsl("snv", n_tiles, par), rsl("snmu", n_tiles, par),
                 rsl("snmu", n_tiles, par), ALU.mult)
            G.tt(rsl("snv", n_tiles, par), rsl("snv", n_tiles, par),
                 rsl("sex2", n_tiles, par), ALU.subtract)
            G.fence(S["sst"])
            A.wait(S["sst"])
            A.act(rsl("sL", n_tiles, par), rsl("snv", n_tiles, par),
                  AF.Ln, scale=-1.0, bias=EPS)

        def ln_stats(src, n_tiles, par):
            ln_sums(src, n_tiles, par)
            A.act(rsl("srstd", n_tiles, par), rsl("sL", n_tiles, par),
                  AF.Exp, scale=-0.5)
            A.fence(S["sst"])
            G.wait(S["sst"])
            G.tt(rsl("sMn", n_tiles, par), rsl("snmu", n_tiles, par),
                 rsl("srstd", n_tiles, par), ALU.mult)
            G.fence(S["srd"])

        def ln_stats_fused(src, n_tiles, par):
            ln_sums(src, n_tiles, par)
            A.act(rsl("srstd", n_tiles, par), rsl("sL", n_tiles, par),
                  AF.Exp, scale=-0.5)
            A.act(rsl("sT", n_tiles, par), rsl("sL", n_tiles, par),
                  AF.Exp, scale=-1.0)
            A.act(rsl("sU", n_tiles, par), rsl("sT", n_tiles, par),
                  AF.Ln, scale=-EPS, bias=1.0 + EPS)
            A.act(rsl("sIvr", n_tiles, par), rsl("sU", n_tiles, par),
                  AF.Exp, scale=0.5)
            A.act(rsl("sR2b", n_tiles, par), rsl("sU", n_tiles, par),
                  AF.Exp, scale=-0.5)
            A.fence(S["sst"])
            G.wait(S["sst"])
            G.tt(rsl("sR2", n_tiles, par), rsl("srstd", n_tiles, par),
                 rsl("sR2b", n_tiles, par), ALU.mult)
            G.tt(rsl("sM2n", n_tiles, par), rsl("snmu", n_tiles, par),
                 rsl("sR2", n_tiles, par), ALU.mult)
            G.fence(S["srd"])

        def ln_norm(dst, src, n_tiles, par, on=None, scale_nm="srstd",
                    bias_nm="sMn"):
            e = V if on is None else on
            e.wait(S["srd"])
            for n in range(n_tiles):
                if e is A:
                    e.act(dst[:, n, :], src[:, n, :], AF.Identity,
                          scale=RS[scale_nm][:, n, par:par + 1],
                          bias=RS[bias_nm][:, n, par:par + 1])
                else:
                    e.ts(dst[:, n, :], src[:, n, :],
                         RS[scale_nm][:, n, par:par + 1], ALU.mult,
                         RS[bias_nm][:, n, par:par + 1], ALU.add)
            e.fence(S["snr"])
            consumed(("stats", par), S["snr"])

        def transpose_affine(src_tok, dst_ch, par_key, gcol, bcol, accum_slot=None):
            off = 1 if dst_ch.shape[2] == CHA + 1 else 0
            for kk in range(KT):
                bank = 4 + (kk % 2)
                bank_wait(bank)
                for n in range(NTA):
                    ii = P.tr(psum[bank][:, n * 128:(n + 1) * 128],
                              src_tok[:, n, kk * 128:(kk + 1) * 128], idn[:])
                S["str_"].bump(ii, 1)
                A.wait(S["str_"])
                kwargs = {}
                if accum_slot is not None:
                    kwargs["accum_out"] = accum_slot(kk)
                if gcol is not None:
                    ia = A.act(dst_ch[:, kk, off:off + CHA], psum[bank][:, 0:CHA],
                               AF.Identity, scale=gcol[:, kk:kk + 1],
                               bias=bcol[:, kk:kk + 1], **kwargs)
                else:
                    ia = A.act(dst_ch[:, kk, off:off + CHA], psum[bank][:, 0:CHA],
                               AF.Copy, **kwargs)
                S["saf"].bump(ia, 1)
                bank_set(bank, S["saf"])

        def transpose_add(src_ch, dst_tok, n_tiles):
            for n in range(n_tiles):
                bank_wait(4)
                bank_wait(5)
                for kk in range(KT):
                    bank = 4 + kk // 4
                    ii = P.tr(psum[bank][:, (kk % 4) * 128:(kk % 4 + 1) * 128],
                              src_ch[:, kk, n * 128:(n + 1) * 128], idn[:])
                    if kk == 7:
                        S["str_"].bump(ii, 1)
                V.wait(S["str_"])
                V.tt(dst_tok[:, n, 0:512], dst_tok[:, n, 0:512],
                     psum[4][:, 0:512], ALU.add)
                V.tt(dst_tok[:, n, 512:1024], dst_tok[:, n, 512:1024],
                     psum[5][:, 0:512], ALU.add)
                V.fence(S["sad"])
                bank_set(4, S["sad"])
                bank_set(5, S["sad"])

        barrier()

        if "A1" in phases:
            sc_A1 = st.enter_context(nc.named_scope("PH_A1"))
            with contextlib.ExitStack() as sa:
                EA = sa.enter_context
                wkb = EA(nc.sbuf_tensor("wkb", [128, KT, D], BF16))
                wvb = EA(nc.sbuf_tensor("wvb", [128, KT, D], BF16))
                wrb = EA(nc.sbuf_tensor("wrb", [128, KT, D], BF16))
                wob = EA(nc.sbuf_tensor("wob", [128, KT, D], BF16))
                ewb = EA(nc.sbuf_tensor("ewb", [128, KT, CHA], F32))
                ones2 = EA(nc.sbuf_tensor("ones2", [128, CHA], F32))
                xa = [EA(nc.sbuf_tensor(f"xa{i}", [128, NTA, D], F32)) for i in range(2)]
                hch = [EA(nc.sbuf_tensor(f"hch{i}", [128, KT, CHA + 1], F32))
                       for i in range(2)]
                dbf = EA(nc.sbuf_tensor("dbf", [128, KT, CHA], F32))
                mixb = {m: [EA(nc.sbuf_tensor(f"mx{m}{i}", [128, KT, CHA], BF16))
                            for i in range(2)] for m in "kvr"}
                eKb2 = [[EA(nc.sbuf_tensor(f"eK{s}_{i}", [128, CHA], F32))
                         for i in range(3)] for s in range(2)]
                erb2 = [[EA(nc.sbuf_tensor(f"er{s}_{i}", [128, CHA], F32))
                         for i in range(3)] for s in range(2)]
                vvb2 = [[EA(nc.sbuf_tensor(f"vv{s}_{i}", [128, CHA], F32))
                         for i in range(3)] for s in range(2)]
                ekvb2 = [[EA(nc.sbuf_tensor(f"ekv{s}_{i}", [128, CHA], F32))
                          for i in range(3)] for s in range(2)]
                NNb2 = [[EA(nc.sbuf_tensor(f"NN{s}_{i}", [128, CHA + 1], F32))
                         for i in range(3)] for s in range(2)]
                QQb2 = [[EA(nc.sbuf_tensor(f"QQ{s}_{i}", [128, CHA + 1], F32))
                         for i in range(3)] for s in range(2)]
                rwk = [EA(nc.sbuf_tensor(f"rwk{i}", [128, KT, CHA], BF16))
                       for i in range(2)]
                attc = EA(nc.sbuf_tensor("attc", [128, KT, CHA], F32))

                for buf, src in [(wkb, "att_wk"), (wvb, "att_wv"),
                                 (wrb, "att_wr"), (wob, "att_wo")]:
                    GD.go(buf[:], wn[src].rearrange("(k p) c -> p k c", p=128),
                          sem=S["sbw"])
                V.memset(ones2[:], 1.0)
                V.fence(S["sst"])
                A.wait(S["sst"])
                for kk in range(KT):
                    A.act(ewb[:, kk, :], ones2[:], AF.Copy,
                          scale=PB["ew"][:, kk:kk + 1])
                A.fence(S["sst"])
                for e in (V, A, G, P):
                    e.wait(S["sbw"])

                chunks = [(b, ci) for ci in range(NCA) for b in range(NB)]
                load_n = {}
                mix_n = {}
                snr_n = {}
                rwk_n = {}

                def a1_load(idx):
                    b, ci = chunks[idx]
                    par = b
                    t0 = ci * CHA
                    wait_free(SY, ("xa", par))
                    SY.go(xa[par][:], X[b, t0:t0 + CHA, :]
                          .rearrange("(n p) c -> p n c", p=128), sem=S["sx"])
                    load_n[idx] = S["sx"].n

                def a1_front(idx):
                    b, ci = chunks[idx]
                    par = b
                    A.wait(S["sx"], load_n[idx])
                    G.wait(S["sx"], load_n[idx])
                    ln_stats_fused(xa[par], NTA, par)
                    eng = V if par == 0 else A
                    eng.wait(S["srd"])
                    for n in range(NTA):
                        if eng is A:
                            A.act(xa[par][:, n, :], xa[par][:, n, :],
                                  AF.Identity,
                                  scale=RS["sR2"][:, n, par:par + 1],
                                  bias=RS["sM2n"][:, n, par:par + 1])
                        else:
                            V.ts(xa[par][:, n, :], xa[par][:, n, :],
                                 RS["sR2"][:, n, par:par + 1], ALU.mult,
                                 RS["sM2n"][:, n, par:par + 1], ALU.add)
                    eng.fence(S["snr"])
                    snr_n[idx] = S["snr"].n
                    wait_free(G, ("hchcol", par))
                    if ci == 0:
                        G.memset(hch[par][:, :, 0:1], 0.0)
                    else:
                        G.copy(hch[par][:, :, 0:1],
                               hch[par][:, :, CHA:CHA + 1])
                    G.fence(S["ssq"])
                    P.wait(S["snr"], snr_n[idx])
                    P.wait(S["ssq"])
                    wait_free(A, ("hch", par))
                    transpose_affine(xa[par], hch[par], ("hch", par),
                                     PB["lag"], PB["lab"])
                    A.fence(S["saf"])
                    h1 = hch[par][:, :, 1:CHA + 1]
                    h0 = hch[par][:, :, 0:CHA]
                    V.wait(S["saf"])
                    wait_free(V, ("dbf",))
                    V.tt(dbf[:], h1, h0, ALU.subtract)
                    for m, mixp in [("k", "mk"), ("v", "mv"), ("r", "mr")]:
                        wait_free(V, ("mix" + m, par))
                        for kk in range(KT):
                            V.stt(mixb[m][par][:, kk, :], dbf[:, kk, :],
                                  PB[mixp][:, kk:kk + 1], h0[:, kk, :],
                                  ALU.mult, ALU.add)
                    V.fence(S["smx"])
                    mix_n[idx] = S["smx"].n
                    consumed(("dbf",), S["smx"])
                    consumed(("hchcol", par), S["smx"])
                    consumed(("hch", par), S["smx"])

                def a1_back_kvr(idx):
                    b, ci = chunks[idx]
                    par = b
                    if ci == 0:
                        V.memset(carN[:, b, :], 0.0)
                        V.memset(carQ[:, b, :], 0.0)
                    eKb, erb, vvb = eKb2[par], erb2[par], vvb2[par]
                    ekvb, NNb, QQb = ekvb2[par], NNb2[par], QQb2[par]
                    P.wait(S["smx"], mix_n[idx])
                    ev_n = {}
                    v1_n = {}
                    g2_n = {}
                    a3_n = {}

                    def a1_mm(j):
                        for gi, m in enumerate(("k", "v", "r")):
                            bank = (j * 3 + gi) % 6
                            bank = bank if bank < 4 else bank + 2
                            bank_wait(bank)
                            wsb = {"k": wkb, "v": wvb, "r": wrb}[m]
                            for kk in range(KT):
                                ii = P.mm(psum[bank][:, 0:CHA],
                                          wsb[:, kk, j * 128:(j + 1) * 128],
                                          mixb[m][par][:, kk, :],
                                          start=(kk == 0), stop=(kk == KT - 1))
                            S["smm"].bump(ii, 1)

                    def a1_ev(j):
                        jp = j % 3
                        A.wait(S["smm"])
                        wait_free(A, ("wkv", par, jp))
                        bks = []
                        for gi in range(3):
                            bank = (j * 3 + gi) % 6
                            bks.append(bank if bank < 4 else bank + 2)
                        A.act(eKb[jp][:], psum[bks[0]][:, 0:CHA], AF.Exp)
                        A.act(vvb[jp][:], psum[bks[1]][:, 0:CHA], AF.Copy)
                        ia = A.act(erb[jp][:], psum[bks[2]][:, 0:CHA],
                                   AF.Exp, scale=-1.0)
                        S["sek"].bump(ia, 1)
                        ev_n[j] = S["sek"].n
                        for bank in bks:
                            bank_set(bank, S["sek"])

                    def a1_g1(j):
                        jp = j % 3
                        G.wait(S["sek"], ev_n[j])
                        ig = G.tt(ekvb[jp][:], eKb[jp][:], vvb[jp][:], ALU.mult)
                        S["ssc"].bump(ig, 1)

                    def a1_v(j):
                        jp = j % 3
                        V.wait(S["ssc"])
                        if j == 0:
                            wait_free(V, ("rwk", par))
                        V.copy(NNb[jp][:, 0:1], carN[:, b, j:j + 1])
                        V.copy(QQb[jp][:, 0:1], carQ[:, b, j:j + 1])
                        V.scan(NNb[jp][:, 1:CHA + 1], ewb[:, j, :],
                               ekvb[jp][:], carN[:, b, j:j + 1])
                        V.scan(QQb[jp][:, 1:CHA + 1], ewb[:, j, :],
                               eKb[jp][:], carQ[:, b, j:j + 1])
                        V.copy(carN[:, b, j:j + 1], NNb[jp][:, CHA:CHA + 1])
                        V.copy(carQ[:, b, j:j + 1], QQb[jp][:, CHA:CHA + 1])
                        V.stt(ekvb[jp][:], ekvb[jp][:], PB["eu"][:, j:j + 1],
                              NNb[jp][:, 0:CHA], ALU.mult, ALU.add)
                        V.stt(eKb[jp][:], eKb[jp][:],
                              PB["eu"][:, j:j + 1],
                              QQb[jp][:, 0:CHA], ALU.mult, ALU.add)
                        V.stt(eKb[jp][:], erb[jp][:], 1.0, eKb[jp][:],
                              ALU.add, ALU.mult)
                        V.recip(eKb[jp][:], eKb[jp][:])
                        iv = V.tt(rwk[par][:, j, :], ekvb[jp][:], eKb[jp][:],
                                  ALU.mult)
                        S["sdv"].bump(iv, 1)
                        consumed(("wkv", par, jp), S["sdv"])

                    for j in range(KT):
                        a1_mm(j)
                        a1_ev(j)
                        a1_g1(j)
                        a1_v(j)
                    rwk_n[idx] = S["sdv"].n
                    consumed(("mixk", par), S["smm"])
                    consumed(("mixv", par), S["smm"])
                    consumed(("mixr", par), S["smm"])

                def a1_back_wo(idx):
                    b, ci = chunks[idx]
                    par = b
                    t0 = ci * CHA
                    P.wait(S["sdv"], rwk_n[idx])
                    wait_free(A, ("attc",))
                    for co in range(KT):
                        bank = co % 2
                        bank_wait(bank)
                        for kk in range(KT):
                            ii = P.mm(psum[bank][:, 0:CHA],
                                      wob[:, kk, co * 128:(co + 1) * 128],
                                      rwk[par][:, kk, :],
                                      start=(kk == 0), stop=(kk == KT - 1))
                        S["swo"].bump(ii, 1)
                        A.wait(S["swo"])
                        ia = A.act(attc[:, co, :], psum[bank][:, 0:CHA], AF.Copy)
                        S["sek"].bump(ia, 1)
                        bank_set(bank, S["sek"])
                    consumed(("rwk", par), S["swo"])
                    P.wait(S["sek"])
                    V.wait(S["snr"], snr_n[idx])
                    for n in range(NTA):
                        bank_wait(4)
                        bank_wait(5)
                        for kk in range(KT):
                            bank = 4 + kk // 4
                            ii = P.tr(psum[bank][:, (kk % 4) * 128:(kk % 4 + 1) * 128],
                                      attc[:, kk, n * 128:(n + 1) * 128], idn[:])
                            if kk == 7:
                                S["str_"].bump(ii, 1)
                        V.wait(S["str_"])
                        V.stt(xa[par][:, n, 0:512], xa[par][:, n, 0:512],
                              RS["sIvr"][:, n, par:par + 1],
                              psum[4][:, 0:512], ALU.mult, ALU.add)
                        V.stt(xa[par][:, n, 512:1024], xa[par][:, n, 512:1024],
                              RS["sIvr"][:, n, par:par + 1],
                              psum[5][:, 0:512], ALU.mult, ALU.add)
                        V.fence(S["sad"])
                        bank_set(4, S["sad"])
                        bank_set(5, S["sad"])
                    consumed(("attc",), S["str_"])
                    consumed(("stats", par), S["sad"])
                    SY.wait(S["sad"])
                    SY.go(S_XPP[b, t0:t0 + CHA, :]
                          .rearrange("(n p) c -> p n c", p=128),
                          xa[par][:], sem=S["sou"])
                    consumed(("xa", par), S["sou"])

                a1_load(0)
                a1_load(1)
                a1_front(0)
                a1_back_kvr(0)
                for idx in range(1, len(chunks)):
                    a1_front(idx)
                    a1_back_wo(idx - 1)
                    if idx + 1 < len(chunks):
                        a1_load(idx + 1)
                    a1_back_kvr(idx)
                a1_back_wo(len(chunks) - 1)
            barrier()

        if "A2a" in phases:
            sc_A2a = st.enter_context(nc.named_scope("PH_A2a"))
            with contextlib.ExitStack() as sa:
                EA = sa.enter_context
                fwkb = EA(nc.sbuf_tensor("fwkb", [128, KT, DFF], BF16))
                fwrb = EA(nc.sbuf_tensor("fwrb", [128, KT, D], BF16))
                xa2 = [EA(nc.sbuf_tensor(f"xa2_{i}", [128, NTA, D], F32))
                       for i in range(2)]
                h2pre = EA(nc.sbuf_tensor("h2pre", [128, NTA, D], F32))
                h2ch = [EA(nc.sbuf_tensor(f"h2ch{i}", [128, KT, CHA + 1], F32))
                        for i in range(2)]
                dbf2 = EA(nc.sbuf_tensor("dbf2", [128, KT, CHA], BF16))
                mfk = [EA(nc.sbuf_tensor(f"mfk{i}", [128, KT, CHA], BF16))
                       for i in range(2)]
                mfr = [EA(nc.sbuf_tensor(f"mfr{i}", [128, KT, CHA], BF16))
                       for i in range(2)]
                kfb = [EA(nc.sbuf_tensor(f"kfb{i}", [128, JT, CHA], BF16))
                       for i in range(2)]
                trel = EA(nc.sbuf_tensor("trel", [128, 4, CHA], F32))
                sr2b = [EA(nc.sbuf_tensor(f"sr2b{i}", [128, KT, CHA], F32))
                        for i in range(2)]

                GD.go(fwkb[:], wn["ffn_wk"].rearrange("(k p) c -> p k c", p=128),
                      sem=S["sbw"])
                GD.go(fwrb[:], wn["ffn_wr"].rearrange("(k p) c -> p k c", p=128),
                      sem=S["sbw"])
                for e in (V, A, G, P):
                    e.wait(S["sbw"])

                chunks = [(b, ci) for ci in range(NCA) for b in range(NB)]
                load_n = {}
                mix_n = {}

                def a2a_load(idx):
                    b, ci = chunks[idx]
                    par = b
                    t0 = ci * CHA
                    wait_free(SY, ("xa2", par))
                    SY.go(xa2[par][:], S_XPP[b, t0:t0 + CHA, :]
                          .rearrange("(n p) c -> p n c", p=128), sem=S["sx"])
                    load_n[idx] = S["sx"].n

                def a2a_front(idx):
                    b, ci = chunks[idx]
                    par = b
                    A.wait(S["sx"], load_n[idx])
                    G.wait(S["sx"], load_n[idx])
                    ln_stats(xa2[par], NTA, par)
                    eng = V if par == 0 else A
                    wait_free(eng, ("h2pre",))
                    ln_norm(h2pre, xa2[par], NTA, par, on=eng)
                    consumed(("xa2", par), S["snr"])
                    wait_free(G, ("h2col", par))
                    if ci == 0:
                        G.memset(h2ch[par][:, :, 0:1], 0.0)
                    else:
                        G.copy(h2ch[par][:, :, 0:1],
                               h2ch[par][:, :, CHA:CHA + 1])
                    G.fence(S["ssq"])
                    P.wait(S["snr"])
                    P.wait(S["ssq"])
                    wait_free(A, ("h2ch", par))
                    transpose_affine(h2pre, h2ch[par], ("h2ch", par),
                                     PB["lfg"], PB["lfb"])
                    A.fence(S["saf"])
                    consumed(("h2pre",), S["saf"])
                    h1 = h2ch[par][:, :, 1:CHA + 1]
                    h0 = h2ch[par][:, :, 0:CHA]
                    V.wait(S["saf"])
                    wait_free(V, ("dbf2",))
                    V.tt(dbf2[:], h1, h0, ALU.subtract)
                    wait_free(V, ("mfk", par))
                    for kk in range(KT):
                        V.stt(mfk[par][:, kk, :], dbf2[:, kk, :],
                              PB["fmk"][:, kk:kk + 1], h0[:, kk, :],
                              ALU.mult, ALU.add)
                    wait_free(V, ("mfr", par))
                    for kk in range(KT):
                        V.stt(mfr[par][:, kk, :], dbf2[:, kk, :],
                              PB["fmr"][:, kk:kk + 1], h0[:, kk, :],
                              ALU.mult, ALU.add)
                    V.fence(S["smx"])
                    mix_n[idx] = S["smx"].n
                    consumed(("dbf2",), S["smx"])
                    consumed(("h2col", par), S["smx"])
                    consumed(("h2ch", par), S["smx"])

                def a2a_back_kf(idx):
                    b, ci = chunks[idx]
                    par = b
                    P.wait(S["smx"], mix_n[idx])
                    wait_free(A, ("kfb", par))
                    wait_free(V, ("kfb", par))
                    for j in range(JT):
                        bank = j % 2
                        bank_wait(bank)
                        for kk in range(KT):
                            ii = P.mm(psum[bank][:, 0:CHA],
                                      fwkb[:, kk, j * 128:(j + 1) * 128],
                                      mfk[par][:, kk, :],
                                      start=(kk == 0), stop=(kk == KT - 1))
                        S["smm"].bump(ii, 1)
                        A.wait(S["smm"])
                        if j % 4 == 0:
                            A.wait(S["sdv"])
                        ia = A.act(trel[:, j % 4, :], psum[bank][:, 0:CHA],
                                   AF.Relu)
                        S["sek"].bump(ia, 1)
                        bank_set(bank, S["sek"])
                        V.wait(S["sek"])
                        iv = V.tt(kfb[par][:, j, :], trel[:, j % 4, :],
                                  trel[:, j % 4, :], ALU.mult)
                        if j % 4 == 3 or j == JT - 1:
                            S["sdv"].bump(iv, 1)
                    consumed(("mfk", par), S["smm"])

                def a2a_back_fr(idx):
                    b, ci = chunks[idx]
                    par = b
                    t0 = ci * CHA
                    tg = b * L + t0
                    wait_free(A, ("sr2b", par))
                    for co in range(KT):
                        bank = 2 + co % 2
                        bank_wait(bank)
                        for kk in range(KT):
                            ii = P.mm(psum[bank][:, 0:CHA],
                                      fwrb[:, kk, co * 128:(co + 1) * 128],
                                      mfr[par][:, kk, :],
                                      start=(kk == 0), stop=(kk == KT - 1))
                        S["smm"].bump(ii, 1)
                        A.wait(S["smm"])
                        ia = A.act(sr2b[par][:, co, :], psum[bank][:, 0:CHA],
                                   AF.Exp, scale=-1.0)
                        S["sek"].bump(ia, 1)
                        bank_set(bank, S["sek"])
                    consumed(("mfr", par), S["smm"])
                    A.act(sr2b[par][:], sr2b[par][:], AF.Ln, bias=1.0)
                    A.act(sr2b[par][:], sr2b[par][:], AF.Exp, scale=-1.0)
                    A.fence(S["sek"])
                    SY.wait(S["sek"])
                    SY.wait(S["sdv"])
                    SY.go(kf_r[:, :, tg:tg + CHA], kfb[par][:], sem=S["sou"])
                    SY.go(sr2_r[:, :, tg:tg + CHA], sr2b[par][:], sem=S["sou"])
                    consumed(("kfb", par), S["sou"])
                    consumed(("sr2b", par), S["sou"])

                a2a_load(0)
                a2a_load(1)
                a2a_front(0)
                for idx in range(len(chunks)):
                    a2a_back_kf(idx)
                    if idx + 1 < len(chunks):
                        a2a_front(idx + 1)
                    a2a_back_fr(idx)
                    if idx + 2 < len(chunks):
                        a2a_load(idx + 2)
            barrier()

        if "A2b" in phases:
            sc_A2b = st.enter_context(nc.named_scope("PH_A2b"))
            with contextlib.ExitStack() as sa:
                EA = sa.enter_context
                fwvb = EA(nc.sbuf_tensor("fwvb", [128, JT, D], BF16))
                kfw = [EA(nc.sbuf_tensor(f"kfw{i}", [128, JT, CHA], BF16))
                       for i in range(2)]
                sr2w = [EA(nc.sbuf_tensor(f"sr2w{i}", [128, KT, CHA], F32))
                        for i in range(2)]
                xa3 = [EA(nc.sbuf_tensor(f"xa3_{i}", [128, NTA, D], F32))
                       for i in range(2)]
                xr = [EA(nc.sbuf_tensor(f"xr{i}", [128, NTA, D], F32))
                      for i in range(2)]
                ffc = [EA(nc.sbuf_tensor(f"ffc{i}", [128, KT, CHA], F32))
                       for i in range(2)]
                lnb = [EA(nc.sbuf_tensor(f"lnb{i}", [128, NTA, D], F32))
                       for i in range(2)]
                gbuf = [EA(nc.sbuf_tensor(f"gbuf{i}", [128, NTA, D], F32))
                        for i in range(2)]
                gbfo = [EA(nc.sbuf_tensor("gbfo0", [128, KT, CHA], BF16))] * 2

                GD.go(fwvb[:], wn["ffn_wv"].rearrange("(j p) c -> p j c", p=128),
                      sem=S["sbw"])
                for e in (V, A, G, P):
                    e.wait(S["sbw"])

                chunks = [(b, ci) for b in range(NB) for ci in range(NCA)]
                load_n = {}
                ffc_n = {}

                def a2b_load(idx):
                    b, ci = chunks[idx]
                    par = idx % 2
                    t0 = ci * CHA
                    tg = b * L + t0
                    wait_free(SY, ("kfw", par))
                    SY.go(kfw[par][:], kf_r[:, :, tg:tg + CHA], sem=S["sx"])
                    wait_free(SY, ("sr2w", par))
                    SY.go(sr2w[par][:], sr2_r[:, :, tg:tg + CHA], sem=S["sx"])
                    wait_free(SY, ("xa3", par))
                    SY.go(xa3[par][:], S_XPP[b, t0:t0 + CHA, :]
                          .rearrange("(n p) c -> p n c", p=128), sem=S["sx"])
                    wait_free(SY, ("xr", par))
                    SY.go(xr[par][:], X[b, t0:t0 + CHA, :]
                          .rearrange("(n p) c -> p n c", p=128), sem=S["sx"])
                    load_n[idx] = S["sx"].n

                def a2b_back1(idx):
                    b, ci = chunks[idx]
                    par = idx % 2
                    P.wait(S["sx"], load_n[idx])
                    V.wait(S["sx"], load_n[idx])
                    wait_free(V, ("ffc", par))
                    for co in range(KT):
                        bank = co % 2
                        bank_wait(bank)
                        for j in range(JT):
                            ii = P.mm(psum[bank][:, 0:CHA],
                                      fwvb[:, j, co * 128:(co + 1) * 128],
                                      kfw[par][:, j, :],
                                      start=(j == 0), stop=(j == JT - 1))
                        S["smm"].bump(ii, 1)
                        V.wait(S["smm"])
                        iv = V.tt(ffc[par][:, co, :], sr2w[par][:, co, :],
                                  psum[bank][:, 0:CHA], ALU.mult)
                        S["sdv"].bump(iv, 1)
                        bank_set(bank, S["sdv"])
                    V.fence(S["sdv"])
                    ffc_n[idx] = S["sdv"].n
                    consumed(("kfw", par), S["smm"])
                    consumed(("sr2w", par), S["sdv"])

                def a2b_back2a(idx):
                    b, ci = chunks[idx]
                    par = idx % 2
                    t0 = ci * CHA
                    sb = par * 16
                    P.wait(S["sdv"], ffc_n[idx])
                    transpose_add(ffc[par], xa3[par], NTA)
                    consumed(("ffc", par), S["str_"])
                    for e in (A, G):
                        e.wait(S["sad"])
                    ln_stats(xa3[par], NTA, par)
                    wait_free(V, ("lnb", par))
                    ln_norm(lnb[par], xa3[par], NTA, par)
                    consumed(("xa3", par), S["snr"])
                    V.wait(S["sx"], load_n[idx])
                    V.tt(lnb[par][:], lnb[par][:], xr[par][:], ALU.add)
                    V.fence(S["snr"])
                    consumed(("xr", par), S["snr"])
                    SY.wait(S["snr"])
                    SY.go(S_Y[b, t0:t0 + CHA, :]
                          .rearrange("(n p) c -> p n c", p=128), lnb[par][:],
                          sem=S["sou"])
                    for e in (A, G):
                        e.wait(S["snr"])
                    ln_stats(lnb[par], NTA, par)

                def a2b_back2b(idx):
                    b, ci = chunks[idx]
                    par = idx % 2
                    t0 = ci * CHA
                    tg = b * L + t0
                    cg = b * NCA + ci
                    sb = par * 16
                    wait_free(V, ("gbuf", par))
                    ln_norm(gbuf[par], lnb[par], NTA, par)
                    consumed(("lnb", par), S["snr"])
                    SY.wait(S["snr"])
                    SY.go(S_G32[b, t0:t0 + CHA, :]
                          .rearrange("(n p) c -> p n c", p=128),
                          gbuf[par][:], sem=S["sou"])
                    P.wait(S["snr"])
                    wait_free(A, ("gbfo", 0))
                    transpose_affine(gbuf[par], gbfo[par], ("gbfo", 0),
                                     None, None,
                                     accum_slot=lambda kk, _cg=cg:
                                     pacc[:, kk, _cg:_cg + 1])
                    A.fence(S["saf"])
                    consumed(("gbuf", par), S["saf"])
                    SY.wait(S["saf"])
                    SY.go(gbf_r[:, :, tg:tg + CHA], gbfo[par][:], sem=S["sou"])
                    consumed(("gbfo", 0), S["sou"])

                a2b_load(0)
                a2b_load(1)
                a2b_back1(0)
                for idx in range(1, len(chunks)):
                    a2b_back1(idx)
                    if idx >= 2:
                        a2b_back2b(idx - 2)
                    a2b_back2a(idx - 1)
                    if idx + 1 < len(chunks):
                        a2b_load(idx + 1)
                a2b_back2b(len(chunks) - 2)
                a2b_back2a(len(chunks) - 1)
                a2b_back2b(len(chunks) - 1)
            barrier()

        if "R" in phases:
            sc_R = st.enter_context(nc.named_scope("PH_R"))
            with contextlib.ExitStack() as sa:
                EA = sa.enter_context
                W1s = EA(nc.sbuf_tensor("W1s", [128, KT, 256], F32))
                W2s = EA(nc.sbuf_tensor("W2s", [128, 2, NZ], F32))
                b1bc = EA(nc.sbuf_tensor("b1bc", [2, 256], F32))
                b2bc = EA(nc.sbuf_tensor("b2bc", [2, NZ], F32))
                pl2 = EA(nc.sbuf_tensor("pl2", [128, KT, 2], F32))
                hrt = EA(nc.sbuf_tensor("hrt", [2, 256], F32))
                hrtT = EA(nc.sbuf_tensor("hrtT", [128, 2, 2], F32))
                rsc = EA(nc.sbuf_tensor("rsc", [2, 64], F32))
                wzrow = EA(nc.sbuf_tensor("wzrow", [1, 16], F32))

                SY.go(W1s[:], wn["W1"].rearrange("(k p) h -> p k h", p=128),
                      sem=S["srt"])
                SY.go(W2s[:], wn["W2"].rearrange("(k p) z -> p k z", p=128),
                      sem=S["srt"])
                for prt in range(2):
                    SY.go(b1bc[prt:prt + 1, :], wn["b1"].rearrange("(o h) -> o h", o=1),
                          sem=S["srt"])
                    SY.go(b2bc[prt:prt + 1, :], wn["b2"].rearrange("(o z) -> o z", o=1),
                          sem=S["srt"])
                for e in (V, A, G, P):
                    e.wait(S["srt"])
                for b in range(NB):
                    V.red(pl2[:, :, b:b + 1],
                          pacc[:, :, b * NCA:(b + 1) * NCA], ALU.add)
                V.ts(pl2[:], pl2[:], 1.0 / L, ALU.mult)
                V.fence(S["sst"])
                P.wait(S["sst"])
                bank_wait(6)
                for kk in range(KT):
                    ii = P.mm(psum[6][0:2, 0:256], pl2[:, kk, :], W1s[:, kk, :],
                              start=(kk == 0), stop=(kk == KT - 1))
                S["smm"].bump(ii, 1)
                V.wait(S["smm"])
                V.tt(hrt[:], psum[6][0:2, 0:256], b1bc[:], ALU.add)
                V.fence(S["sdv"])
                bank_set(6, S["sdv"])
                A.wait(S["sdv"])
                A.act(hrt[:], hrt[:], AF.Tanh)
                A.fence(S["sek"])
                P.wait(S["sek"])
                bank_wait(4)
                for kk in range(2):
                    ii = P.tr(psum[4][:, kk * 2:kk * 2 + 2],
                              hrt[0:2, kk * 128:(kk + 1) * 128], idn[0:2, 0:2])
                S["str_"].bump(ii, 1)
                A.wait(S["str_"])
                A.act(hrtT[:, 0, :], psum[4][:, 0:2], AF.Copy)
                A.act(hrtT[:, 1, :], psum[4][:, 2:4], AF.Copy)
                A.fence(S["sek"])
                bank_set(4, S["sek"])
                P.wait(S["sek"])
                for kk in range(2):
                    ii = P.mm(psum[6][0:2, 256:256 + NZ], hrtT[:, kk, :],
                              W2s[:, kk, :], start=(kk == 0), stop=(kk == 1))
                S["smm"].bump(ii, 1)
                V.wait(S["smm"])
                lg = rsc[:, 0:8]
                V.tt(lg, psum[6][0:2, 256:256 + NZ], b2bc[:], ALU.add)
                V.red(rsc[:, 8:9], lg, ALU.max)
                V.ts(rsc[:, 8:9], rsc[:, 8:9], -1.0, ALU.mult)
                V.fence(S["sdv"])
                bank_set(6, S["sdv"])
                A.wait(S["sdv"])
                A.act(lg, lg, AF.Exp, bias=rsc[:, 8:9])
                A.fence(S["sek"])
                V.wait(S["sek"])
                V.red(rsc[:, 9:10], lg, ALU.add)
                V.recip(rsc[:, 9:10], rsc[:, 9:10])
                V.ts(lg, lg, rsc[:, 9:10], ALU.mult)
                V.red(rsc[:, 18:19], lg, ALU.max)
                V.ts(rsc[:, 40:48], lg, rsc[:, 18:19], ALU.is_ge)
                V.stt(rsc[:, 10:18], rsc[:, 40:48], -1e30, lg, ALU.mult, ALU.add)
                V.red(rsc[:, 19:20], rsc[:, 10:18], ALU.max)
                V.ts(rsc[:, 48:56], rsc[:, 10:18], rsc[:, 19:20], ALU.is_ge)
                V.stt(rsc[:, 26:34], rsc[:, 48:56], -1e30, rsc[:, 10:18],
                      ALU.mult, ALU.add)
                V.red(rsc[:, 20:21], rsc[:, 26:34], ALU.max)
                V.ts(rsc[:, 56:64], rsc[:, 26:34], rsc[:, 20:21], ALU.is_ge)
                V.tt(rsc[:, 35:36], rsc[:, 18:19], rsc[:, 19:20], ALU.add)
                V.tt(rsc[:, 35:36], rsc[:, 35:36], rsc[:, 20:21], ALU.add)
                V.recip(rsc[:, 36:37], rsc[:, 35:36])
                for r_ in range(3):
                    V.ts(rsc[:, 24 + r_:25 + r_], rsc[:, 18 + r_:19 + r_],
                         rsc[:, 36:37], ALU.mult)
                for r_ in range(3):
                    V.tt(rsc[:, 10:18], rsc[:, 40 + 8 * r_:48 + 8 * r_],
                         iota8f[:], ALU.mult)
                    V.red(rsc[:, 21 + r_:22 + r_], rsc[:, 10:18], ALU.add)
                V.fence(S["sdv"])
                SY.wait(S["sdv"])
                SY.go(wzrow[0:1, 0:3], rsc[0:1, 24:27], sem=S["srt"])
                SY.go(wzrow[0:1, 3:6], rsc[1:2, 24:27], sem=S["srt"])
                SY.go(wzrow[0:1, 8:11], rsc[0:1, 21:24], sem=S["srt"])
                SY.go(wzrow[0:1, 11:14], rsc[1:2, 21:24], sem=S["srt"])
                P.wait(S["srt"])
                bank_wait(6)
                for be in list(range(6)) + list(range(8, 14)):
                    ii = P.mm(psum[6][:, be:be + 1], onesr[0:1, :],
                              wzrow[0:1, be:be + 1], start=True, stop=True)
                S["smm"].bump(ii, 1)
                A.wait(S["smm"])
                A.act(wzbc[:], psum[6][:, 0:16], AF.Copy)
                A.fence(S["sek"])
                bank_set(6, S["sek"])
                G.wait(S["sek"])
                for ps_ in range(6):
                    ecol = wzbc[:, 8 + ps_:9 + ps_]
                    G.ts(rsF[:, 0:1], ecol, 1024.0, ALU.mult)
                    G.ts(idxTf[:, 0:8], jjoff_f[:, 0:8], rsF[:, 0:1], ALU.add)
                    G.copy(idxAi[:, 8 * ps_:8 * (ps_ + 1)], idxTf[:, 0:8])
                    G.ts(rsF[:, 0:1], ecol, 4096.0, ALU.mult)
                    G.ts(idxTf[:], jjoff_f[:], rsF[:, 0:1], ALU.add)
                    G.copy(idxBi[:, 32 * ps_:32 * (ps_ + 1)], idxTf[:])
                G.fence(S["ssc"])
            barrier()

        if "B" in phases:
            sc_B = st.enter_context(nc.named_scope("PH_B"))
            with contextlib.ExitStack() as sa:
                EA = sa.enter_context
                wab = EA(nc.sbuf_tensor("wab", [128, KT, DFF], BF16))
                wbb = EA(nc.sbuf_tensor("wbb", [128, JT, D], BF16))
                gw = [EA(nc.sbuf_tensor(f"gw{i}", [128, KT, TB], BF16))
                      for i in range(2)]
                zb = EA(nc.sbuf_tensor("zb", [128, JT, TB], BF16))
                zcb = EA(nc.sbuf_tensor("zcb", [128, KT, TB], F32))

                WaF = wn["Wa"].rearrange("e r c -> (e r) c")
                WbF = wn["Wb"].rearrange("e r c -> (e r) c")
                NPASS = NB * 3
                NCBP = L // TB

                def load_wa(p):
                    for kk in range(KT):
                        gi = GD.e.indirect_dma_start(
                            wab[:, kk, :], None, WaF,
                            IndirectOffsetOnAxis(
                                idxAi[:, 8 * p + kk:8 * p + kk + 1], axis=0))
                        S["sbw"].bump(gi, 16)
                    return S["sbw"].n

                def load_wb(p):
                    for jj in range(JT):
                        gi = GD.e.indirect_dma_start(
                            wbb[:, jj, :], None, WbF,
                            IndirectOffsetOnAxis(
                                idxBi[:, 32 * p + jj:32 * p + jj + 1], axis=0))
                        S["sbw"].bump(gi, 16)
                    return S["sbw"].n

                wa_ready = load_wa(0)
                wb_ready = load_wb(0)

                m2_done = None
                for p in range(NPASS):
                    b = p // 3
                    r = p % 3
                    for ch in range(NCBP):
                        tg = b * L + ch * TB
                        par = ch % 2
                        wait_free(SY, ("gw", par))
                        SY.go(gw[par][:], gbf_r[:, :, tg:tg + TB], sem=S["sbg"])
                        P.wait(S["sbg"])
                        if ch == 0:
                            P.wait(S["sbw"], wa_ready)
                        if m2_done is not None:
                            A.e.wait_ge(m2_done[0].h, m2_done[1])
                        for j in range(JT):
                            bank = j % 2
                            bank_wait(bank)
                            for kk in range(KT):
                                ii = P.mm(psum[bank][:, 0:TB],
                                          wab[:, kk, j * 128:(j + 1) * 128],
                                          gw[par][:, kk, :],
                                          start=(kk == 0), stop=(kk == KT - 1))
                            S["sbm1"].bump(ii, 1)
                            A.wait(S["sbm1"])
                            ia = A.act(zb[:, j, :], psum[bank][:, 0:TB],
                                       AF.Gelu_apprx_tanh)
                            S["sbz"].bump(ia, 1)
                            bank_set(bank, S["sbz"])
                        consumed(("gw", par), S["sbm1"])
                        if ch == NCBP - 1 and p < NPASS - 1:
                            GD.wait(S["sbm1"])
                            wa_ready = load_wa(p + 1)
                        P.wait(S["sbz"])
                        if p == 0 and ch == 0:
                            P.wait(S["sbw"], wb_ready)
                        for cog in range(4):
                            bk0 = 2 + 2 * (cog % 2)
                            bk1 = 3 + 2 * (cog % 2)
                            bank_wait(bk0)
                            bank_wait(bk1)
                            for j in range(JT):
                                P.mm(psum[bk0][:, 0:TB],
                                     wbb[:, j, (2 * cog) * 128:(2 * cog + 1) * 128],
                                     zb[:, j, :], start=(j == 0), stop=(j == JT - 1))
                                ii = P.mm(psum[bk1][:, 0:TB],
                                          wbb[:, j,
                                              (2 * cog + 1) * 128:(2 * cog + 2) * 128],
                                          zb[:, j, :], start=(j == 0),
                                          stop=(j == JT - 1))
                            S["sbm2"].bump(ii, 1)
                            V.wait(S["sbm2"])
                            wait_free(V, ("zcb", cog))
                            V.ts(zcb[:, 2 * cog, :], psum[bk0][:, 0:TB],
                                 wzbc[:, p:p + 1], ALU.mult)
                            V.ts(zcb[:, 2 * cog + 1, :], psum[bk1][:, 0:TB],
                                 wzbc[:, p:p + 1], ALU.mult)
                            V.fence(S["sbev"])
                            bank_set(bk0, S["sbev"])
                            bank_set(bk1, S["sbev"])
                        m2_done = (S["sbm2"], S["sbm2"].n)
                        GD.wait(S["sbev"])
                        if r > 0:
                            GD.go(zac_r[:, :, tg:tg + TB], zcb[:],
                                  sem=S["sbac"], accum_op=ALU.add)
                        else:
                            GD.go(zac_r[:, :, tg:tg + TB], zcb[:], sem=S["sbac"])
                        for cog in range(4):
                            consumed(("zcb", cog), S["sbac"])
                    if p < NPASS - 1:
                        GD.wait(S["sbm2"])
                        wb_ready = load_wb(p + 1)
            barrier()

        if "C" in phases:
            sc_C = st.enter_context(nc.named_scope("PH_C"))
            with contextlib.ExitStack() as sa:
                EA = sa.enter_context
                zw = [EA(nc.sbuf_tensor(f"zw{i}", [128, KT, TC], F32))
                      for i in range(2)]
                g32w = [EA(nc.sbuf_tensor(f"g32w{i}", [128, NTC, D], F32))
                        for i in range(2)]
                yw = [EA(nc.sbuf_tensor(f"yw{i}", [128, NTC, D], F32))
                      for i in range(2)]
                sqc = EA(nc.sbuf_tensor("sqc", [128, 1, D], F32))
                ow = [EA(nc.sbuf_tensor(f"ow{i}", [128, NTC, D], F32))
                      for i in range(2)]
                stc = EA(nc.sbuf_tensor("stc", [128, NTC, 32], F32))

                c_load_n = {}

                def c_load(ch):
                    tg = ch * TC
                    b = tg // L
                    t0 = tg % L
                    par = ch % 2
                    wait_free(SY, ("zw", par))
                    SY.go(zw[par][:], zac_r[:, :, tg:tg + TC], sem=S["sx"])
                    wait_free(SY, ("g32w", par))
                    SY.go(g32w[par][:], S_G32[b, t0:t0 + TC, :]
                          .rearrange("(n p) c -> p n c", p=128), sem=S["sx"])
                    wait_free(SY, ("yw", par))
                    SY.go(yw[par][:], S_Y[b, t0:t0 + TC, :]
                          .rearrange("(n p) c -> p n c", p=128), sem=S["sx"])
                    c_load_n[ch] = S["sx"].n

                c_load(0)
                if NCC > 1:
                    c_load(1)
                for ch in range(NCC):
                    tg = ch * TC
                    b = tg // L
                    t0 = tg % L
                    par = ch % 2
                    o = par * 16
                    P.wait(S["sx"], c_load_n[ch])
                    V.wait(S["sx"], c_load_n[ch])
                    transpose_add(zw[par], g32w[par], NTC)
                    consumed(("zw", par), S["sad"])
                    for e in (A, G):
                        e.wait(S["sad"])
                    wait_free(A, ("stc", o))
                    for n in range(NTC):
                        A.act(sqc[:, 0, :], g32w[par][:, n, :], AF.Square,
                              accum_out=stc[:, n, o + 1:o + 2])
                        A.act(sqc[:, 0, :], g32w[par][:, n, :], AF.Copy,
                              accum_out=stc[:, n, o + 0:o + 1])
                    A.fence(S["srd"])
                    G.wait(S["srd"])
                    c = 1.0 / D
                    scl = lambda i: stc[:, :, o + i:o + i + 1]
                    G.ts(scl(2), scl(0), c, ALU.mult)
                    G.tt(scl(3), scl(2), scl(2), ALU.mult)
                    G.ts(scl(4), scl(1), c, ALU.mult)
                    G.tt(scl(4), scl(4), scl(3), ALU.subtract)
                    G.fence(S["sst"])
                    A.wait(S["sst"])
                    A.act(scl(5), scl(4), AF.Ln, bias=EPS)
                    A.act(scl(7), scl(5), AF.Exp, scale=-0.5)
                    A.fence(S["sst"])
                    G.wait(S["sst"])
                    G.tt(scl(6), scl(2), scl(7), ALU.mult)
                    G.ts(scl(6), scl(6), -1.0, ALU.mult)
                    G.fence(S["srd"])
                    V.wait(S["srd"])
                    wait_free(V, ("ow", par))
                    for n in range(NTC):
                        V.ts(ow[par][:, n, :], g32w[par][:, n, :],
                             stc[:, n, o + 7:o + 8], ALU.mult,
                             stc[:, n, o + 6:o + 7], ALU.add)
                    V.tt(ow[par][:], ow[par][:], yw[par][:], ALU.add)
                    V.fence(S["snr"])
                    consumed(("stc", o), S["snr"])
                    consumed(("g32w", par), S["snr"])
                    consumed(("yw", par), S["snr"])
                    SY.wait(S["snr"])
                    SY.go(OUT[b, t0:t0 + TC, :]
                          .rearrange("(n p) c -> p n c", p=128),
                          ow[par][:], sem=S["sfin"])
                    consumed(("ow", par), S["sfin"])
                    n_out_dma[0] += 1
                    if ch + 2 < NCC:
                        c_load(ch + 2)
            SY.wait(S["sfin"], n_out_dma[0] * 16)
        else:
            SY.wait(S["sou"], S["sou"].n)

    return nc


def kernel(**inputs):
    xs = np.ascontiguousarray(np.asarray(inputs["L0_attn_x"], dtype=np.float32))
    base = {}
    for k, v in inputs.items():
        if k == "L0_attn_x":
            continue
        base[k] = np.ascontiguousarray(np.asarray(v, dtype=np.float32))
    base["identity"] = np.eye(128, dtype=np.float32)
    in_maps = []
    for c in range(8):
        m = dict(base)
        m["x"] = np.ascontiguousarray(xs[2 * c:2 * c + 2])
        in_maps.append(m)
    nc = build()
    res = run_bass_kernel_spmd(nc, in_maps, core_ids=list(range(8)))
    out = np.concatenate([np.asarray(r["out"], dtype=np.float32)
                          for r in res.results], axis=0)
    return out.reshape(16, L, D)


if __name__ == "__main__":
    nc = build()
    print("instructions:", sum(1 for _ in nc.all_instructions()))


# version 12 (final)
# speedup vs baseline: 1.4205x; 1.0662x over previous
import contextlib
import numpy as np
import concourse.bass as bass
import concourse.mybir as mybir
from concourse.bass_utils import run_bass_kernel_spmd
from concourse.bass import IndirectOffsetOnAxis

F32 = mybir.dt.float32
BF16 = mybir.dt.bfloat16
AF = mybir.ActivationFunctionType
ALU = mybir.AluOpType
AX = mybir.AxisListType

NB = 2
L = 2048
D = 1024
KT = 8
DFF = 4096
JT = 32
CHA = 256
NCA = L // CHA
NTA = CHA // 128
TB = 512
NCB = NB * L // TB
TC = 512
NCC = NB * L // TC
NTC = TC // 128
EPS = 1e-5
NZ = 8


def _names(args):
    s = set()
    for ap in args:
        if ap is None or isinstance(ap, (int, float)):
            continue
        t = getattr(ap, "tensor", None)
        n = getattr(t, "name", None)
        if n is not None:
            s.add(n)
    return s


class Sem:
    def __init__(self, nc, stack, name):
        self.h = stack.enter_context(nc.semaphore(name))
        self.n = 0

    def bump(self, instr, k=1):
        instr.then_inc(self.h, k)
        self.n += k
        return self.n


class Eng:

    def __init__(self, eng):
        self.e = eng
        self.dirty = set()

    def wait(self, sem, n=None):
        self.e.wait_ge(sem.h, sem.n if n is None else n)

    def drain(self):
        self.e.drain()
        self.dirty = set()

    def inc(self, sem, k=1):
        self.e.sem_inc(sem.h, k)
        sem.n += k

    def fence(self, sem):
        self.drain()
        self.inc(sem)

    def _pre(self, outs, ins):
        if (_names(outs) | _names(ins)) & self.dirty:
            self.e.drain()
            self.dirty = set()
        self.dirty |= _names(outs)

    def copy(self, out, in_):
        self._pre([out], [in_])
        return self.e.tensor_copy(out, in_)

    def memset(self, out, v):
        self._pre([out], [])
        return self.e.memset(out, v)

    def tt(self, out, a, b, op):
        self._pre([out], [a, b])
        return self.e.tensor_tensor(out, a, b, op)

    def ts(self, out, a, s1, op0, s2=None, op1=ALU.bypass, accum_out=None):
        self._pre([out, accum_out], [a, s1, s2])
        return self.e.tensor_scalar(out, a, s1, s2, op0, op1, accum_out)

    def stt(self, out, in0, s, in1, op0, op1):
        self._pre([out], [in0, s, in1])
        return self.e.scalar_tensor_tensor(out, in0, s, in1, op0, op1)

    def scan(self, out, d0, d1, init):
        self._pre([out], [d0, d1, init])
        return self.e.tensor_tensor_scan(out, d0, d1, init, ALU.mult, ALU.add)

    def red(self, out, in_, op):
        self._pre([out], [in_])
        return self.e.tensor_reduce(out, in_, AX.X, op)

    def act(self, out, in_, func, bias=0.0, scale=1.0, accum_out=None):
        self._pre([out, accum_out], [in_, bias, scale])
        return self.e.activation(out, in_, func, bias=bias, scale=scale,
                                 accum_out=accum_out)

    def recip(self, out, in_):
        self._pre([out], [in_])
        return self.e.reciprocal(out, in_)


class PEW:
    def __init__(self, eng):
        self.e = eng

    def wait(self, sem, n=None):
        self.e.wait_ge(sem.h, sem.n if n is None else n)

    def inc(self, sem, k=1):
        self.e.sem_inc(sem.h, k)
        sem.n += k

    def mm(self, out, lhsT, rhs, start, stop):
        return self.e.matmul(out, lhsT, rhs, start=start, stop=stop)

    def tr(self, out, in_, idn):
        return self.e.transpose(out, in_, idn)


class DMAQ:
    def __init__(self, eng):
        self.e = eng

    def wait(self, sem, n=None):
        self.e.wait_ge(sem.h, sem.n if n is None else n)

    def inc(self, sem, k=1):
        self.e.sem_inc(sem.h, k)
        sem.n += k

    def go(self, out, in_, sem=None, **kw):
        i = self.e.dma_start(out=out, in_=in_, **kw)
        if sem is not None:
            sem.bump(i, 16)
        return i


def build(phases=("A1", "A2a", "A2b", "R", "B", "C"), debug=False):
    nc = bass.Bass("TRN2", target_bir_lowering=False)

    X = nc.dram_tensor("x", [NB, L, D], F32, kind="ExternalInput").ap()
    OUT = nc.dram_tensor("out", [NB, L, D], F32, kind="ExternalOutput").ap()
    IDN = nc.dram_tensor("identity", [128, 128], F32, kind="ExternalInput").ap()

    wn = {}
    for nm, shp in [
        ("ln_att_g", [D]), ("ln_att_b", [D]), ("ln_ffn_g", [D]), ("ln_ffn_b", [D]),
        ("att_time_decay", [D]), ("att_time_first", [D]),
        ("att_mix_k", [D]), ("att_mix_v", [D]), ("att_mix_r", [D]),
        ("att_wk", [D, D]), ("att_wv", [D, D]), ("att_wr", [D, D]), ("att_wo", [D, D]),
        ("ffn_mix_k", [D]), ("ffn_mix_r", [D]),
        ("ffn_wk", [D, DFF]), ("ffn_wv", [DFF, D]), ("ffn_wr", [D, D]),
    ]:
        wn[nm] = nc.dram_tensor("L0_attn_" + nm, shp, F32, kind="ExternalInput").ap()
    for nm, shp in [("W1", [D, 256]), ("b1", [256]), ("W2", [256, NZ]), ("b2", [NZ]),
                    ("Wa", [NZ, D, DFF]), ("Wb", [NZ, DFF, D])]:
        wn[nm] = nc.dram_tensor("L0_ffn_" + nm, shp, F32, kind="ExternalInput").ap()

    skind = "ExternalOutput" if debug else "Internal"
    S_XPP = nc.dram_tensor("s_xpp", [NB, L, D], F32, kind=skind).ap()
    S_Y = nc.dram_tensor("s_y", [NB, L, D], F32, kind=skind).ap()
    S_G32 = nc.dram_tensor("s_g32", [NB, L, D], F32, kind=skind).ap()
    S_GBF = nc.dram_tensor("s_gbf", [D, NB * L], BF16, kind=skind).ap()
    S_KF = nc.dram_tensor("s_kf", [DFF, NB * L], BF16, kind=skind).ap()
    S_SR2 = nc.dram_tensor("s_sr2", [D, NB * L], F32, kind=skind).ap()
    S_ZAC = nc.dram_tensor("s_zac", [D, NB * L], F32, kind=skind).ap()

    gbf_r = S_GBF.rearrange("(k p) t -> p k t", p=128)
    kf_r = S_KF.rearrange("(j p) t -> p j t", p=128)
    sr2_r = S_SR2.rearrange("(k p) t -> p k t", p=128)
    zac_r = S_ZAC.rearrange("(k p) t -> p k t", p=128)

    st = contextlib.ExitStack()
    with st:
        E = st.enter_context

        V = Eng(nc.vector)
        A = Eng(nc.scalar)
        G = Eng(nc.gpsimd)
        P = PEW(nc.tensor)
        SY = DMAQ(nc.sync)
        GD = DMAQ(nc.gpsimd)

        semnames = ["ld", "sx", "ssq", "srd", "sst", "snr", "str_", "saf", "smx",
                    "smm", "sek", "sdv", "ssc", "swo", "sad", "sou", "sbw", "sbg",
                    "sbz", "sbm1", "sbm2", "sbev", "sbac", "sfin", "sbar", "srt"]
        S = {n: Sem(nc, st, n) for n in semnames}

        dma_sems = [S[k] for k in ("ld", "sx", "sou", "sbw", "sbg", "sbac",
                                   "srt", "sfin")]

        def barrier():
            for dsem in dma_sems:
                SY.wait(dsem)
            for e in (V, A, G):
                e.fence(S["sbar"])
            P.e.drain()
            P.inc(S["sbar"])
            SY.inc(S["sbar"])
            tgt = S["sbar"].n
            for e in (V, A, G, P, SY):
                e.wait(S["sbar"], tgt)

        FREE = {}

        def consumed(key, sem):
            FREE[key] = (sem, sem.n)

        def wait_free(eng, key):
            if key in FREE:
                sem, n = FREE[key]
                eng.wait(sem, n)

        BANK = {}

        def bank_wait(i):
            if i in BANK:
                sem, n = BANK[i]
                P.wait(sem, n)

        def bank_set(i, sem):
            BANK[i] = (sem, sem.n)

        idn = E(nc.sbuf_tensor("idn", [128, 128], F32))
        pnames = ["lag", "lab", "lfg", "lfb", "mk", "mv", "mr",
                  "fmk", "fmr", "ew", "eu"]
        PB = {n: E(nc.sbuf_tensor("pp_" + n, [128, KT], F32)) for n in pnames}
        carN = E(nc.sbuf_tensor("carN", [128, NB, KT], F32))
        carQ = E(nc.sbuf_tensor("carQ", [128, NB, KT], F32))
        pacc = E(nc.sbuf_tensor("pacc", [128, KT, 16], F32))
        RNAMES = ["ssum", "ssum2", "snmu", "sex2", "snv", "sL", "srstd",
                  "sMn", "sT", "sU", "sR2", "sR2b", "sM2n", "sIvr"]
        RS = {n: E(nc.sbuf_tensor("r_" + n, [128, NTA, 2], F32)) for n in RNAMES}
        sqA = E(nc.sbuf_tensor("sqA", [128, D], F32))
        wzbc = E(nc.sbuf_tensor("wzbc", [128, 16], F32))
        onesr = E(nc.sbuf_tensor("onesr", [1, 128], F32))
        cst = E(nc.sbuf_tensor("cst", [128, 4], F32))
        I32 = mybir.dt.int32
        iota8f = E(nc.sbuf_tensor("iota8f", [2, 8], F32))
        iota8i = E(nc.sbuf_tensor("iota8i", [2, 8], I32))
        jjoff_i = E(nc.sbuf_tensor("jjoff_i", [128, 32], I32))
        jjoff_f = E(nc.sbuf_tensor("jjoff_f", [128, 32], F32))
        idxAi = E(nc.sbuf_tensor("idxAi", [128, 48], I32))
        idxBi = E(nc.sbuf_tensor("idxBi", [128, 192], I32))
        idxTf = E(nc.sbuf_tensor("idxTf", [128, 32], F32))
        rsF = E(nc.sbuf_tensor("rsF", [128, 1], F32))
        nc.const_aps.register(F32, EPS, cst[:, 0:1])
        nc.const_aps.register(F32, 1.0, cst[:, 1:2])
        nc.const_aps.register(F32, 1.0 + EPS, cst[:, 2:3])
        nc.const_aps.register(F32, -EPS, cst[:, 3:4])
        psum = [E(nc.psum_tensor(f"pb{i}", [128, 512], F32)) for i in range(8)]

        SY.go(idn[:], IDN, sem=S["ld"])
        for nm, src in [("lag", "ln_att_g"), ("lab", "ln_att_b"),
                        ("lfg", "ln_ffn_g"), ("lfb", "ln_ffn_b"),
                        ("mk", "att_mix_k"), ("mv", "att_mix_v"),
                        ("mr", "att_mix_r"), ("fmk", "ffn_mix_k"),
                        ("fmr", "ffn_mix_r"), ("ew", "att_time_decay"),
                        ("eu", "att_time_first")]:
            SY.go(PB[nm][:], wn[src].rearrange("(k p) -> p k", p=128),
                  sem=S["ld"], allow_slow_non_contiguous=True)
        for e in (V, A, G):
            e.wait(S["ld"])
        V.memset(onesr[:], 1.0)
        A.act(PB["ew"][:], PB["ew"][:], AF.Exp)
        A.act(PB["ew"][:], PB["ew"][:], AF.Exp, scale=-1.0)
        A.act(PB["eu"][:], PB["eu"][:], AF.Exp)
        V.fence(S["sst"])
        A.fence(S["sst"])
        G.memset(carN[:], 0.0)
        G.memset(carQ[:], 0.0)
        G.memset(pacc[:], 0.0)
        G.memset(cst[:, 0:1], EPS)
        G.memset(cst[:, 1:2], 1.0)
        G.memset(cst[:, 2:3], 1.0 + EPS)
        G.memset(cst[:, 3:4], -EPS)
        G.e.iota(jjoff_i[:], [[128, 32]], base=0, channel_multiplier=1)
        G.e.iota(iota8i[:], [[1, 8]], base=0, channel_multiplier=0)
        G.drain()
        G.copy(jjoff_f[:], jjoff_i[:])
        G.copy(iota8f[:], iota8i[:])
        G.fence(S["sst"])

        n_out_dma = [0]

        def rsl(nm, n_tiles, par):
            return RS[nm][:, 0:n_tiles, par:par + 1]

        def ln_sums(src, n_tiles, par):
            wait_free(A, ("stats", par))
            wait_free(G, ("stats", par))
            for n in range(n_tiles):
                A.e.activation(sqA[:], src[:, n, :], AF.Square,
                               accum_out=RS["ssum2"][:, n, par:par + 1])
                A.e.activation(sqA[:], src[:, n, :], AF.Copy,
                               accum_out=RS["ssum"][:, n, par:par + 1])
            A.fence(S["srd"])
            G.wait(S["srd"])
            c = 1.0 / D
            G.ts(rsl("snmu", n_tiles, par), rsl("ssum", n_tiles, par),
                 -c, ALU.mult)
            G.ts(rsl("sex2", n_tiles, par), rsl("ssum2", n_tiles, par),
                 c, ALU.mult)
            G.tt(rsl("snv", n_tiles, par), rsl("snmu", n_tiles, par),
                 rsl("snmu", n_tiles, par), ALU.mult)
            G.tt(rsl("snv", n_tiles, par), rsl("snv", n_tiles, par),
                 rsl("sex2", n_tiles, par), ALU.subtract)
            G.fence(S["sst"])
            A.wait(S["sst"])
            A.act(rsl("sL", n_tiles, par), rsl("snv", n_tiles, par),
                  AF.Ln, scale=-1.0, bias=EPS)

        def ln_stats(src, n_tiles, par):
            ln_sums(src, n_tiles, par)
            A.act(rsl("srstd", n_tiles, par), rsl("sL", n_tiles, par),
                  AF.Exp, scale=-0.5)
            A.fence(S["sst"])
            G.wait(S["sst"])
            G.tt(rsl("sMn", n_tiles, par), rsl("snmu", n_tiles, par),
                 rsl("srstd", n_tiles, par), ALU.mult)
            G.fence(S["srd"])

        def ln_stats_fused(src, n_tiles, par):
            ln_sums(src, n_tiles, par)
            A.act(rsl("srstd", n_tiles, par), rsl("sL", n_tiles, par),
                  AF.Exp, scale=-0.5)
            A.act(rsl("sT", n_tiles, par), rsl("sL", n_tiles, par),
                  AF.Exp, scale=-1.0)
            A.act(rsl("sU", n_tiles, par), rsl("sT", n_tiles, par),
                  AF.Ln, scale=-EPS, bias=1.0 + EPS)
            A.act(rsl("sIvr", n_tiles, par), rsl("sU", n_tiles, par),
                  AF.Exp, scale=0.5)
            A.act(rsl("sR2b", n_tiles, par), rsl("sU", n_tiles, par),
                  AF.Exp, scale=-0.5)
            A.fence(S["sst"])
            G.wait(S["sst"])
            G.tt(rsl("sR2", n_tiles, par), rsl("srstd", n_tiles, par),
                 rsl("sR2b", n_tiles, par), ALU.mult)
            G.tt(rsl("sM2n", n_tiles, par), rsl("snmu", n_tiles, par),
                 rsl("sR2", n_tiles, par), ALU.mult)
            G.fence(S["srd"])

        def ln_norm(dst, src, n_tiles, par, on=None, scale_nm="srstd",
                    bias_nm="sMn"):
            e = V if on is None else on
            e.wait(S["srd"])
            for n in range(n_tiles):
                if e is A:
                    e.act(dst[:, n, :], src[:, n, :], AF.Identity,
                          scale=RS[scale_nm][:, n, par:par + 1],
                          bias=RS[bias_nm][:, n, par:par + 1])
                else:
                    e.ts(dst[:, n, :], src[:, n, :],
                         RS[scale_nm][:, n, par:par + 1], ALU.mult,
                         RS[bias_nm][:, n, par:par + 1], ALU.add)
            e.fence(S["snr"])
            consumed(("stats", par), S["snr"])

        def transpose_affine(src_tok, dst_ch, par_key, gcol, bcol, accum_slot=None):
            off = 1 if dst_ch.shape[2] == CHA + 1 else 0
            for kk in range(KT):
                bank = 4 + (kk % 2)
                bank_wait(bank)
                for n in range(NTA):
                    ii = P.tr(psum[bank][:, n * 128:(n + 1) * 128],
                              src_tok[:, n, kk * 128:(kk + 1) * 128], idn[:])
                S["str_"].bump(ii, 1)
                A.wait(S["str_"])
                kwargs = {}
                if accum_slot is not None:
                    kwargs["accum_out"] = accum_slot(kk)
                if gcol is not None:
                    ia = A.act(dst_ch[:, kk, off:off + CHA], psum[bank][:, 0:CHA],
                               AF.Identity, scale=gcol[:, kk:kk + 1],
                               bias=bcol[:, kk:kk + 1], **kwargs)
                else:
                    ia = A.act(dst_ch[:, kk, off:off + CHA], psum[bank][:, 0:CHA],
                               AF.Copy, **kwargs)
                S["saf"].bump(ia, 1)
                bank_set(bank, S["saf"])

        def transpose_add(src_ch, dst_tok, n_tiles):
            for n in range(n_tiles):
                bank_wait(4)
                bank_wait(5)
                for kk in range(KT):
                    bank = 4 + kk // 4
                    ii = P.tr(psum[bank][:, (kk % 4) * 128:(kk % 4 + 1) * 128],
                              src_ch[:, kk, n * 128:(n + 1) * 128], idn[:])
                    if kk == 7:
                        S["str_"].bump(ii, 1)
                V.wait(S["str_"])
                V.tt(dst_tok[:, n, 0:512], dst_tok[:, n, 0:512],
                     psum[4][:, 0:512], ALU.add)
                V.tt(dst_tok[:, n, 512:1024], dst_tok[:, n, 512:1024],
                     psum[5][:, 0:512], ALU.add)
                V.fence(S["sad"])
                bank_set(4, S["sad"])
                bank_set(5, S["sad"])

        barrier()

        if "A1" in phases:
            sc_A1 = st.enter_context(nc.named_scope("PH_A1"))
            with contextlib.ExitStack() as sa:
                EA = sa.enter_context
                wkb = EA(nc.sbuf_tensor("wkb", [128, KT, D], BF16))
                wvb = EA(nc.sbuf_tensor("wvb", [128, KT, D], BF16))
                wrb = EA(nc.sbuf_tensor("wrb", [128, KT, D], BF16))
                wob = EA(nc.sbuf_tensor("wob", [128, KT, D], BF16))
                ewb = EA(nc.sbuf_tensor("ewb", [128, KT, CHA], F32))
                ones2 = EA(nc.sbuf_tensor("ones2", [128, CHA], F32))
                xa = [EA(nc.sbuf_tensor(f"xa{i}", [128, NTA, D], F32)) for i in range(2)]
                hch = [EA(nc.sbuf_tensor(f"hch{i}", [128, KT, CHA + 1], F32))
                       for i in range(2)]
                dbf = EA(nc.sbuf_tensor("dbf", [128, KT, CHA], F32))
                mixb = {m: [EA(nc.sbuf_tensor(f"mx{m}{i}", [128, KT, CHA], BF16))
                            for i in range(2)] for m in "kvr"}
                eKb2 = [[EA(nc.sbuf_tensor(f"eK{s}_{i}", [128, CHA], F32))
                         for i in range(3)] for s in range(2)]
                erb2 = [[EA(nc.sbuf_tensor(f"er{s}_{i}", [128, CHA], F32))
                         for i in range(3)] for s in range(2)]
                vvb2 = [[EA(nc.sbuf_tensor(f"vv{s}_{i}", [128, CHA], F32))
                         for i in range(3)] for s in range(2)]
                ekvb2 = [[EA(nc.sbuf_tensor(f"ekv{s}_{i}", [128, CHA], F32))
                          for i in range(3)] for s in range(2)]
                NNb2 = [[EA(nc.sbuf_tensor(f"NN{s}_{i}", [128, CHA + 1], F32))
                         for i in range(3)] for s in range(2)]
                QQb2 = [[EA(nc.sbuf_tensor(f"QQ{s}_{i}", [128, CHA + 1], F32))
                         for i in range(3)] for s in range(2)]
                rwk = [EA(nc.sbuf_tensor(f"rwk{i}", [128, KT, CHA], BF16))
                       for i in range(2)]
                attc = EA(nc.sbuf_tensor("attc", [128, KT, CHA], F32))

                for buf, src in [(wkb, "att_wk"), (wvb, "att_wv"),
                                 (wrb, "att_wr"), (wob, "att_wo")]:
                    GD.go(buf[:], wn[src].rearrange("(k p) c -> p k c", p=128),
                          sem=S["sbw"])
                V.memset(ones2[:], 1.0)
                V.fence(S["sst"])
                A.wait(S["sst"])
                for kk in range(KT):
                    A.act(ewb[:, kk, :], ones2[:], AF.Copy,
                          scale=PB["ew"][:, kk:kk + 1])
                A.fence(S["sst"])
                for e in (V, A, G, P):
                    e.wait(S["sbw"])

                chunks = [(b, ci) for ci in range(NCA) for b in range(NB)]
                load_n = {}
                mix_n = {}
                snr_n = {}
                rwk_n = {}

                def a1_load(idx):
                    b, ci = chunks[idx]
                    par = b
                    t0 = ci * CHA
                    wait_free(SY, ("xa", par))
                    SY.go(xa[par][:], X[b, t0:t0 + CHA, :]
                          .rearrange("(n p) c -> p n c", p=128), sem=S["sx"])
                    load_n[idx] = S["sx"].n

                def a1_front(idx):
                    b, ci = chunks[idx]
                    par = b
                    A.wait(S["sx"], load_n[idx])
                    G.wait(S["sx"], load_n[idx])
                    ln_stats_fused(xa[par], NTA, par)
                    eng = A
                    eng.wait(S["srd"])
                    for n in range(NTA):
                        A.act(xa[par][:, n, :], xa[par][:, n, :],
                              AF.Identity,
                              scale=RS["sR2"][:, n, par:par + 1],
                              bias=RS["sM2n"][:, n, par:par + 1])
                    eng.fence(S["snr"])
                    snr_n[idx] = S["snr"].n
                    wait_free(G, ("hchcol", par))
                    if ci == 0:
                        G.memset(hch[par][:, :, 0:1], 0.0)
                    else:
                        G.copy(hch[par][:, :, 0:1],
                               hch[par][:, :, CHA:CHA + 1])
                    G.fence(S["ssq"])
                    P.wait(S["snr"], snr_n[idx])
                    P.wait(S["ssq"])
                    wait_free(A, ("hch", par))
                    transpose_affine(xa[par], hch[par], ("hch", par),
                                     PB["lag"], PB["lab"])
                    A.fence(S["saf"])
                    h1 = hch[par][:, :, 1:CHA + 1]
                    h0 = hch[par][:, :, 0:CHA]
                    V.wait(S["saf"])
                    wait_free(V, ("dbf",))
                    V.tt(dbf[:], h1, h0, ALU.subtract)
                    for m, mixp in [("k", "mk"), ("v", "mv"), ("r", "mr")]:
                        wait_free(V, ("mix" + m, par))
                        for kk in range(KT):
                            V.stt(mixb[m][par][:, kk, :], dbf[:, kk, :],
                                  PB[mixp][:, kk:kk + 1], h0[:, kk, :],
                                  ALU.mult, ALU.add)
                    V.fence(S["smx"])
                    mix_n[idx] = S["smx"].n
                    consumed(("dbf",), S["smx"])
                    consumed(("hchcol", par), S["smx"])
                    consumed(("hch", par), S["smx"])

                def a1_back_kvr(idx):
                    b, ci = chunks[idx]
                    par = b
                    if ci == 0:
                        V.memset(carN[:, b, :], 0.0)
                        V.memset(carQ[:, b, :], 0.0)
                    eKb, erb, vvb = eKb2[par], erb2[par], vvb2[par]
                    ekvb, NNb, QQb = ekvb2[par], NNb2[par], QQb2[par]
                    P.wait(S["smx"], mix_n[idx])
                    ev_n = {}
                    v1_n = {}
                    g2_n = {}
                    a3_n = {}

                    def a1_mm(j):
                        for gi, m in enumerate(("k", "v", "r")):
                            bank = (j * 3 + gi) % 6
                            bank = bank if bank < 4 else bank + 2
                            bank_wait(bank)
                            wsb = {"k": wkb, "v": wvb, "r": wrb}[m]
                            for kk in range(KT):
                                ii = P.mm(psum[bank][:, 0:CHA],
                                          wsb[:, kk, j * 128:(j + 1) * 128],
                                          mixb[m][par][:, kk, :],
                                          start=(kk == 0), stop=(kk == KT - 1))
                            S["smm"].bump(ii, 1)

                    def a1_ev(j):
                        jp = j % 3
                        A.wait(S["smm"])
                        wait_free(A, ("wkv", par, jp))
                        bks = []
                        for gi in range(3):
                            bank = (j * 3 + gi) % 6
                            bks.append(bank if bank < 4 else bank + 2)
                        A.act(eKb[jp][:], psum[bks[0]][:, 0:CHA], AF.Exp)
                        A.act(vvb[jp][:], psum[bks[1]][:, 0:CHA], AF.Copy)
                        ia = A.act(erb[jp][:], psum[bks[2]][:, 0:CHA],
                                   AF.Exp, scale=-1.0)
                        S["sek"].bump(ia, 1)
                        ev_n[j] = S["sek"].n
                        for bank in bks:
                            bank_set(bank, S["sek"])

                    def a1_g1(j):
                        jp = j % 3
                        G.wait(S["sek"], ev_n[j])
                        ig = G.tt(ekvb[jp][:], eKb[jp][:], vvb[jp][:], ALU.mult)
                        S["ssc"].bump(ig, 1)

                    def a1_v(j):
                        jp = j % 3
                        V.wait(S["ssc"])
                        if j == 0:
                            wait_free(V, ("rwk", par))
                        V.copy(NNb[jp][:, 0:1], carN[:, b, j:j + 1])
                        V.copy(QQb[jp][:, 0:1], carQ[:, b, j:j + 1])
                        V.scan(NNb[jp][:, 1:CHA + 1], ewb[:, j, :],
                               ekvb[jp][:], carN[:, b, j:j + 1])
                        V.scan(QQb[jp][:, 1:CHA + 1], ewb[:, j, :],
                               eKb[jp][:], carQ[:, b, j:j + 1])
                        V.copy(carN[:, b, j:j + 1], NNb[jp][:, CHA:CHA + 1])
                        V.copy(carQ[:, b, j:j + 1], QQb[jp][:, CHA:CHA + 1])
                        V.stt(ekvb[jp][:], ekvb[jp][:], PB["eu"][:, j:j + 1],
                              NNb[jp][:, 0:CHA], ALU.mult, ALU.add)
                        V.stt(eKb[jp][:], eKb[jp][:],
                              PB["eu"][:, j:j + 1],
                              QQb[jp][:, 0:CHA], ALU.mult, ALU.add)
                        V.stt(eKb[jp][:], erb[jp][:], 1.0, eKb[jp][:],
                              ALU.add, ALU.mult)
                        V.recip(eKb[jp][:], eKb[jp][:])
                        iv = V.tt(rwk[par][:, j, :], ekvb[jp][:], eKb[jp][:],
                                  ALU.mult)
                        S["sdv"].bump(iv, 1)
                        consumed(("wkv", par, jp), S["sdv"])

                    for j in range(KT):
                        a1_mm(j)
                        a1_ev(j)
                        a1_g1(j)
                        a1_v(j)
                    rwk_n[idx] = S["sdv"].n
                    consumed(("mixk", par), S["smm"])
                    consumed(("mixv", par), S["smm"])
                    consumed(("mixr", par), S["smm"])

                def a1_back_wo(idx):
                    b, ci = chunks[idx]
                    par = b
                    t0 = ci * CHA
                    P.wait(S["sdv"], rwk_n[idx])
                    wait_free(A, ("attc",))
                    for co in range(KT):
                        bank = co % 2
                        bank_wait(bank)
                        for kk in range(KT):
                            ii = P.mm(psum[bank][:, 0:CHA],
                                      wob[:, kk, co * 128:(co + 1) * 128],
                                      rwk[par][:, kk, :],
                                      start=(kk == 0), stop=(kk == KT - 1))
                        S["swo"].bump(ii, 1)
                        A.wait(S["swo"])
                        ia = A.act(attc[:, co, :], psum[bank][:, 0:CHA], AF.Copy)
                        S["sek"].bump(ia, 1)
                        bank_set(bank, S["sek"])
                    consumed(("rwk", par), S["swo"])
                    P.wait(S["sek"])
                    V.wait(S["snr"], snr_n[idx])
                    for n in range(NTA):
                        bank_wait(4)
                        bank_wait(5)
                        for kk in range(KT):
                            bank = 4 + kk // 4
                            ii = P.tr(psum[bank][:, (kk % 4) * 128:(kk % 4 + 1) * 128],
                                      attc[:, kk, n * 128:(n + 1) * 128], idn[:])
                            if kk == 7:
                                S["str_"].bump(ii, 1)
                        V.wait(S["str_"])
                        V.stt(xa[par][:, n, 0:512], xa[par][:, n, 0:512],
                              RS["sIvr"][:, n, par:par + 1],
                              psum[4][:, 0:512], ALU.mult, ALU.add)
                        V.stt(xa[par][:, n, 512:1024], xa[par][:, n, 512:1024],
                              RS["sIvr"][:, n, par:par + 1],
                              psum[5][:, 0:512], ALU.mult, ALU.add)
                        V.fence(S["sad"])
                        bank_set(4, S["sad"])
                        bank_set(5, S["sad"])
                    consumed(("attc",), S["str_"])
                    consumed(("stats", par), S["sad"])
                    GD.wait(S["sad"])
                    GD.go(S_XPP[b, t0:t0 + CHA, :]
                          .rearrange("(n p) c -> p n c", p=128),
                          xa[par][:], sem=S["sou"])
                    consumed(("xa", par), S["sou"])

                a1_load(0)
                a1_load(1)
                a1_front(0)
                a1_back_kvr(0)
                for idx in range(1, len(chunks)):
                    a1_front(idx)
                    a1_back_wo(idx - 1)
                    if idx + 1 < len(chunks):
                        a1_load(idx + 1)
                    a1_back_kvr(idx)
                a1_back_wo(len(chunks) - 1)
            barrier()

        if "A2a" in phases:
            sc_A2a = st.enter_context(nc.named_scope("PH_A2a"))
            with contextlib.ExitStack() as sa:
                EA = sa.enter_context
                fwkb = EA(nc.sbuf_tensor("fwkb", [128, KT, DFF], BF16))
                fwrb = EA(nc.sbuf_tensor("fwrb", [128, KT, D], BF16))
                xa2 = [EA(nc.sbuf_tensor(f"xa2_{i}", [128, NTA, D], F32))
                       for i in range(2)]
                h2pre = EA(nc.sbuf_tensor("h2pre", [128, NTA, D], F32))
                h2ch = [EA(nc.sbuf_tensor(f"h2ch{i}", [128, KT, CHA + 1], F32))
                        for i in range(2)]
                dbf2 = EA(nc.sbuf_tensor("dbf2", [128, KT, CHA], BF16))
                mfk = [EA(nc.sbuf_tensor(f"mfk{i}", [128, KT, CHA], BF16))
                       for i in range(2)]
                mfr = [EA(nc.sbuf_tensor(f"mfr{i}", [128, KT, CHA], BF16))
                       for i in range(2)]
                kfb = [EA(nc.sbuf_tensor(f"kfb{i}", [128, JT, CHA], BF16))
                       for i in range(2)]
                trel = EA(nc.sbuf_tensor("trel", [128, 4, CHA], F32))
                sr2b = [EA(nc.sbuf_tensor(f"sr2b{i}", [128, KT, CHA], F32))
                        for i in range(2)]

                GD.go(fwkb[:], wn["ffn_wk"].rearrange("(k p) c -> p k c", p=128),
                      sem=S["sbw"])
                GD.go(fwrb[:], wn["ffn_wr"].rearrange("(k p) c -> p k c", p=128),
                      sem=S["sbw"])
                for e in (V, A, G, P):
                    e.wait(S["sbw"])

                chunks = [(b, ci) for ci in range(NCA) for b in range(NB)]
                load_n = {}
                mix_n = {}

                def a2a_load(idx):
                    b, ci = chunks[idx]
                    par = b
                    t0 = ci * CHA
                    wait_free(SY, ("xa2", par))
                    SY.go(xa2[par][:], S_XPP[b, t0:t0 + CHA, :]
                          .rearrange("(n p) c -> p n c", p=128), sem=S["sx"])
                    load_n[idx] = S["sx"].n

                def a2a_front(idx):
                    b, ci = chunks[idx]
                    par = b
                    A.wait(S["sx"], load_n[idx])
                    G.wait(S["sx"], load_n[idx])
                    ln_stats(xa2[par], NTA, par)
                    eng = V if par == 0 else A
                    wait_free(eng, ("h2pre",))
                    ln_norm(h2pre, xa2[par], NTA, par, on=eng)
                    consumed(("xa2", par), S["snr"])
                    wait_free(G, ("h2col", par))
                    if ci == 0:
                        G.memset(h2ch[par][:, :, 0:1], 0.0)
                    else:
                        G.copy(h2ch[par][:, :, 0:1],
                               h2ch[par][:, :, CHA:CHA + 1])
                    G.fence(S["ssq"])
                    P.wait(S["snr"])
                    P.wait(S["ssq"])
                    wait_free(A, ("h2ch", par))
                    transpose_affine(h2pre, h2ch[par], ("h2ch", par),
                                     PB["lfg"], PB["lfb"])
                    A.fence(S["saf"])
                    consumed(("h2pre",), S["saf"])
                    h1 = h2ch[par][:, :, 1:CHA + 1]
                    h0 = h2ch[par][:, :, 0:CHA]
                    V.wait(S["saf"])
                    wait_free(V, ("dbf2",))
                    V.tt(dbf2[:], h1, h0, ALU.subtract)
                    wait_free(V, ("mfk", par))
                    for kk in range(KT):
                        V.stt(mfk[par][:, kk, :], dbf2[:, kk, :],
                              PB["fmk"][:, kk:kk + 1], h0[:, kk, :],
                              ALU.mult, ALU.add)
                    wait_free(V, ("mfr", par))
                    for kk in range(KT):
                        V.stt(mfr[par][:, kk, :], dbf2[:, kk, :],
                              PB["fmr"][:, kk:kk + 1], h0[:, kk, :],
                              ALU.mult, ALU.add)
                    V.fence(S["smx"])
                    mix_n[idx] = S["smx"].n
                    consumed(("dbf2",), S["smx"])
                    consumed(("h2col", par), S["smx"])
                    consumed(("h2ch", par), S["smx"])

                def a2a_back_kf(idx):
                    b, ci = chunks[idx]
                    par = b
                    P.wait(S["smx"], mix_n[idx])
                    wait_free(A, ("kfb", par))
                    wait_free(V, ("kfb", par))
                    for j in range(JT):
                        bank = j % 2
                        bank_wait(bank)
                        for kk in range(KT):
                            ii = P.mm(psum[bank][:, 0:CHA],
                                      fwkb[:, kk, j * 128:(j + 1) * 128],
                                      mfk[par][:, kk, :],
                                      start=(kk == 0), stop=(kk == KT - 1))
                        S["smm"].bump(ii, 1)
                        A.wait(S["smm"])
                        if j % 4 == 0:
                            A.wait(S["sdv"])
                        ia = A.act(trel[:, j % 4, :], psum[bank][:, 0:CHA],
                                   AF.Relu)
                        S["sek"].bump(ia, 1)
                        bank_set(bank, S["sek"])
                        V.wait(S["sek"])
                        iv = V.tt(kfb[par][:, j, :], trel[:, j % 4, :],
                                  trel[:, j % 4, :], ALU.mult)
                        if j % 4 == 3 or j == JT - 1:
                            S["sdv"].bump(iv, 1)
                    consumed(("mfk", par), S["smm"])

                def a2a_back_fr(idx):
                    b, ci = chunks[idx]
                    par = b
                    t0 = ci * CHA
                    tg = b * L + t0
                    wait_free(A, ("sr2b", par))
                    for co in range(KT):
                        bank = 2 + co % 2
                        bank_wait(bank)
                        for kk in range(KT):
                            ii = P.mm(psum[bank][:, 0:CHA],
                                      fwrb[:, kk, co * 128:(co + 1) * 128],
                                      mfr[par][:, kk, :],
                                      start=(kk == 0), stop=(kk == KT - 1))
                        S["smm"].bump(ii, 1)
                        A.wait(S["smm"])
                        ia = A.act(sr2b[par][:, co, :], psum[bank][:, 0:CHA],
                                   AF.Exp, scale=-1.0)
                        S["sek"].bump(ia, 1)
                        bank_set(bank, S["sek"])
                    consumed(("mfr", par), S["smm"])
                    A.act(sr2b[par][:], sr2b[par][:], AF.Ln, bias=1.0)
                    A.act(sr2b[par][:], sr2b[par][:], AF.Exp, scale=-1.0)
                    A.fence(S["sek"])
                    GD.wait(S["sek"])
                    GD.wait(S["sdv"])
                    GD.go(kf_r[:, :, tg:tg + CHA], kfb[par][:], sem=S["sou"])
                    GD.go(sr2_r[:, :, tg:tg + CHA], sr2b[par][:], sem=S["sou"])
                    consumed(("kfb", par), S["sou"])
                    consumed(("sr2b", par), S["sou"])

                a2a_load(0)
                a2a_load(1)
                a2a_front(0)
                for idx in range(len(chunks)):
                    a2a_back_kf(idx)
                    if idx + 1 < len(chunks):
                        a2a_front(idx + 1)
                    a2a_back_fr(idx)
                    if idx + 2 < len(chunks):
                        a2a_load(idx + 2)
            barrier()

        if "A2b" in phases:
            sc_A2b = st.enter_context(nc.named_scope("PH_A2b"))
            with contextlib.ExitStack() as sa:
                EA = sa.enter_context
                fwvb = EA(nc.sbuf_tensor("fwvb", [128, JT, D], BF16))
                kfw = [EA(nc.sbuf_tensor(f"kfw{i}", [128, JT, CHA], BF16))
                       for i in range(2)]
                sr2w = [EA(nc.sbuf_tensor(f"sr2w{i}", [128, KT, CHA], F32))
                        for i in range(2)]
                xa3 = [EA(nc.sbuf_tensor(f"xa3_{i}", [128, NTA, D], F32))
                       for i in range(2)]
                xr = [EA(nc.sbuf_tensor(f"xr{i}", [128, NTA, D], F32))
                      for i in range(2)]
                ffc = [EA(nc.sbuf_tensor(f"ffc{i}", [128, KT, CHA], F32))
                       for i in range(2)]
                lnb = [EA(nc.sbuf_tensor(f"lnb{i}", [128, NTA, D], F32))
                       for i in range(2)]
                gbuf = [EA(nc.sbuf_tensor(f"gbuf{i}", [128, NTA, D], F32))
                        for i in range(2)]
                gbfo = [EA(nc.sbuf_tensor("gbfo0", [128, KT, CHA], BF16))] * 2

                GD.go(fwvb[:], wn["ffn_wv"].rearrange("(j p) c -> p j c", p=128),
                      sem=S["sbw"])
                for e in (V, A, G, P):
                    e.wait(S["sbw"])

                chunks = [(b, ci) for b in range(NB) for ci in range(NCA)]
                load_n = {}
                ffc_n = {}

                def a2b_load(idx):
                    b, ci = chunks[idx]
                    par = idx % 2
                    t0 = ci * CHA
                    tg = b * L + t0
                    wait_free(SY, ("kfw", par))
                    SY.go(kfw[par][:], kf_r[:, :, tg:tg + CHA], sem=S["sx"])
                    wait_free(SY, ("sr2w", par))
                    SY.go(sr2w[par][:], sr2_r[:, :, tg:tg + CHA], sem=S["sx"])
                    wait_free(SY, ("xa3", par))
                    SY.go(xa3[par][:], S_XPP[b, t0:t0 + CHA, :]
                          .rearrange("(n p) c -> p n c", p=128), sem=S["sx"])
                    wait_free(SY, ("xr", par))
                    SY.go(xr[par][:], X[b, t0:t0 + CHA, :]
                          .rearrange("(n p) c -> p n c", p=128), sem=S["sx"])
                    load_n[idx] = S["sx"].n

                def a2b_back1(idx):
                    b, ci = chunks[idx]
                    par = idx % 2
                    P.wait(S["sx"], load_n[idx])
                    V.wait(S["sx"], load_n[idx])
                    wait_free(V, ("ffc", par))
                    for co in range(KT):
                        bank = co % 2
                        bank_wait(bank)
                        for j in range(JT):
                            ii = P.mm(psum[bank][:, 0:CHA],
                                      fwvb[:, j, co * 128:(co + 1) * 128],
                                      kfw[par][:, j, :],
                                      start=(j == 0), stop=(j == JT - 1))
                        S["smm"].bump(ii, 1)
                        V.wait(S["smm"])
                        iv = V.tt(ffc[par][:, co, :], sr2w[par][:, co, :],
                                  psum[bank][:, 0:CHA], ALU.mult)
                        S["sdv"].bump(iv, 1)
                        bank_set(bank, S["sdv"])
                    V.fence(S["sdv"])
                    ffc_n[idx] = S["sdv"].n
                    consumed(("kfw", par), S["smm"])
                    consumed(("sr2w", par), S["sdv"])

                def a2b_back2a(idx):
                    b, ci = chunks[idx]
                    par = idx % 2
                    t0 = ci * CHA
                    sb = par * 16
                    P.wait(S["sdv"], ffc_n[idx])
                    transpose_add(ffc[par], xa3[par], NTA)
                    consumed(("ffc", par), S["str_"])
                    for e in (A, G):
                        e.wait(S["sad"])
                    ln_stats(xa3[par], NTA, par)
                    wait_free(V, ("lnb", par))
                    ln_norm(lnb[par], xa3[par], NTA, par)
                    consumed(("xa3", par), S["snr"])
                    V.wait(S["sx"], load_n[idx])
                    V.tt(lnb[par][:], lnb[par][:], xr[par][:], ALU.add)
                    V.fence(S["snr"])
                    consumed(("xr", par), S["snr"])
                    GD.wait(S["snr"])
                    GD.go(S_Y[b, t0:t0 + CHA, :]
                          .rearrange("(n p) c -> p n c", p=128), lnb[par][:],
                          sem=S["sou"])
                    for e in (A, G):
                        e.wait(S["snr"])
                    ln_stats(lnb[par], NTA, par)

                def a2b_back2b(idx):
                    b, ci = chunks[idx]
                    par = idx % 2
                    t0 = ci * CHA
                    tg = b * L + t0
                    cg = b * NCA + ci
                    sb = par * 16
                    wait_free(V, ("gbuf", par))
                    ln_norm(gbuf[par], lnb[par], NTA, par)
                    consumed(("lnb", par), S["snr"])
                    GD.wait(S["snr"])
                    GD.go(S_G32[b, t0:t0 + CHA, :]
                          .rearrange("(n p) c -> p n c", p=128),
                          gbuf[par][:], sem=S["sou"])
                    P.wait(S["snr"])
                    wait_free(A, ("gbfo", 0))
                    transpose_affine(gbuf[par], gbfo[par], ("gbfo", 0),
                                     None, None,
                                     accum_slot=lambda kk, _cg=cg:
                                     pacc[:, kk, _cg:_cg + 1])
                    A.fence(S["saf"])
                    consumed(("gbuf", par), S["saf"])
                    GD.wait(S["saf"])
                    GD.go(gbf_r[:, :, tg:tg + CHA], gbfo[par][:], sem=S["sou"])
                    consumed(("gbfo", 0), S["sou"])

                a2b_load(0)
                a2b_load(1)
                a2b_back1(0)
                for idx in range(1, len(chunks)):
                    a2b_back1(idx)
                    if idx >= 2:
                        a2b_back2b(idx - 2)
                    a2b_back2a(idx - 1)
                    if idx + 1 < len(chunks):
                        a2b_load(idx + 1)
                a2b_back2b(len(chunks) - 2)
                a2b_back2a(len(chunks) - 1)
                a2b_back2b(len(chunks) - 1)
            barrier()

        if "R" in phases:
            sc_R = st.enter_context(nc.named_scope("PH_R"))
            with contextlib.ExitStack() as sa:
                EA = sa.enter_context
                W1s = EA(nc.sbuf_tensor("W1s", [128, KT, 256], F32))
                W2s = EA(nc.sbuf_tensor("W2s", [128, 2, NZ], F32))
                b1bc = EA(nc.sbuf_tensor("b1bc", [2, 256], F32))
                b2bc = EA(nc.sbuf_tensor("b2bc", [2, NZ], F32))
                pl2 = EA(nc.sbuf_tensor("pl2", [128, KT, 2], F32))
                hrt = EA(nc.sbuf_tensor("hrt", [2, 256], F32))
                hrtT = EA(nc.sbuf_tensor("hrtT", [128, 2, 2], F32))
                rsc = EA(nc.sbuf_tensor("rsc", [2, 64], F32))
                wzrow = EA(nc.sbuf_tensor("wzrow", [1, 16], F32))

                SY.go(W1s[:], wn["W1"].rearrange("(k p) h -> p k h", p=128),
                      sem=S["srt"])
                SY.go(W2s[:], wn["W2"].rearrange("(k p) z -> p k z", p=128),
                      sem=S["srt"])
                for prt in range(2):
                    SY.go(b1bc[prt:prt + 1, :], wn["b1"].rearrange("(o h) -> o h", o=1),
                          sem=S["srt"])
                    SY.go(b2bc[prt:prt + 1, :], wn["b2"].rearrange("(o z) -> o z", o=1),
                          sem=S["srt"])
                for e in (V, A, G, P):
                    e.wait(S["srt"])
                for b in range(NB):
                    V.red(pl2[:, :, b:b + 1],
                          pacc[:, :, b * NCA:(b + 1) * NCA], ALU.add)
                V.ts(pl2[:], pl2[:], 1.0 / L, ALU.mult)
                V.fence(S["sst"])
                P.wait(S["sst"])
                bank_wait(6)
                for kk in range(KT):
                    ii = P.mm(psum[6][0:2, 0:256], pl2[:, kk, :], W1s[:, kk, :],
                              start=(kk == 0), stop=(kk == KT - 1))
                S["smm"].bump(ii, 1)
                V.wait(S["smm"])
                V.tt(hrt[:], psum[6][0:2, 0:256], b1bc[:], ALU.add)
                V.fence(S["sdv"])
                bank_set(6, S["sdv"])
                A.wait(S["sdv"])
                A.act(hrt[:], hrt[:], AF.Tanh)
                A.fence(S["sek"])
                P.wait(S["sek"])
                bank_wait(4)
                for kk in range(2):
                    ii = P.tr(psum[4][:, kk * 2:kk * 2 + 2],
                              hrt[0:2, kk * 128:(kk + 1) * 128], idn[0:2, 0:2])
                S["str_"].bump(ii, 1)
                A.wait(S["str_"])
                A.act(hrtT[:, 0, :], psum[4][:, 0:2], AF.Copy)
                A.act(hrtT[:, 1, :], psum[4][:, 2:4], AF.Copy)
                A.fence(S["sek"])
                bank_set(4, S["sek"])
                P.wait(S["sek"])
                for kk in range(2):
                    ii = P.mm(psum[6][0:2, 256:256 + NZ], hrtT[:, kk, :],
                              W2s[:, kk, :], start=(kk == 0), stop=(kk == 1))
                S["smm"].bump(ii, 1)
                V.wait(S["smm"])
                lg = rsc[:, 0:8]
                V.tt(lg, psum[6][0:2, 256:256 + NZ], b2bc[:], ALU.add)
                V.red(rsc[:, 8:9], lg, ALU.max)
                V.ts(rsc[:, 8:9], rsc[:, 8:9], -1.0, ALU.mult)
                V.fence(S["sdv"])
                bank_set(6, S["sdv"])
                A.wait(S["sdv"])
                A.act(lg, lg, AF.Exp, bias=rsc[:, 8:9])
                A.fence(S["sek"])
                V.wait(S["sek"])
                V.red(rsc[:, 9:10], lg, ALU.add)
                V.recip(rsc[:, 9:10], rsc[:, 9:10])
                V.ts(lg, lg, rsc[:, 9:10], ALU.mult)
                V.red(rsc[:, 18:19], lg, ALU.max)
                V.ts(rsc[:, 40:48], lg, rsc[:, 18:19], ALU.is_ge)
                V.stt(rsc[:, 10:18], rsc[:, 40:48], -1e30, lg, ALU.mult, ALU.add)
                V.red(rsc[:, 19:20], rsc[:, 10:18], ALU.max)
                V.ts(rsc[:, 48:56], rsc[:, 10:18], rsc[:, 19:20], ALU.is_ge)
                V.stt(rsc[:, 26:34], rsc[:, 48:56], -1e30, rsc[:, 10:18],
                      ALU.mult, ALU.add)
                V.red(rsc[:, 20:21], rsc[:, 26:34], ALU.max)
                V.ts(rsc[:, 56:64], rsc[:, 26:34], rsc[:, 20:21], ALU.is_ge)
                V.tt(rsc[:, 35:36], rsc[:, 18:19], rsc[:, 19:20], ALU.add)
                V.tt(rsc[:, 35:36], rsc[:, 35:36], rsc[:, 20:21], ALU.add)
                V.recip(rsc[:, 36:37], rsc[:, 35:36])
                for r_ in range(3):
                    V.ts(rsc[:, 24 + r_:25 + r_], rsc[:, 18 + r_:19 + r_],
                         rsc[:, 36:37], ALU.mult)
                for r_ in range(3):
                    V.tt(rsc[:, 10:18], rsc[:, 40 + 8 * r_:48 + 8 * r_],
                         iota8f[:], ALU.mult)
                    V.red(rsc[:, 21 + r_:22 + r_], rsc[:, 10:18], ALU.add)
                V.fence(S["sdv"])
                SY.wait(S["sdv"])
                SY.go(wzrow[0:1, 0:3], rsc[0:1, 24:27], sem=S["srt"])
                SY.go(wzrow[0:1, 3:6], rsc[1:2, 24:27], sem=S["srt"])
                SY.go(wzrow[0:1, 8:11], rsc[0:1, 21:24], sem=S["srt"])
                SY.go(wzrow[0:1, 11:14], rsc[1:2, 21:24], sem=S["srt"])
                P.wait(S["srt"])
                bank_wait(6)
                for be in list(range(6)) + list(range(8, 14)):
                    ii = P.mm(psum[6][:, be:be + 1], onesr[0:1, :],
                              wzrow[0:1, be:be + 1], start=True, stop=True)
                S["smm"].bump(ii, 1)
                A.wait(S["smm"])
                A.act(wzbc[:], psum[6][:, 0:16], AF.Copy)
                A.fence(S["sek"])
                bank_set(6, S["sek"])
                G.wait(S["sek"])
                for ps_ in range(6):
                    ecol = wzbc[:, 8 + ps_:9 + ps_]
                    G.ts(rsF[:, 0:1], ecol, 1024.0, ALU.mult)
                    G.ts(idxTf[:, 0:8], jjoff_f[:, 0:8], rsF[:, 0:1], ALU.add)
                    G.copy(idxAi[:, 8 * ps_:8 * (ps_ + 1)], idxTf[:, 0:8])
                    G.ts(rsF[:, 0:1], ecol, 4096.0, ALU.mult)
                    G.ts(idxTf[:], jjoff_f[:], rsF[:, 0:1], ALU.add)
                    G.copy(idxBi[:, 32 * ps_:32 * (ps_ + 1)], idxTf[:])
                G.fence(S["ssc"])
            barrier()

        if "B" in phases:
            sc_B = st.enter_context(nc.named_scope("PH_B"))
            with contextlib.ExitStack() as sa:
                EA = sa.enter_context
                wab = EA(nc.sbuf_tensor("wab", [128, KT, DFF], BF16))
                wbb = EA(nc.sbuf_tensor("wbb", [128, JT, D], BF16))
                gw = [EA(nc.sbuf_tensor(f"gw{i}", [128, KT, TB], BF16))
                      for i in range(2)]
                zb = EA(nc.sbuf_tensor("zb", [128, JT, TB], BF16))
                zcb = EA(nc.sbuf_tensor("zcb", [128, KT, TB], F32))

                WaF = wn["Wa"].rearrange("e r c -> (e r) c")
                WbF = wn["Wb"].rearrange("e r c -> (e r) c")
                NPASS = NB * 3
                NCBP = L // TB

                def load_wa(p):
                    for kk in range(KT):
                        gi = GD.e.indirect_dma_start(
                            wab[:, kk, :], None, WaF,
                            IndirectOffsetOnAxis(
                                idxAi[:, 8 * p + kk:8 * p + kk + 1], axis=0))
                        S["sbw"].bump(gi, 16)
                    return S["sbw"].n

                def load_wb(p):
                    for jj in range(JT):
                        gi = GD.e.indirect_dma_start(
                            wbb[:, jj, :], None, WbF,
                            IndirectOffsetOnAxis(
                                idxBi[:, 32 * p + jj:32 * p + jj + 1], axis=0))
                        S["sbw"].bump(gi, 16)
                    return S["sbw"].n

                wa_ready = load_wa(0)
                wb_ready = load_wb(0)

                m2_done = None
                for p in range(NPASS):
                    b = p // 3
                    r = p % 3
                    for ch in range(NCBP):
                        tg = b * L + ch * TB
                        par = ch % 2
                        wait_free(SY, ("gw", par))
                        SY.go(gw[par][:], gbf_r[:, :, tg:tg + TB], sem=S["sbg"])
                        P.wait(S["sbg"])
                        if ch == 0:
                            P.wait(S["sbw"], wa_ready)
                        if m2_done is not None:
                            A.e.wait_ge(m2_done[0].h, m2_done[1])
                        for j in range(JT):
                            bank = j % 2
                            bank_wait(bank)
                            for kk in range(KT):
                                ii = P.mm(psum[bank][:, 0:TB],
                                          wab[:, kk, j * 128:(j + 1) * 128],
                                          gw[par][:, kk, :],
                                          start=(kk == 0), stop=(kk == KT - 1))
                            S["sbm1"].bump(ii, 1)
                            A.wait(S["sbm1"])
                            ia = A.act(zb[:, j, :], psum[bank][:, 0:TB],
                                       AF.Gelu_apprx_tanh)
                            S["sbz"].bump(ia, 1)
                            bank_set(bank, S["sbz"])
                        consumed(("gw", par), S["sbm1"])
                        if ch == NCBP - 1 and p < NPASS - 1:
                            GD.wait(S["sbm1"])
                            wa_ready = load_wa(p + 1)
                        P.wait(S["sbz"])
                        if p == 0 and ch == 0:
                            P.wait(S["sbw"], wb_ready)
                        for cog in range(4):
                            bk0 = 2 + 2 * (cog % 2)
                            bk1 = 3 + 2 * (cog % 2)
                            bank_wait(bk0)
                            bank_wait(bk1)
                            for j in range(JT):
                                P.mm(psum[bk0][:, 0:TB],
                                     wbb[:, j, (2 * cog) * 128:(2 * cog + 1) * 128],
                                     zb[:, j, :], start=(j == 0), stop=(j == JT - 1))
                                ii = P.mm(psum[bk1][:, 0:TB],
                                          wbb[:, j,
                                              (2 * cog + 1) * 128:(2 * cog + 2) * 128],
                                          zb[:, j, :], start=(j == 0),
                                          stop=(j == JT - 1))
                            S["sbm2"].bump(ii, 1)
                            V.wait(S["sbm2"])
                            wait_free(V, ("zcb", cog))
                            V.ts(zcb[:, 2 * cog, :], psum[bk0][:, 0:TB],
                                 wzbc[:, p:p + 1], ALU.mult)
                            V.ts(zcb[:, 2 * cog + 1, :], psum[bk1][:, 0:TB],
                                 wzbc[:, p:p + 1], ALU.mult)
                            V.fence(S["sbev"])
                            bank_set(bk0, S["sbev"])
                            bank_set(bk1, S["sbev"])
                        m2_done = (S["sbm2"], S["sbm2"].n)
                        GD.wait(S["sbev"])
                        if r > 0:
                            GD.go(zac_r[:, :, tg:tg + TB], zcb[:],
                                  sem=S["sbac"], accum_op=ALU.add)
                        else:
                            GD.go(zac_r[:, :, tg:tg + TB], zcb[:], sem=S["sbac"])
                        for cog in range(4):
                            consumed(("zcb", cog), S["sbac"])
                    if p < NPASS - 1:
                        GD.wait(S["sbm2"])
                        wb_ready = load_wb(p + 1)
            barrier()

        if "C" in phases:
            sc_C = st.enter_context(nc.named_scope("PH_C"))
            with contextlib.ExitStack() as sa:
                EA = sa.enter_context
                zw = [EA(nc.sbuf_tensor(f"zw{i}", [128, KT, TC], F32))
                      for i in range(2)]
                g32w = [EA(nc.sbuf_tensor(f"g32w{i}", [128, NTC, D], F32))
                        for i in range(2)]
                yw = [EA(nc.sbuf_tensor(f"yw{i}", [128, NTC, D], F32))
                      for i in range(2)]
                sqc = EA(nc.sbuf_tensor("sqc", [128, 1, D], F32))
                ow = [EA(nc.sbuf_tensor(f"ow{i}", [128, NTC, D], F32))
                      for i in range(2)]
                stc = EA(nc.sbuf_tensor("stc", [128, NTC, 32], F32))

                c_load_n = {}

                def c_load(ch):
                    tg = ch * TC
                    b = tg // L
                    t0 = tg % L
                    par = ch % 2
                    wait_free(SY, ("zw", par))
                    SY.go(zw[par][:], zac_r[:, :, tg:tg + TC], sem=S["sx"])
                    wait_free(SY, ("g32w", par))
                    SY.go(g32w[par][:], S_G32[b, t0:t0 + TC, :]
                          .rearrange("(n p) c -> p n c", p=128), sem=S["sx"])
                    wait_free(SY, ("yw", par))
                    SY.go(yw[par][:], S_Y[b, t0:t0 + TC, :]
                          .rearrange("(n p) c -> p n c", p=128), sem=S["sx"])
                    c_load_n[ch] = S["sx"].n

                c_load(0)
                if NCC > 1:
                    c_load(1)
                for ch in range(NCC):
                    tg = ch * TC
                    b = tg // L
                    t0 = tg % L
                    par = ch % 2
                    o = par * 16
                    P.wait(S["sx"], c_load_n[ch])
                    V.wait(S["sx"], c_load_n[ch])
                    transpose_add(zw[par], g32w[par], NTC)
                    consumed(("zw", par), S["sad"])
                    for e in (A, G):
                        e.wait(S["sad"])
                    wait_free(A, ("stc", o))
                    for n in range(NTC):
                        A.act(sqc[:, 0, :], g32w[par][:, n, :], AF.Square,
                              accum_out=stc[:, n, o + 1:o + 2])
                        A.act(sqc[:, 0, :], g32w[par][:, n, :], AF.Copy,
                              accum_out=stc[:, n, o + 0:o + 1])
                    A.fence(S["srd"])
                    G.wait(S["srd"])
                    c = 1.0 / D
                    scl = lambda i: stc[:, :, o + i:o + i + 1]
                    G.ts(scl(2), scl(0), c, ALU.mult)
                    G.tt(scl(3), scl(2), scl(2), ALU.mult)
                    G.ts(scl(4), scl(1), c, ALU.mult)
                    G.tt(scl(4), scl(4), scl(3), ALU.subtract)
                    G.fence(S["sst"])
                    A.wait(S["sst"])
                    A.act(scl(5), scl(4), AF.Ln, bias=EPS)
                    A.act(scl(7), scl(5), AF.Exp, scale=-0.5)
                    A.fence(S["sst"])
                    G.wait(S["sst"])
                    G.tt(scl(6), scl(2), scl(7), ALU.mult)
                    G.ts(scl(6), scl(6), -1.0, ALU.mult)
                    G.fence(S["srd"])
                    V.wait(S["srd"])
                    wait_free(V, ("ow", par))
                    for n in range(NTC):
                        V.ts(ow[par][:, n, :], g32w[par][:, n, :],
                             stc[:, n, o + 7:o + 8], ALU.mult,
                             stc[:, n, o + 6:o + 7], ALU.add)
                    V.tt(ow[par][:], ow[par][:], yw[par][:], ALU.add)
                    V.fence(S["snr"])
                    consumed(("stc", o), S["snr"])
                    consumed(("g32w", par), S["snr"])
                    consumed(("yw", par), S["snr"])
                    GD.wait(S["snr"])
                    GD.go(OUT[b, t0:t0 + TC, :]
                          .rearrange("(n p) c -> p n c", p=128),
                          ow[par][:], sem=S["sfin"])
                    consumed(("ow", par), S["sfin"])
                    n_out_dma[0] += 1
                    if ch + 2 < NCC:
                        c_load(ch + 2)
            SY.wait(S["sfin"], n_out_dma[0] * 16)
        else:
            SY.wait(S["sou"], S["sou"].n)

    return nc


def kernel(**inputs):
    xs = np.ascontiguousarray(np.asarray(inputs["L0_attn_x"], dtype=np.float32))
    base = {}
    for k, v in inputs.items():
        if k == "L0_attn_x":
            continue
        base[k] = np.ascontiguousarray(np.asarray(v, dtype=np.float32))
    base["identity"] = np.eye(128, dtype=np.float32)
    in_maps = []
    for c in range(8):
        m = dict(base)
        m["x"] = np.ascontiguousarray(xs[2 * c:2 * c + 2])
        in_maps.append(m)
    nc = build()
    res = run_bass_kernel_spmd(nc, in_maps, core_ids=list(range(8)))
    out = np.concatenate([np.asarray(r["out"], dtype=np.float32)
                          for r in res.results], axis=0)
    return out.reshape(16, L, D)


if __name__ == "__main__":
    nc = build()
    print("instructions:", sum(1 for _ in nc.all_instructions()))
